# version 02
# baseline (speedup 1.0000x reference)
import numpy as np
import ml_dtypes
from contextlib import ExitStack

import concourse.bass as bass
import concourse.mybir as mybir
from concourse.bass_utils import run_bass_kernel_spmd

F32 = mybir.dt.float32
BF16 = mybir.dt.bfloat16
AF = mybir.ActivationFunctionType
ALU = mybir.AluOpType
AX = mybir.AxisListType

DM = 2048
TOK = 2048
NB = 16
LAT = 1024
INTER = 1024
NE = 32
SH = 4096
NKV = 4
HD = 128
EPS_LN = 1e-5
EPS_AT = 1e-6
NCORES = 8
KB = 1024
ARENA_KB = 120
BIG = 1e30

bfdt = ml_dtypes.bfloat16


class Sched:
    ENGS = ("tensor", "vector", "scalar", "gpsimd", "sync")

    def __init__(self):
        self.ops = []
        self.cnt = {}

    def op(self, eng, fn, waits=(), inc=None):
        ws = [(s, v) for (s, v) in waits if v and v > 0]
        self.ops.append((eng, fn, ws, inc))
        if inc is not None:
            self.cnt[inc[0]] = self.cnt.get(inc[0], 0) + inc[1]
            return self.cnt[inc[0]]
        return None

    def now(self, sem):
        return self.cnt.get(sem, 0)

    def emit(self, block, sems):
        for name in self.ENGS:
            def emitter(eng, eng_name=name):
                last = {}
                for (e, fn, ws, inc) in self.ops:
                    if e != eng_name:
                        continue
                    for (s, v) in ws:
                        if last.get(s, -1) >= v:
                            continue
                        eng.wait_ge(sems[s], v)
                        last[s] = v
                    instr = fn(eng)
                    if inc is not None:
                        instr.then_inc(sems[inc[0]], inc[1])
            getattr(block, name)(emitter)


class MK:
    def __init__(self, dbg=(), stop_after=None):
        self.nc = bass.Bass("TRN2", target_bir_lowering=False)
        self.sc = Sched()
        self.dbg_names = set(dbg)
        self.dbg_t = {}
        self.stop_after = stop_after
        self._dram()

    def _dram(self):
        nc = self.nc

        def din(n, s):
            return nc.dram_tensor(n, s, F32, kind="ExternalInput").ap()

        def dbin(n, s):
            return nc.dram_tensor(n, s, BF16, kind="ExternalInput").ap()

        self.xin = din("xin", [DM, TOK])
        self.wqT = din("wqT", [DM, DM])
        self.wkT = din("wkT", [DM, NKV * HD])
        self.wvT = din("wvT", [DM, NKV * HD])
        self.woT = din("woT", [DM, DM])
        self.gwT = din("gwT", [DM, NE])
        self.biasT = din("biasT", [128, NE])
        self.fc1T = din("fc1T", [DM, LAT])
        self.fc2T = din("fc2T", [LAT, DM])
        self.w1T = din("w1T", [NE * LAT, INTER])
        self.w2T = din("w2T", [NE * INTER, LAT])
        self.supT = din("supT", [DM, SH])
        self.sdnT = din("sdnT", [SH, DM])
        self.cosT = dbin("cosT", [128, NB * HD])
        self.sinT = dbin("sinT", [128, NB * HD])
        self.maskT = dbin("maskT", [128, 128])
        self.identT = dbin("identT", [128, 128])
        self.ehotT = dbin("ehotT", [NE, NE * 128])
        self.flagsT = din("flagsT", [128, NCORES])
        self.xout = nc.dram_tensor("xout", [DM, TOK], F32, kind="ExternalOutput").ap()
        self.cc_in = [nc.dram_tensor(f"ccin{g}", [128, 132], F32).ap() for g in range(NKV)]
        self.otd = nc.dram_tensor("otd", [DM, TOK], BF16).ap()
        self.cc_out = [nc.dram_tensor(f"ccout{g}", [NCORES * 128, 132], F32).ap() for g in range(NKV)]

    def t(self, fn, waits=(), inc=None):
        return self.sc.op("tensor", fn, waits, inc)

    def v(self, fn, waits=(), inc=None):
        return self.sc.op("vector", fn, waits, inc)

    def s(self, fn, waits=(), inc=None):
        return self.sc.op("scalar", fn, waits, inc)

    def g(self, fn, waits=(), inc=None):
        return self.sc.op("gpsimd", fn, waits, inc)

    def q(self, fn, waits=(), inc=None):
        return self.sc.op("sync", fn, waits, inc)

    def vd(self):
        self.sc.op("vector", lambda e: e.drain())

    def sd(self):
        self.sc.op("scalar", lambda e: e.drain())

    def av(self, off_kb, shape, dtype=BF16, part0=None):
        n = int(np.prod(shape))
        it = {BF16: 2, F32: 4}[dtype]
        base = self.arena if part0 is None else self.arena[0:part0]
        a = base[:, off_kb * KB: off_kb * KB + n * it].bitcast(dtype)
        if len(shape) == 1:
            return a
        kw = {f"d{i}": shape[i] for i in range(len(shape))}
        pat = "p (" + " ".join(f"d{i}" for i in range(len(shape))) + ") -> p " + \
              " ".join(f"d{i}" for i in range(len(shape)))
        return a.rearrange(pat, **kw)

    def all_now(self):
        sc = self.sc
        return [(s, sc.now(s)) for s in ("t", "v", "s", "p", "qw", "qio", "cc")]

    def tap(self, name, ap_fn, shape, dtype=F32):
        if name not in self.dbg_names:
            return
        ap = self.nc.dram_tensor("dbg_" + name, shape, dtype, kind="ExternalOutput").ap()
        self.q(lambda e, ap=ap, f=ap_fn: e.dma_start(out=ap[:], in_=f()),
               waits=self.all_now(), inc=("qio", 16))
        self.dbg_t[name] = ap

    def bank_wait(self, name):
        return [self.bank.get(name, ("v", 0))]

    def bank_set(self, name, sem="v"):
        self.bank[name] = (sem, self.sc.now(sem))

    def build(self):
        nc, sc = self.nc, self.sc
        with ExitStack() as st:
            E = st.enter_context
            self.x = E(nc.sbuf_tensor("x", [128, NB * TOK], BF16))
            self.msk = E(nc.sbuf_tensor("msk", [128, 128], BF16))
            self.idn = E(nc.sbuf_tensor("idn", [128, 128], BF16))
            self.eh = E(nc.sbuf_tensor("eh", [NE, NE * 128], BF16))
            self.bias = E(nc.sbuf_tensor("bias", [128, NE], F32))
            self.flg = E(nc.sbuf_tensor("flg", [128, NCORES], F32))
            self.ones_r = E(nc.sbuf_tensor("ones_r", [1, 128], BF16))
            self.ones_c = E(nc.sbuf_tensor("ones_c", [128, 1], BF16))
            self.rrow = E(nc.sbuf_tensor("rrow", [1, TOK], BF16))
            self.murow = E(nc.sbuf_tensor("murow", [1, TOK], BF16))
            self.r2row = E(nc.sbuf_tensor("r2row", [1, TOK], BF16))
            self.rcol = E(nc.sbuf_tensor("rcol", [128, NB], F32))
            self.mucol = E(nc.sbuf_tensor("mucol", [128, NB], F32))
            self.w1s_g = E(nc.sbuf_tensor("w1s_g", [32, 1], F32))
            self.w1s_f1 = E(nc.sbuf_tensor("w1s_f1", [128, 8], F32))
            self.w1s_sup = E(nc.sbuf_tensor("w1s_sup", [128, 32], F32))
            self.wsrow = E(nc.sbuf_tensor("wsrow", [1, 1024], BF16))
            self.cst = E(nc.sbuf_tensor("cst", [128, 1], F32))
            self.arena = E(nc.sbuf_tensor("arena", [128, ARENA_KB * KB], mybir.dt.uint8))
            self.P2 = [E(nc.psum_tensor(f"P2{i}", [128, 1024], F32)) for i in range(2)]
            self.P1 = [E(nc.psum_tensor(f"P1{i}", [128, 512], F32)) for i in range(3)]
            self.PT = E(nc.psum_tensor("PT", [128, 1024], BF16))
            sems = {}
            for sname in ("t", "v", "s", "p", "qw", "qio", "cc"):
                sems[sname] = E(nc.semaphore("sem_" + sname))
            block = E(nc.Block())

            self.bank = {}
            self.w1s_done = False
            self.program()
            sc.emit(block, sems)
        return self.nc

    def program(self):
        st = self.stop_after
        self.init_load()
        if st is not None and st.startswith("ONLYAT"):
            self.attn(stop=st[6:] or None)
            self.store_out()
            return
        if st != "init":
            self.ffn(0, stop=st)
            if st is None or not st.startswith("f0"):
                self.attn(stop=st)
                if st is None or not st.startswith("at"):
                    self.ffn(1, stop=st)
        self.store_out()

    def init_load(self):
        for dst, src in ((self.msk, self.maskT), (self.idn, self.identT),
                         (self.eh, self.ehotT), (self.bias, self.biasT),
                         (self.flg, self.flagsT)):
            self.q(lambda e, d=dst, s=src: e.dma_start(out=d[:], in_=s[:]), inc=("qio", 16))
        self.g(lambda e: e.dma_start(
            out=self.x[:].rearrange("p (c t) -> p c t", c=NB),
            in_=self.xin[:].rearrange("(c p) t -> p c t", p=128)), inc=("qw", 16))
        self.g(lambda e: e.memset(self.cst[:], EPS_LN))
        self.g(lambda e: e.memset(self.ones_r[:], 1.0))
        self.g(lambda e: e.memset(self.ones_c[:], 1.0), inc=("p", 1))

    def stats(self):
        sc = self.sc
        xsq = [self.av(72 + 2 * i, [1024], BF16) for i in range(2)]
        row = [self.av(56 + 4 * i, [1024], F32)[0:1, :] for i in range(4)]
        start_w = self.all_now()
        P2 = self.P2
        xsq_t = [0, 0]
        for half in range(2):
            sum_row = P2[0][0:1, :]
            sq_row = P2[1][0:1, :]
            w_ps = self.bank_wait("P2a") + self.bank_wait("P2b")
            for c in range(NB):
                xb = self.x[:, c * TOK + half * 1024: c * TOK + (half + 1) * 1024]
                sl = xsq[c % 2]
                wv = ([("t", xsq_t[c % 2])] if (half * NB + c) >= 2 else []) + \
                     (start_w if (half == 0 and c < 2) else [])
                self.v(lambda e, xb=xb, sl=sl: e.tensor_tensor(sl[:], xb, xb, ALU.mult),
                       waits=wv, inc=("v", 1))
                vm = sc.now("v")
                for n in range(2):
                    self.t(lambda e, xb=xb, n=n, c=c, sum_row=sum_row: e.matmul(
                        sum_row[:, n * 512:(n + 1) * 512], self.ones_c[:],
                        xb[:, n * 512:(n + 1) * 512], start=(c == 0), stop=(c == NB - 1)),
                        waits=(w_ps + start_w) if (c == 0 and n == 0) else [])
                for n in range(2):
                    xsq_t[c % 2] = self.t(lambda e, sl=sl, n=n, c=c, sq_row=sq_row: e.matmul(
                        sq_row[:, n * 512:(n + 1) * 512], self.ones_c[:],
                        sl[:, n * 512:(n + 1) * 512], start=(c == 0), stop=(c == NB - 1)),
                        waits=[("v", vm)] if n == 0 else [], inc=("t", 1))
            tsum = sc.now("t")
            hs = slice(half * 1024, (half + 1) * 1024)
            self.v(lambda e, sum_row=sum_row: e.tensor_scalar(
                row[0][:], sum_row, 1.0 / DM, None, ALU.mult, ALU.bypass),
                waits=[("t", tsum)], inc=("v", 1))
            self.v(lambda e, sq_row=sq_row: e.tensor_scalar(
                row[1][:], sq_row, 1.0 / DM, None, ALU.mult, ALU.bypass), inc=("v", 1))
            self.vd()
            self.v(lambda e: e.tensor_tensor(row[2][:], row[0][:], row[0][:], ALU.mult), inc=("v", 1))
            self.vd()
            self.v(lambda e: e.tensor_tensor(row[3][:], row[1][:], row[2][:], ALU.subtract), inc=("v", 1))
            self.vd()
            vv = sc.now("v")
            self.s(lambda e: e.activation(row[1][:], row[3][:], AF.Ln, bias=self.cst[0:1, 0:1]),
                   waits=[("v", vv)], inc=("s", 1))
            self.sd()
            self.s(lambda e: e.activation(row[2][:], row[1][:], AF.Exp, scale=-0.5), inc=("s", 1))
            sv = sc.now("s")
            self.v(lambda e, hs=hs: e.tensor_copy(self.rrow[0:1, hs], row[2][:]),
                   waits=[("s", sv)], inc=("v", 1))
            self.v(lambda e, hs=hs: e.tensor_copy(self.murow[0:1, hs], row[0][:]), inc=("v", 1))
            self.v(lambda e: e.tensor_tensor(row[3][:], row[2][:], row[2][:], ALU.mult), inc=("v", 1))
            self.vd()
            self.v(lambda e, hs=hs: e.tensor_copy(self.r2row[0:1, hs], row[3][:]), inc=("v", 1))
            self.bank_set("P2a")
            self.bank_set("P2b")
        vrows = sc.now("v")
        P = self.P1[2]
        for tb in range(NB):
            wb = self.bank_wait("P1c") + ([("v", vrows)] if tb == 0 else [])
            self.t(lambda e, tb=tb: e.matmul(
                P[:, 0:1],
                self.rrow[0:1, tb * 128:(tb + 1) * 128], self.ones_r[0:1, 0:1],
                start=True, stop=True), waits=wb)
            self.t(lambda e, tb=tb: e.matmul(
                P[:, 4:5],
                self.murow[0:1, tb * 128:(tb + 1) * 128], self.ones_r[0:1, 0:1],
                start=True, stop=True), inc=("t", 1))
            tm = sc.now("t")
            self.v(lambda e, tb=tb: e.tensor_copy(
                self.rcol[:, tb:tb + 1], P[:, 0:1]), waits=[("t", tm)])
            self.v(lambda e, tb=tb: e.tensor_copy(
                self.mucol[:, tb:tb + 1], P[:, 4:5]), inc=("v", 1))
            self.bank_set("P1c")
        self.stats_v = sc.now("v")
        self.tap("rrow", lambda: self.rrow[:], [1, TOK], BF16)
        self.tap("murow", lambda: self.murow[:], [1, TOK], BF16)
        self.tap("rcol", lambda: self.rcol[:], [128, NB], F32)

    def bcast_rows(self, rows_offs):
        sc = self.sc
        P = self.P1[2]
        outs = []
        for (srow, off) in rows_offs:
            dst = self.av(off, [TOK], BF16)
            outs.append(dst)
            for n in range(4):
                wb = self.bank_wait("P1c")
                self.t(lambda e, srow=srow, n=n: e.matmul(
                    P[:], self.ones_r[:], srow[0:1, n * 512:(n + 1) * 512],
                    start=True, stop=True),
                    waits=wb + [("v", self.stats_v)], inc=("t", 1))
                tm = sc.now("t")
                self.v(lambda e, dst=dst, n=n: e.tensor_copy(dst[:, n * 512:(n + 1) * 512], P[:]),
                       waits=[("t", tm)], inc=("v", 1))
                self.bank_set("P1c")
        return outs

    def w1s_from_slot(self, wslot, nchunk, dst_col, mb, row_off, qwait):
        sc = self.sc
        P = self.P1[2]
        wb = self.bank_wait("P1c")
        for c in range(nchunk):
            self.t(lambda e, c=c: e.matmul(
                P[0:1, 0:128], self.ones_c[:], wslot[:, c, :],
                start=(c == 0), stop=(c == nchunk - 1)),
                waits=(wb + [("qw", qwait)]) if c == 0 else [],
                inc=("t", 1) if c == nchunk - 1 else None)
        tm = sc.now("t")
        ro = self.wsrow[0:1, row_off:row_off + 128]
        self.v(lambda e, ro=ro: e.tensor_copy(ro, P[0:1, 0:128]), waits=[("t", tm)], inc=("v", 1))
        vm = sc.now("v")
        self.bank_set("P1c")
        wb2 = self.bank_wait("P1c")
        self.t(lambda e, ro=ro: e.matmul(P[:, 256:257], ro, self.ones_r[0:1, 0:1],
                                         start=True, stop=True),
               waits=[("v", vm)] + wb2, inc=("t", 1))
        tm2 = sc.now("t")
        self.v(lambda e, mb=mb: e.tensor_copy(dst_col[:, mb:mb + 1], P[:, 256:257]),
               waits=[("t", tm2)], inc=("v", 1))
        self.bank_set("P1c")

    def ffn(self, idx, stop=None):
        pre = f"f{idx}"
        self.stats()
        if stop == pre + "stats":
            return
        Rb, Mub, R2b = self.bcast_rows([(self.rrow, 36), (self.murow, 40), (self.r2row, 44)])
        self.Rb, self.Mub, self.R2b = Rb, Mub, R2b
        if stop == pre + "bcast":
            return
        self.gate_and_route(idx)
        if stop == pre + "gate":
            return
        self.fc1(idx)
        if stop == pre + "fc1":
            return
        self.shared(idx)
        if stop == pre + "shared":
            return
        self.experts(idx)
        if stop == pre + "exp":
            return
        self.fc2(idx)
        self.w1s_done = True

    def gate_and_route(self, idx):
        sc = self.sc
        Rb, Mub = self.Rb, self.Mub
        gw = self.av(48, [NB, NE], BF16)
        lg = self.av(50, [NB, NE], F32)
        scr = self.av(52, [NB, NE], F32)
        sbm = self.av(54, [NB, NE], F32)
        cwa = self.av(56, [NB, NE], F32)
        sm0 = self.av(58, [NB, 4], F32)
        sm1 = self.av(59, [NB, 4], F32)
        sm2 = self.av(60, [NB, NE], F32)
        tg32 = self.av(62, [512], F32, part0=32)
        lgT = self.av(64, [TOK], BF16, part0=32)
        self.g(lambda e: e.dma_start(
            out=gw[:], in_=self.gwT[:].rearrange("(c p) n -> p c n", p=128)), inc=("qw", 16))
        qn = sc.now("qw")
        if not self.w1s_done:
            Pg = self.P1[2]
            wb = self.bank_wait("P1c")
            for c in range(NB):
                self.t(lambda e, c=c, Pg=Pg: e.matmul(
                    Pg[0:1, 0:32], self.ones_c[:], gw[:, c, :],
                    start=(c == 0), stop=(c == NB - 1)),
                    waits=(wb + [("qw", qn)]) if c == 0 else [],
                    inc=("t", 1) if c == NB - 1 else None)
            tm = sc.now("t")
            self.v(lambda e, Pg=Pg: e.tensor_copy(self.wsrow[0:1, 0:32], Pg[0:1, 0:32]),
                   waits=[("t", tm)], inc=("v", 1))
            vm = sc.now("v")
            self.bank_set("P1c")
            wb = self.bank_wait("P1c")
            self.t(lambda e, Pg=Pg: e.matmul(Pg[0:32, 256:257], self.wsrow[0:1, 0:32],
                                             self.ones_r[0:1, 0:1], start=True, stop=True),
                   waits=[("v", vm)] + wb, inc=("t", 1))
            tm = sc.now("t")
            self.v(lambda e, Pg=Pg: e.tensor_copy(self.w1s_g[:], Pg[0:32, 256:257]),
                   waits=[("t", tm)], inc=("v", 1))
            self.bank_set("P1c")
        for n in range(4):
            bk = "P1a" if n % 2 == 0 else "P1b"
            P = self.P1[n % 2]
            wb = self.bank_wait(bk)
            for c in range(NB):
                self.t(lambda e, n=n, c=c, P=P: e.matmul(
                    P[0:32, :], gw[:, c, :],
                    self.x[:, c * TOK + n * 512: c * TOK + (n + 1) * 512],
                    start=(c == 0), stop=(c == NB - 1)),
                    waits=(wb + [("qw", qn)]) if c == 0 else [],
                    inc=("t", 1) if c == NB - 1 else None)
            tm = sc.now("t")
            nsl = slice(n * 512, (n + 1) * 512)
            self.v(lambda e, P=P, nsl=nsl: e.scalar_tensor_tensor(
                tg32[:], Mub[0:32, nsl], self.w1s_g[:], P[0:32, :], ALU.mult, ALU.subtract),
                waits=[("t", tm)], inc=("v", 1))
            self.vd()
            self.v(lambda e, nsl=nsl: e.scalar_tensor_tensor(
                lgT[:, nsl], tg32[:], -1.0, Rb[0:32, nsl], ALU.mult, ALU.mult), inc=("v", 1))
            self.bank_set(bk)
        vv = sc.now("v")
        for half in range(2):
            wpt = self.bank_wait("PT")
            for j in range(8):
                tb = half * 8 + j
                self.t(lambda e, tb=tb, j=j: e.transpose(
                    self.PT[:, j * 128: j * 128 + 32], lgT[:, tb * 128:(tb + 1) * 128],
                    self.idn[0:32, 0:32]),
                    waits=(wpt + [("v", vv)]) if j == 0 else [],
                    inc=("t", 1) if j == 7 else None)
            tm = sc.now("t")
            self.v(lambda e, half=half: e.tensor_copy(
                lg[:, half * 8:(half + 1) * 8, :],
                self.PT[:].rearrange("p (a b) -> p a b", a=8)[:, :, 0:32]),
                waits=[("t", tm)], inc=("v", 1))
            self.bank_set("PT")
        vv = sc.now("v")
        self.s(lambda e: e.activation(scr[:], lg[:], AF.Sigmoid), waits=[("v", vv)], inc=("s", 1))
        sv = sc.now("s")
        V, VD = self.v, self.vd
        r3 = lambda a: a[:].rearrange("p a b -> p a b")
        g4s = sbm[:].rearrange("p a (g k) -> p a g k", g=4)
        M64 = self.av(58, [64], F32)
        aux = self.av(59, [96], F32)
        M64b, m16, m16b = aux[:, 0:64], aux[:, 64:80], aux[:, 80:96]
        EXP = lg[:].rearrange("p a b -> p (a b)")
        sbf = sbm[:].rearrange("p a b -> p (a b)")
        scf = scr[:].rearrange("p a b -> p (a b)")
        cwf = cwa[:].rearrange("p a b -> p (a b)")
        smf = sm2[:].rearrange("p a b -> p (a b)")
        V(lambda e: e.tensor_tensor(sbm[:], scr[:],
                                    self.bias[:].unsqueeze(1).broadcast_to([128, NB, NE]), ALU.add),
          waits=[("s", sv)])
        VD()
        V(lambda e: e.tensor_reduce(M64.rearrange("p (a g) -> p a g", a=NB), g4s, axis=AX.X, op=ALU.max))
        VD()
        for hh in range(2):
            V(lambda e, hh=hh: e.tensor_copy(
                EXP.rearrange("p (a k) -> p a k", a=64)[:, hh * 32:(hh + 1) * 32, :],
                M64[:, hh * 32:(hh + 1) * 32].unsqueeze(2).broadcast_to([128, 32, 8])))
        VD()
        V(lambda e: e.tensor_tensor(smf, sbf, EXP, ALU.is_ge))
        VD()
        V(lambda e: e.tensor_scalar(smf, smf, -BIG, None, ALU.mult, ALU.bypass))
        VD()
        V(lambda e: e.tensor_tensor(smf, smf, sbf, ALU.add))
        VD()
        V(lambda e: e.tensor_reduce(M64b.rearrange("p (a g) -> p a g", a=NB),
                                    sm2[:].rearrange("p a (g k) -> p a g k", g=4), axis=AX.X, op=ALU.max))
        VD()
        V(lambda e: e.tensor_tensor(M64, M64, M64b, ALU.add))
        VD()
        V(lambda e: e.tensor_reduce(m16.rearrange("p (a g) -> p a g", a=NB, g=1).squeeze(2),
                                    M64.rearrange("p (a g) -> p a g", a=NB), axis=AX.X, op=ALU.max))
        VD()
        V(lambda e: e.tensor_copy(M64b.rearrange("p (a g) -> p a g", a=NB),
                                  m16.unsqueeze(2).broadcast_to([128, 16, 4])))
        VD()
        V(lambda e: e.tensor_tensor(smf[:, 0:64], M64, M64b, ALU.is_ge))
        VD()
        V(lambda e: e.tensor_scalar(smf[:, 0:64], smf[:, 0:64], -BIG, None, ALU.mult, ALU.bypass))
        VD()
        V(lambda e: e.tensor_tensor(smf[:, 0:64], smf[:, 0:64], M64, ALU.add))
        VD()
        V(lambda e: e.tensor_reduce(m16b.rearrange("p (a g) -> p a g", a=NB, g=1).squeeze(2),
                                    smf[:, 0:64].rearrange("p (a g) -> p a g", a=NB), axis=AX.X, op=ALU.max))
        VD()
        V(lambda e: e.tensor_copy(M64b.rearrange("p (a g) -> p a g", a=NB),
                                  m16b.unsqueeze(2).broadcast_to([128, 16, 4])))
        VD()
        V(lambda e: e.tensor_tensor(smf[:, 0:64], M64, M64b, ALU.is_ge))
        VD()
        V(lambda e: e.tensor_scalar(smf[:, 0:64], smf[:, 0:64], 1.0, BIG, ALU.subtract, ALU.mult))
        VD()
        for hh in range(2):
            V(lambda e, hh=hh: e.tensor_copy(
                EXP.rearrange("p (a k) -> p a k", a=64)[:, hh * 32:(hh + 1) * 32, :],
                smf[:, hh * 32:(hh + 1) * 32].unsqueeze(2).broadcast_to([128, 32, 8])))
        VD()
        V(lambda e: e.tensor_tensor(sbf, sbf, EXP, ALU.add))
        VD()
        V(lambda e: e.memset(cwa[:], 0.0))
        VD()
        for it in range(8):
            V(lambda e: e.tensor_reduce(m16.rearrange("p (a g) -> p a g", a=NB, g=1).squeeze(2),
                                        r3(sbm), axis=AX.X, op=ALU.max))
            VD()
            for hh in range(2):
                V(lambda e, hh=hh: e.tensor_copy(
                    EXP.rearrange("p (a k) -> p a k", a=NB)[:, hh * 8:(hh + 1) * 8, :],
                    m16[:, hh * 8:(hh + 1) * 8].unsqueeze(2).broadcast_to([128, 8, 32])))
            VD()
            V(lambda e: e.tensor_tensor(smf, sbf, EXP, ALU.is_ge))
            VD()
            V(lambda e: e.scalar_tensor_tensor(EXP, smf, -BIG, sbf, ALU.mult, ALU.add))
            VD()
            V(lambda e: e.tensor_tensor(smf, smf, scf, ALU.mult))
            VD()
            V(lambda e: e.tensor_tensor(cwf, cwf, smf, ALU.add))
            VD()
            V(lambda e: e.tensor_copy(sbf, EXP))
            VD()
        V(lambda e: e.tensor_reduce(m16.rearrange("p (a g) -> p a g", a=NB, g=1).squeeze(2),
                                    r3(cwa), axis=AX.X, op=ALU.add))
        VD()
        V(lambda e: e.reciprocal(m16b, m16))
        VD()
        for hh in range(2):
            V(lambda e, hh=hh: e.tensor_copy(
                EXP.rearrange("p (a k) -> p a k", a=NB)[:, hh * 8:(hh + 1) * 8, :],
                m16b[:, hh * 8:(hh + 1) * 8].unsqueeze(2).broadcast_to([128, 8, 32])))
        VD()
        V(lambda e: e.tensor_scalar(cwf, cwf, 2.5, None, ALU.mult, ALU.bypass))
        VD()
        V(lambda e: e.tensor_tensor(cwf, cwf, EXP, ALU.mult), inc=("v", 1))
        self.vd()
        vv = sc.now("v")
        self.tap(f"cw{idx}", lambda: cwa[:].rearrange("p a b -> p (a b)"), [128, 512])
        cwb = self.av(52, [NB, NE], BF16)
        V(lambda e: e.tensor_copy(cwb[:], cwa[:]), waits=[("v", vv)], inc=("v", 1))
        vv = sc.now("v")
        cwT = self.av(32, [TOK], BF16, part0=32)
        for half in range(2):
            wpt = self.bank_wait("PT")
            for j in range(8):
                tb = half * 8 + j
                self.t(lambda e, tb=tb, j=j: e.transpose(
                    self.PT[0:32, j * 128:(j + 1) * 128],
                    cwb[:, tb, :], self.idn[:]),
                    waits=(wpt + [("v", vv)]) if j == 0 else [],
                    inc=("t", 1) if j == 7 else None)
            tm = sc.now("t")
            self.v(lambda e, half=half: e.tensor_copy(
                cwT[:, half * 1024:(half + 1) * 1024].rearrange("p (a b) -> p a b", a=8),
                self.PT[0:32].rearrange("p (a b) -> p a b", a=8)),
                waits=[("t", tm)], inc=("v", 1))
            self.bank_set("PT")
        self.cwT = cwT
        self.cwT_v = sc.now("v")

    def shared(self, idx):
        sc = self.sc
        Mub, R2b = self.Mub, self.R2b
        h_sh = self.av(56, [32, 512], BF16)
        supw = [self.av(88 + 4 * i, [NB, 128], BF16) for i in range(2)]
        sdnw = [self.av(96 + 8 * i, [32, 128], BF16) for i in range(2)]
        tf = [self.av(112 + 2 * i, [512], F32) for i in range(2)]
        rb = [self.av(116 + i, [512], BF16) for i in range(2)]
        pw = [("t", sc.now("t")), ("v", sc.now("v")), ("s", sc.now("s"))]
        self._pw = {0: pw, 1: pw}
        sup_t = [0, 0]
        sdn_t = [0, 0]
        relu_s = [0, 0]
        for grp in range(4):
            gsl = slice(grp * 512, (grp + 1) * 512)
            for mb in range(32):
                sl = mb % 2
                self.g(lambda e, mb=mb, sl=sl: e.dma_start(
                    out=supw[sl][:],
                    in_=self.supT[:, mb * 128:(mb + 1) * 128].rearrange("(c p) n -> p c n", p=128)),
                    waits=([("t", sup_t[sl])] if (grp, mb) >= (0, 2) else pw), inc=("qw", 16))
                qn = sc.now("qw")
                if not self.w1s_done and grp == 0:
                    self.w1s_from_slot(supw[sl], NB, self.w1s_sup, mb, 128, qn)
                bk = "P1a" if sl == 0 else "P1b"
                P = self.P1[sl]
                wb = self.bank_wait(bk)
                for c in range(NB):
                    sup_t[sl] = self.t(lambda e, c=c, P=P, grp=grp, sl=sl: e.matmul(
                        P[:], supw[sl][:, c, :],
                        self.x[:, c * TOK + grp * 512: c * TOK + (grp + 1) * 512],
                        start=(c == 0), stop=(c == NB - 1)),
                        waits=(wb + [("qw", qn), ("v", self.sc.now("v"))]) if c == 0 else [],
                        inc=("t", 1) if c == NB - 1 else None) or sup_t[sl]
                tm = sc.now("t")
                self.v(lambda e, mb=mb, P=P, gsl=gsl, sl=sl: e.scalar_tensor_tensor(
                    tf[sl][:], Mub[:, gsl], self.w1s_sup[:, mb:mb + 1], P[:],
                    ALU.mult, ALU.subtract), waits=[("t", tm), ("s", relu_s[sl])], inc=("v", 1))
                vv = sc.now("v")
                self.bank_set(bk)
                relu_s[sl] = self.s(lambda e, sl=sl: e.activation(rb[sl][:], tf[sl][:], AF.Relu, scale=-1.0),
                       waits=[("v", vv)], inc=("s", 1))
                sv = sc.now("s")
                self.v(lambda e, sl=sl: e.tensor_tensor(tf[sl][:], rb[sl][:], rb[sl][:], ALU.mult),
                       waits=[("s", sv)])
                self.vd()
                self.v(lambda e, mb=mb, sl=sl, gsl=gsl: e.tensor_tensor(
                    h_sh[:, mb, :], tf[sl][:], R2b[:, gsl], ALU.mult), inc=("v", 1))
            hv = sc.now("v")
            if grp == 0:
                self.tap("hsh", lambda: h_sh[:].rearrange("p a b -> p (a b)"), [128, 32 * 512], BF16)
                self.tap("w1s_sup", lambda: self.w1s_sup[:], [128, 32], F32)
            for mb in range(NB):
                sl = mb % 2
                self.g(lambda e, mb=mb, sl=sl: e.dma_start(
                    out=sdnw[sl][:],
                    in_=self.sdnT[:, mb * 128:(mb + 1) * 128].rearrange("(c p) n -> p c n", p=128)),
                    waits=([("t", sdn_t[sl])] if (grp, mb) >= (0, 2) else pw), inc=("qw", 16))
                qn = sc.now("qw")
                bk = "P2a" if sl == 0 else "P2b"
                P = self.P2[sl][:, 0:512]
                wb = self.bank_wait(bk)
                for c in range(32):
                    sdn_t[sl] = self.t(lambda e, c=c, P=P, sl=sl: e.matmul(
                        P, sdnw[sl][:, c, :], h_sh[:, c, :],
                        start=(c == 0), stop=(c == 31)),
                        waits=(wb + [("qw", qn), ("v", hv)]) if c == 0 else [],
                        inc=("t", 1) if c == 31 else None) or sdn_t[sl]
                tm = sc.now("t")
                xsl = self.x[:, mb * TOK + grp * 512: mb * TOK + (grp + 1) * 512]
                self.v(lambda e, P=P, xsl=xsl: e.scalar_tensor_tensor(
                    xsl, P, 0.5, xsl, ALU.mult, ALU.add), waits=[("t", tm)], inc=("v", 1))
                self.bank_set(bk)

    def fc1(self, idx):
        sc = self.sc
        Rb, Mub = self.Rb, self.Mub
        xlat = self.av(0, [8, TOK], BF16)
        fw = [self.av(88 + 4 * i, [NB, 128], BF16) for i in range(2)]
        tf = [self.av(112 + 4 * i, [1024], F32) for i in range(2)]
        pw = [("t", sc.now("t")), ("v", sc.now("v")), ("s", sc.now("s"))]
        f_t = [0, 0]
        for mb in range(8):
            sl = mb % 2
            self.g(lambda e, mb=mb, sl=sl: e.dma_start(
                out=fw[sl][:],
                in_=self.fc1T[:, mb * 128:(mb + 1) * 128].rearrange("(c p) n -> p c n", p=128)),
                waits=([("t", f_t[sl])] if mb >= 2 else pw), inc=("qw", 16))
            qn = sc.now("qw")
            if not self.w1s_done:
                self.w1s_from_slot(fw[sl], NB, self.w1s_f1, mb, 256, qn)
            for nh in range(2):
                bk = "P2a" if sl == 0 else "P2b"
                P = self.P2[sl]
                wb = self.bank_wait(bk) if nh == 0 else [("v", sc.now("v"))]
                for half512 in range(2):
                    off = nh * 1024 + half512 * 512
                    for c in range(NB):
                        f_t[sl] = self.t(lambda e, c=c, off=off, P=P, sl=sl, half512=half512: e.matmul(
                            P[:, half512 * 512:(half512 + 1) * 512],
                            fw[sl][:, c, :], self.x[:, c * TOK + off: c * TOK + off + 512],
                            start=(c == 0), stop=(c == NB - 1)),
                            waits=(wb + [("qw", qn)]) if (c == 0 and half512 == 0) else [],
                            inc=("t", 1) if c == NB - 1 else None) or f_t[sl]
                tm = sc.now("t")
                tsl = slice(nh * 1024, (nh + 1) * 1024)
                self.v(lambda e, mb=mb, P=P, tsl=tsl, sl=sl: e.scalar_tensor_tensor(
                    tf[sl][:], Mub[:, tsl], self.w1s_f1[:, mb:mb + 1], P[:], ALU.mult, ALU.subtract),
                    waits=[("t", tm)], inc=("v", 1))
                self.vd()
                self.v(lambda e, mb=mb, tsl=tsl, sl=sl: e.scalar_tensor_tensor(
                    xlat[:, mb, tsl], tf[sl][:], -1.0, Rb[:, tsl], ALU.mult, ALU.mult),
                    inc=("v", 1))
                if nh == 1:
                    self.bank_set(bk)
        self.xlat_v = sc.now("v")
        self.tap(f"xlat{idx}", lambda: xlat[:].rearrange("p a b -> p (a b)"), [128, 8 * TOK], BF16)

    def experts(self, idx):
        sc = self.sc
        xlat = self.av(0, [8, TOK], BF16)
        cwT = self.cwT
        wA = [self.av(36, [8, 1024], BF16), self.av(104, [8, 1024], BF16)]
        Be = self.av(52, [TOK], BF16)
        y = self.av(56, [8, TOK], BF16)
        h = self.av(88, [8, 1024], BF16)
        pw = [("t", sc.now("t")), ("v", sc.now("v")), ("s", sc.now("s"))]
        w1_t = 0
        w2_t = 0
        yv = 0
        w1q = w2q = 0
        for e_i in range(NE):
            self.g(lambda e, e_i=e_i: e.dma_start(
                out=wA[0][:],
                in_=self.w1T[e_i * LAT:(e_i + 1) * LAT, :].rearrange("(c p) n -> p c n", p=128)),
                waits=([("t", w1_t)] if e_i > 0 else pw), inc=("qw", 16))
            w1q = sc.now("qw")
            self.g(lambda e, e_i=e_i: e.dma_start(
                out=wA[1][:],
                in_=self.w2T[e_i * INTER:(e_i + 1) * INTER, :].rearrange("(c p) n -> p c n", p=128)),
                waits=([("t", w2_t)] if e_i > 0 else pw), inc=("qw", 16))
            w2q = sc.now("qw")
            for n in range(4):
                wb = self.bank_wait("P1a")
                self.t(lambda e, e_i=e_i, n=n: e.matmul(
                    self.P1[0][:], self.eh[:, e_i * 128:(e_i + 1) * 128],
                    cwT[:, n * 512:(n + 1) * 512], start=True, stop=True),
                    waits=wb + ([("v", max(yv, self.cwT_v))] if n == 0 else []), inc=("t", 1))
                tm = sc.now("t")
                self.v(lambda e, n=n: e.tensor_copy(Be[:, n * 512:(n + 1) * 512], self.P1[0][:]),
                       waits=[("t", tm)], inc=("v", 1))
                self.bank_set("P1a")
            for half in range(2):
                hsl = slice(half * 1024, (half + 1) * 1024)
                for ib in range(8):
                    bk = "P2a" if ib % 2 == 0 else "P2b"
                    P = self.P2[ib % 2]
                    wb = self.bank_wait(bk)
                    for kc in range(8):
                        for n2 in range(2):
                            w1_t = self.t(lambda e, kc=kc, ib=ib, n2=n2, P=P, half=half: e.matmul(
                                P[:, n2 * 512:(n2 + 1) * 512],
                                wA[0][:, kc, ib * 128:(ib + 1) * 128],
                                xlat[:, kc, half * 1024 + n2 * 512: half * 1024 + (n2 + 1) * 512],
                                start=(kc == 0), stop=(kc == 7)),
                                waits=(wb + [("qw", w1q), ("v", self.xlat_v)]) if (kc == 0 and n2 == 0) else [],
                                inc=("t", 1) if (kc == 7 and n2 == 1) else None) or w1_t
                    tm = sc.now("t")
                    self.s(lambda e, ib=ib, P=P: e.activation(h[:, ib, :], P[:], AF.Relu),
                           waits=[("t", tm)], inc=("s", 1))
                    sv = sc.now("s")
                    self.v(lambda e, ib=ib: e.tensor_tensor(h[:, ib, :], h[:, ib, :], h[:, ib, :],
                                                            ALU.mult), waits=[("s", sv)])
                    self.vd()
                    self.v(lambda e, ib=ib, hsl=hsl: e.tensor_tensor(
                        h[:, ib, :], h[:, ib, :], Be[:, hsl], ALU.mult), inc=("v", 1))
                    self.bank_set(bk, "s")
                hv = sc.now("v")
                for mb in range(8):
                    bk = "P2a" if mb % 2 == 0 else "P2b"
                    P = self.P2[mb % 2]
                    wb = self.bank_wait(bk)
                    for kc in range(8):
                        for n2 in range(2):
                            w2_t = self.t(lambda e, kc=kc, mb=mb, n2=n2, P=P: e.matmul(
                                P[:, n2 * 512:(n2 + 1) * 512],
                                wA[1][:, kc, mb * 128:(mb + 1) * 128],
                                h[:, kc, n2 * 512:(n2 + 1) * 512],
                                start=(kc == 0), stop=(kc == 7)),
                                waits=(wb + [("qw", w2q), ("v", hv)]) if (kc == 0 and n2 == 0) else [],
                                inc=("t", 1) if (kc == 7 and n2 == 1) else None) or w2_t
                    tm = sc.now("t")
                    if e_i == 0:
                        yv = self.v(lambda e, mb=mb, P=P, hsl=hsl: e.tensor_copy(
                            y[:, mb, hsl], P[:]), waits=[("t", tm)], inc=("v", 1))
                    else:
                        yv = self.v(lambda e, mb=mb, P=P, hsl=hsl: e.tensor_tensor(
                            y[:, mb, hsl], P[:], y[:, mb, hsl], ALU.add),
                            waits=[("t", tm)], inc=("v", 1))
                    self.bank_set(bk)
        self.y_done = sc.now("v")
        self.tap(f"ylat{idx}", lambda: y[:].rearrange("p a b -> p (a b)"), [128, 8 * TOK], BF16)

    def fc2(self, idx):
        import os as _os
        NMB = int(_os.environ.get("FC2_MB", NB))
        sc = self.sc
        y = self.av(56, [8, TOK], BF16)
        fw = [self.av(88 + 2 * i, [8, 128], BF16) for i in range(2)]
        pw = [("t", sc.now("t")), ("v", sc.now("v")), ("s", sc.now("s"))]
        f_t = [0, 0]
        for mb in range(NMB):
            sl = mb % 2
            self.g(lambda e, mb=mb, sl=sl: e.dma_start(
                out=fw[sl][:],
                in_=self.fc2T[:, mb * 128:(mb + 1) * 128].rearrange("(c p) n -> p c n", p=128)),
                waits=([("t", f_t[sl])] if mb >= 2 else pw), inc=("qw", 16))
            qn = sc.now("qw")
            for nh in range(2):
                bk = "P2a" if sl == 0 else "P2b"
                P = self.P2[sl]
                wb = self.bank_wait(bk) if nh == 0 else [("v", sc.now("v"))]
                for n2 in range(2):
                    for kc in range(8):
                        f_t[sl] = self.t(lambda e, kc=kc, nh=nh, n2=n2, P=P, sl=sl: e.matmul(
                            P[:, n2 * 512:(n2 + 1) * 512],
                            fw[sl][:, kc, :], y[:, kc, nh * 1024 + n2 * 512: nh * 1024 + (n2 + 1) * 512],
                            start=(kc == 0), stop=(kc == 7)),
                            waits=(wb + ([("qw", qn), ("v", self.y_done)] if nh == 0 else [])) if (kc == 0 and n2 == 0) else [],
                            inc=("t", 1) if kc == 7 else None) or f_t[sl]
                tm = sc.now("t")
                xsl = self.x[:, mb * TOK + nh * 1024: mb * TOK + (nh + 1) * 1024]
                self.v(lambda e, P=P, xsl=xsl: e.scalar_tensor_tensor(
                    xsl, P[:], 0.5, xsl, ALU.mult, ALU.add), waits=[("t", tm)], inc=("v", 1))
                if nh == 1:
                    self.bank_set(bk)

    def attn(self, stop=None):
        sc = self.sc
        self.stats()
        if stop == 'atstats':
            return
        cosA = self.av(112, [NB, HD], BF16)
        sinA = self.av(116, [NB, HD], BF16)
        self.q(lambda e: e.dma_start(out=cosA[:].rearrange("p a b -> p (a b)"), in_=self.cosT[:]),
               waits=[("v", sc.now("v")), ("t", sc.now("t")), ("s", sc.now("s"))], inc=("qio", 16))
        self.q(lambda e: e.dma_start(out=sinA[:].rearrange("p a b -> p (a b)"), in_=self.sinT[:]),
               inc=("qio", 16))
        tabq = sc.now("qio")
        wq = self.av(0, [NB, 512], BF16)
        wk = self.av(16, [NB, 128], BF16)
        wv = self.av(20, [NB, 128], BF16)
        fx0 = self.av(24, [768], F32)
        fx1 = self.av(27, [640], F32)
        fx2 = self.av(30, [640], F32)
        qtok = self.av(33, [NB, 512], BF16)
        OT = self.av(16, [4, TOK], BF16)
        qfT = self.av(49, [4, TOK], BF16)
        ktok = self.av(65, [NB, 128], BF16)
        kT = self.av(69, [TOK], BF16)
        vaug = self.av(73, [NB, 132], BF16)
        osb = self.av(78, [NB, 4, 129], BF16)
        sg = self.av(95, [NCORES, 132], F32)
        sfin = self.av(100, [132], F32)
        ssb = self.av(101, [132], BF16)
        sinb = self.av(102, [132], BF16)
        acc = [self.av(103, [132], F32), self.av(104, [132], F32)]
        amT = [self.av(105, [512], BF16), self.av(106, [512], BF16)]
        W1Sb = self.av(107, [768], BF16)
        ft = self.av(109, [4, 129], F32)
        den = self.av(104, [8], F32)
        woT_s = self.av(0, [4, TOK], BF16)
        OHEAD = [0, 256, 512, 768]

        prev_corr_v = 0
        for grp in range(NKV):
            pwA = [("t", sc.now("t")), ("v", sc.now("v")), ("s", sc.now("s")),
                   ("qio", sc.now("qio"))]
            wqw = self.g(lambda e, grp=grp: e.dma_start(
                out=wq[:], in_=self.wqT[:, grp * 512:(grp + 1) * 512].rearrange("(c p) n -> p c n", p=128)),
                waits=pwA, inc=("qw", 16))
            self.g(lambda e, grp=grp: e.dma_start(
                out=wk[:], in_=self.wkT[:, grp * 128:(grp + 1) * 128].rearrange("(c p) n -> p c n", p=128)),
                inc=("qw", 16))
            self.g(lambda e, grp=grp: e.dma_start(
                out=wv[:], in_=self.wvT[:, grp * 128:(grp + 1) * 128].rearrange("(c p) n -> p c n", p=128)),
                inc=("qw", 16))
            qn = sc.now("qw")
            Pw = self.P2[0]
            wb = self.bank_wait("P2a")
            for c in range(NB):
                self.t(lambda e, c=c, Pw=Pw: e.matmul(Pw[0:1, 0:512], self.ones_c[:], wq[:, c, :],
                                                      start=(c == 0), stop=(c == NB - 1)),
                       waits=(wb + [("qw", qn), ("v", prev_corr_v)]) if c == 0 else [])
            for c in range(NB):
                self.t(lambda e, c=c, Pw=Pw: e.matmul(Pw[0:1, 512:640], self.ones_c[:], wk[:, c, :],
                                                      start=(c == 0), stop=(c == NB - 1)))
            for c in range(NB):
                self.t(lambda e, c=c, Pw=Pw: e.matmul(Pw[0:1, 640:768], self.ones_c[:], wv[:, c, :],
                                                      start=(c == 0), stop=(c == NB - 1)),
                       inc=("t", 1) if c == NB - 1 else None)
            tm = sc.now("t")
            self.v(lambda e, Pw=Pw: e.tensor_copy(self.wsrow[0:1, 0:768], Pw[0:1, 0:768]),
                   waits=[("t", tm)], inc=("v", 1))
            vm = sc.now("v")
            self.bank_set("P2a")
            for (o0, n0) in ((0, 512), (512, 256)):
                wb = self.bank_wait("P1a")
                self.t(lambda e, o0=o0, n0=n0: e.matmul(
                    self.P1[0][:, 0:n0], self.ones_r[:], self.wsrow[0:1, o0:o0 + n0],
                    start=True, stop=True), waits=wb + ([("v", vm)] if o0 == 0 else []),
                    inc=("t", 1))
                tm = sc.now("t")
                self.v(lambda e, o0=o0, n0=n0: e.tensor_copy(W1Sb[:, o0:o0 + n0], self.P1[0][:, 0:n0]),
                       waits=[("t", tm)], inc=("v", 1))
                self.bank_set("P1a")
            w1sv = sc.now("v")
            self.g(lambda e: e.memset(vaug[:, :, 128:129], 1.0),
                   waits=[("v", sc.now("v")), ("t", sc.now("t"))], inc=("p", 1))
            pmem = sc.now("p")
            for tb in range(NB):
                bk = "P2a" if tb % 2 == 0 else "P2b"
                P = self.P2[tb % 2]
                wb = self.bank_wait(bk)
                for c in range(NB):
                    self.t(lambda e, c=c, tb=tb, P=P: e.matmul(
                        P[:, 0:512], self.x[:, c * TOK + tb * 128: c * TOK + (tb + 1) * 128],
                        wq[:, c, :], start=(c == 0), stop=(c == NB - 1)),
                        waits=(wb + [("qw", qn), ("qio", tabq)]) if c == 0 else [])
                for c in range(NB):
                    self.t(lambda e, c=c, tb=tb, P=P: e.matmul(
                        P[:, 512:640], self.x[:, c * TOK + tb * 128: c * TOK + (tb + 1) * 128],
                        wk[:, c, :], start=(c == 0), stop=(c == NB - 1)))
                for c in range(NB):
                    self.t(lambda e, c=c, tb=tb, P=P: e.matmul(
                        P[:, 640:768], self.x[:, c * TOK + tb * 128: c * TOK + (tb + 1) * 128],
                        wv[:, c, :], start=(c == 0), stop=(c == NB - 1)),
                        inc=("t", 1) if c == NB - 1 else None)
                tm = sc.now("t")
                self.v(lambda e, tb=tb: e.tensor_scalar(
                    fx0[:], W1Sb[:], self.mucol[:, tb:tb + 1], None, ALU.mult, ALU.bypass),
                    waits=[("t", tm), ("v", w1sv), ("p", pmem), ("qio", tabq)], inc=("v", 1))
                self.vd()
                self.v(lambda e, tb=tb, P=P: e.scalar_tensor_tensor(
                    fx0[:], P[:, 0:768], self.rcol[:, tb:tb + 1], fx0[:], ALU.mult, ALU.subtract),
                    inc=("v", 1))
                self.vd()
                self.bank_set(bk)
                self.v(lambda e, tb=tb: e.tensor_copy(vaug[:, tb, 0:128], fx0[:, 640:768]), inc=("v", 1))
                qk5 = fx0[:, 0:640].rearrange("p (h d) -> p h d", h=5)
                f15 = fx1[:].rearrange("p (h d) -> p h d", h=5)
                f25 = fx2[:].rearrange("p (h d) -> p h d", h=5)
                self.v(lambda e, tb=tb, qk5=qk5, f15=f15: e.tensor_tensor(
                    f15, qk5, cosA[:, tb, :].unsqueeze(1).broadcast_to([128, 5, 128]), ALU.mult))
                self.v(lambda e, tb=tb, qk5=qk5, f25=f25: e.tensor_tensor(
                    f25[:, :, 0:64], qk5[:, :, 64:128],
                    sinA[:, tb, 0:64].unsqueeze(1).broadcast_to([128, 5, 64]), ALU.mult))
                self.v(lambda e, tb=tb, qk5=qk5, f25=f25: e.tensor_tensor(
                    f25[:, :, 64:128], qk5[:, :, 0:64],
                    sinA[:, tb, 64:128].unsqueeze(1).broadcast_to([128, 5, 64]), ALU.mult))
                self.vd()
                self.v(lambda e: e.tensor_tensor(fx1[:], fx1[:], fx2[:], ALU.add), inc=("v", 1))
                self.vd()
                self.v(lambda e: e.tensor_scalar(fx0[:, 0:640], fx1[:], 0.0, None, ALU.min, ALU.bypass),
                       inc=("v", 1))
                vv = sc.now("v")
                self.s(lambda e: e.activation(fx2[:], fx0[:, 0:640], AF.Exp), waits=[("v", vv)],
                       inc=("s", 1))
                sv = sc.now("s")
                self.v(lambda e: e.tensor_scalar(fx0[:, 0:640], fx1[:], 0.0, None, ALU.max, ALU.bypass),
                       waits=[("s", sv)])
                self.vd()
                self.v(lambda e, tb=tb: e.tensor_tensor(qtok[:, tb, :], fx2[:, 0:512],
                                                        fx0[:, 0:512], ALU.add))
                self.v(lambda e, tb=tb: e.tensor_tensor(ktok[:, tb, :], fx2[:, 512:640],
                                                        fx0[:, 512:640], ALU.add), inc=("v", 1))
                self.vd()
            projv = sc.now("v")
            self.bank["wq_t"] = ("t", sc.now("t"))
            if grp == 1:
                self.tap("qtok", lambda: qtok[:].rearrange("p a b -> p (a b)"), [128, NB * 512], BF16)
                self.tap("ktok", lambda: ktok[:].rearrange("p a b -> p (a b)"), [128, NB * 128], BF16)
                self.tap("vaug", lambda: vaug[:].rearrange("p a b -> p (a b)"), [128, NB * 132], BF16)
            if stop == f"atproj{grp}":
                return
            def tr_batch(srcs, dst, n_in, waitv):
                wpt = self.bank_wait("PT")
                for j, sap in enumerate(srcs):
                    self.t(lambda e, j=j, sap=sap: e.transpose(
                        self.PT[:, j * 128:(j + 1) * 128], sap, self.idn[:]),
                        waits=(wpt + [("v", waitv)]) if j == 0 else [],
                        inc=("t", 1) if j == len(srcs) - 1 else None)
                tm = sc.now("t")
                self.v(lambda e, dst=dst, n=len(srcs): e.tensor_copy(
                    dst, self.PT[:, 0:n * 128]), waits=[("t", tm)], inc=("v", 1))
                self.bank_set("PT")
            for hh in range(4):
                for half in range(2):
                    tr_batch([qtok[:, half * 8 + j, hh * 128:(hh + 1) * 128] for j in range(8)],
                             qfT[:, hh, half * 1024:(half + 1) * 1024], 8, projv)
            for half in range(2):
                tr_batch([ktok[:, half * 8 + j, :] for j in range(8)],
                         kT[:, half * 1024:(half + 1) * 1024], 8, projv)
            trv = sc.now("v")
            psS = self.P1[2][:, 0:132]
            self.g(lambda e: e.memset(ssb[:], 0.0), waits=[("t", sc.now("t"))], inc=("p", 1))
            pz = sc.now("p")
            ssb_v = 0
            st_t = 0
            for cchunk in range(NB):
                bkA = "P1a" if cchunk % 2 == 0 else "P1b"
                PA = self.P1[cchunk % 2]
                wbA = self.bank_wait(bkA)
                for hh in range(4):
                    self.t(lambda e, cchunk=cchunk, hh=hh, PA=PA: e.matmul(
                        PA[:, hh * 128:(hh + 1) * 128], kT[:, cchunk * 128:(cchunk + 1) * 128],
                        qfT[:, hh, cchunk * 128:(cchunk + 1) * 128], start=True, stop=True),
                        waits=(wbA + [("v", trv)]) if hh == 0 else [],
                        inc=("t", 1) if hh == 3 else None)
                tmA = sc.now("t")
                am = amT[cchunk % 2]
                self.v(lambda e, PA=PA, am=am: e.tensor_tensor(
                    am[:].rearrange("p (h d) -> p h d", h=4),
                    PA[:].rearrange("p (h d) -> p h d", h=4),
                    self.msk[:].unsqueeze(1).broadcast_to([128, 4, 128]), ALU.mult),
                    waits=[("t", tmA)], inc=("v", 1))
                amv = sc.now("v")
                self.bank_set(bkA)
                bkO = "P2a" if cchunk % 2 == 0 else "P2b"
                PO = self.P2[cchunk % 2]
                wbO = self.bank_wait(bkO)
                for hh in range(4):
                    self.t(lambda e, cchunk=cchunk, hh=hh, PO=PO, am=am: e.matmul(
                        PO[:, OHEAD[hh]:OHEAD[hh] + 129], am[:, hh * 128:(hh + 1) * 128],
                        vaug[:, cchunk, 0:129], start=True, stop=False),
                        waits=(wbO + [("v", amv), ("v", ssb_v), ("p", pz)]) if hh == 0 else [])
                    self.t(lambda e, cchunk=cchunk, hh=hh, PO=PO: e.matmul(
                        PO[:, OHEAD[hh]:OHEAD[hh] + 129], qfT[:, hh, cchunk * 128:(cchunk + 1) * 128],
                        ssb[:, 0:129], start=False, stop=True),
                        inc=("t", 1) if hh == 3 else None)
                tmO = sc.now("t")
                st_t = self.t(lambda e, cchunk=cchunk: e.matmul(
                    psS[:, 0:129], ktok[:, cchunk, :], vaug[:, cchunk, 0:129],
                    start=(cchunk == 0), stop=True, skip_group_check=True),
                    waits=[("v", ssb_v)], inc=("t", 1))
                self.v(lambda e: e.tensor_copy(ssb[:, 0:129], psS[:, 0:129]),
                       waits=[("t", st_t)], inc=("v", 1))
                ssb_v = sc.now("v")
                self.v(lambda e, cchunk=cchunk, PO=PO: e.tensor_copy(
                    osb[:, cchunk, :, :],
                    PO[:, 0:1024].rearrange("p (a b) -> p a b", a=4)[:, :, 0:129]),
                    waits=[("t", tmO)], inc=("v", 1))
                self.bank_set(bkO)
            scan_t = sc.now("t")
            if grp == 1:
                self.tap("osb", lambda: osb[:].rearrange("p a b c -> p (a b c)"), [128, NB * 4 * 129], BF16)
            if stop == f"atscan{grp}":
                self.v(lambda e: e.tensor_copy(sfin[:, 0:129], psS[:, 0:129]),
                       waits=[("t", scan_t)], inc=("v", 1))
                return
            self.v(lambda e: e.tensor_copy(sfin[:, 0:129], psS[:, 0:129]),
                   waits=[("t", scan_t)], inc=("v", 1))
            self.v(lambda e: e.memset(sfin[:, 129:132], 0.0), inc=("v", 1))
            sfv = sc.now("v")
            self.g(lambda e, grp=grp: e.dma_start(out=self.cc_in[grp][:], in_=sfin[:]),
                   waits=[("v", sfv)], inc=("qw", 16))
            qcc = sc.now("qw")
            self.g(lambda e, grp=grp: e.collective_compute(
                "AllGather", ALU.bypass, replica_groups=[list(range(NCORES))],
                ins=[self.cc_in[grp][:]], outs=[self.cc_out[grp][:]]),
                waits=[("qw", qcc)], inc=("cc", 1))
            ccv = sc.now("cc")
            self.g(lambda e, grp=grp: e.dma_start(
                out=sg[:].rearrange("p a b -> p a b"),
                in_=self.cc_out[grp][:].rearrange("(r p) n -> p r n", p=128)),
                waits=[("cc", ccv)], inc=("qw", 16))
            qsg = sc.now("qw")
            self.v(lambda e: e.tensor_scalar(acc[0][:], sg[:, 0, :], self.flg[:, 0:1], None,
                                             ALU.mult, ALU.bypass), waits=[("qw", qsg)], inc=("v", 1))
            for r in range(1, NCORES):
                self.vd()
                self.v(lambda e, r=r: e.scalar_tensor_tensor(
                    acc[r % 2][:], sg[:, r, :], self.flg[:, r:r + 1], acc[(r - 1) % 2][:],
                    ALU.mult, ALU.add), inc=("v", 1))
            self.vd()
            self.v(lambda e: e.tensor_copy(sinb[:], acc[(NCORES - 1) % 2][:]), inc=("v", 1))
            sinv = sc.now("v")
            ofin = qtok
            for cchunk in range(NB):
                bkO = "P2a" if cchunk % 2 == 0 else "P2b"
                PO = self.P2[cchunk % 2]
                wbO = self.bank_wait(bkO)
                for hh in range(4):
                    self.t(lambda e, cchunk=cchunk, hh=hh, PO=PO: e.matmul(
                        PO[:, OHEAD[hh]:OHEAD[hh] + 129], qfT[:, hh, cchunk * 128:(cchunk + 1) * 128],
                        sinb[:, 0:129], start=True, stop=True),
                        waits=(wbO + [("v", sinv)]) if hh == 0 else [],
                        inc=("t", 1) if hh == 3 else None)
                tm = sc.now("t")
                self.v(lambda e, cchunk=cchunk, PO=PO: e.tensor_tensor(
                    ft[:],
                    PO[:, 0:1024].rearrange("p (a b) -> p a b", a=4)[:, :, 0:129],
                    osb[:, cchunk, :, :], ALU.add), waits=[("t", tm)], inc=("v", 1))
                self.vd()
                self.bank_set(bkO)
                self.v(lambda e: e.tensor_scalar(den[:, 0:4], ft[:, :, 128], EPS_AT, None,
                                                 ALU.add, ALU.bypass), inc=("v", 1))
                self.vd()
                self.v(lambda e: e.reciprocal(den[:, 4:8], den[:, 0:4]), inc=("v", 1))
                self.vd()
                for hh in range(4):
                    self.v(lambda e, cchunk=cchunk, hh=hh: e.tensor_scalar(
                        ofin[:, cchunk, hh * 128:(hh + 1) * 128], ft[:, hh, 0:128],
                        den[:, 4 + hh:5 + hh], None, ALU.mult, ALU.bypass),
                        inc=("v", 1) if hh == 3 else None)
                self.vd()
            corrv = sc.now("v")
            for hh in range(4):
                for half in range(2):
                    tr_batch([ofin[:, half * 8 + j, hh * 128:(hh + 1) * 128] for j in range(8)],
                             OT[:, hh, half * 1024:(half + 1) * 1024], 8, corrv)
            otv = sc.now("v")
            self.q(lambda e, grp=grp: e.dma_start(
                out=self.otd[grp * 512:(grp + 1) * 512, :].rearrange("(c p) n -> p c n", p=128),
                in_=OT[:]), waits=[("v", otv)], inc=("qio", 16))
            self.ot_spill_q = sc.now("qio")
            self.bank["wo_t"] = ("t", sc.now("t"))
            prev_corr_v = sc.now("v")
            if stop == f"atg{grp}":
                return
        otf = self.av(0, [NB, TOK], BF16)
        wos = [self.av(64 + 4 * i, [NB, 128], BF16) for i in range(2)]
        self.q(lambda e: e.dma_start(
            out=otf[:], in_=self.otd[:].rearrange("(c p) n -> p c n", p=128)),
            waits=[("qio", sc.now("qio")), ("t", sc.now("t"))], inc=("qio", 16))
        qotf = sc.now("qio")
        wo_t = [0, 0]
        for mb in range(NB):
            sl = mb % 2
            self.g(lambda e, mb=mb, sl=sl: e.dma_start(
                out=wos[sl][:],
                in_=self.woT[:, mb * 128:(mb + 1) * 128].rearrange("(c p) n -> p c n", p=128)),
                waits=([("t", wo_t[sl])] if mb >= 2 else
                       [("t", sc.now("t")), ("v", sc.now("v")), ("s", sc.now("s"))]),
                inc=("qw", 16))
            qn = sc.now("qw")
            for n in range(4):
                bk = "P1a" if sl == 0 else "P1b"
                P = self.P1[sl]
                wb = self.bank_wait(bk) if n == 0 else [("v", sc.now("v"))]
                for kc in range(NB):
                    wo_t[sl] = self.t(lambda e, mb=mb, n=n, kc=kc, P=P, sl=sl: e.matmul(
                        P[:], wos[sl][:, kc, :], otf[:, kc, n * 512:(n + 1) * 512],
                        start=(kc == 0), stop=(kc == NB - 1)),
                        waits=(wb + [("qw", qn), ("qio", qotf)]) if kc == 0 else [],
                        inc=("t", 1) if kc == NB - 1 else None) or wo_t[sl]
                tm = sc.now("t")
                xsl = self.x[:, mb * TOK + n * 512: mb * TOK + (n + 1) * 512]
                self.v(lambda e, P=P, xsl=xsl: e.tensor_tensor(xsl, P[:], xsl, ALU.add),
                       waits=[("t", tm)], inc=("v", 1))
                self.bank_set(bk)

    def store_out(self):
        sc = self.sc
        self.g(lambda e: e.dma_start(
            out=self.xout[:].rearrange("(c p) t -> p c t", p=128),
            in_=self.x[:].rearrange("p (c t) -> p c t", c=NB)),
            waits=self.all_now(), inc=("qw", 16))
        fin = sc.now("qw")
        self.g(lambda e: e.nop(), waits=[("qw", fin)])


def _host_tables(core):
    b, half = core // 2, core % 2
    inv = 1.0 / (10000.0 ** (np.arange(0, HD, 2, dtype=np.float64) / HD))
    t = np.arange(S_FULL := 4096, dtype=np.float64)
    fr = np.outer(t, inv)
    emb = np.concatenate([fr, fr], axis=-1)
    cos = np.cos(emb).astype(np.float32)
    sin = np.sin(emb).astype(np.float32)
    sgn = np.ones((1, HD), np.float32)
    sgn[0, :64] = -1.0
    sinS = sin * sgn
    sl = slice(half * TOK, (half + 1) * TOK)
    def fold(tab):
        return np.ascontiguousarray(
            tab[sl].reshape(NB, 128, HD).transpose(1, 0, 2).reshape(128, NB * HD)).astype(bfdt)
    return fold(cos), fold(sinS)


def kernel(L0_attn_x, L0_attn_Wq, L0_attn_Wk, L0_attn_Wv, L0_attn_Wo,
           L0_ffn_gate_w, L0_ffn_gate_bias, L0_ffn_fc1_w, L0_ffn_fc2_w,
           L0_ffn_w1, L0_ffn_w2, L0_ffn_shared_up_w, L0_ffn_shared_down_w):
    f32 = np.float32
    C = np.ascontiguousarray
    wqT = C(np.asarray(L0_attn_Wq, f32).T)
    wkT = C(np.asarray(L0_attn_Wk, f32).T)
    wvT = C(np.asarray(L0_attn_Wv, f32).T)
    woT = C(np.asarray(L0_attn_Wo, f32).T)
    gwT = C(np.asarray(L0_ffn_gate_w, f32).T)
    biasT = C(np.broadcast_to(np.asarray(L0_ffn_gate_bias, f32), (128, NE)))
    fc1T = C(np.asarray(L0_ffn_fc1_w, f32).T)
    fc2T = C(np.asarray(L0_ffn_fc2_w, f32).T)
    w1T = C(np.asarray(L0_ffn_w1, f32).transpose(0, 2, 1).reshape(NE * LAT, INTER))
    w2T = C(np.asarray(L0_ffn_w2, f32).transpose(0, 2, 1).reshape(NE * INTER, LAT))
    supT = C(np.asarray(L0_ffn_shared_up_w, f32).T)
    sdnT = C(np.asarray(L0_ffn_shared_down_w, f32).T)
    maskU = np.triu(np.ones((128, 128), np.float32)).astype(bfdt)
    ident = np.eye(128, dtype=np.float32).astype(bfdt)
    ehot = np.zeros((NE, NE * 128), np.float32)
    for e in range(NE):
        ehot[e, e * 128:(e + 1) * 128] = 1.0
    ehot = ehot.astype(bfdt)
    x = np.asarray(L0_attn_x, f32)

    import os
    mk = MK(stop_after=os.environ.get("MK_STOP") or None)
    nc = mk.build()
    in_maps = _in_maps(x, wqT, wkT, wvT, woT, gwT, biasT, fc1T, fc2T, w1T, w2T,
                       supT, sdnT, maskU, ident, ehot)
    res = run_bass_kernel_spmd(nc, in_maps, core_ids=list(range(NCORES)))
    out = np.empty((4, 4096, DM), f32)
    for c in range(NCORES):
        b, half = c // 2, c % 2
        out[b, half * TOK:(half + 1) * TOK, :] = np.asarray(res.results[c]["xout"]).T
    return out


def _in_maps(x, wqT, wkT, wvT, woT, gwT, biasT, fc1T, fc2T, w1T, w2T,
             supT, sdnT, maskU, ident, ehot):
    f32 = np.float32
    C = np.ascontiguousarray
    in_maps = []
    for c in range(NCORES):
        b, half = c // 2, c % 2
        xin = C(x[b, half * TOK:(half + 1) * TOK, :].T)
        cosA, sinA = _host_tables(c)
        flags = np.zeros((128, NCORES), f32)
        if half == 1:
            flags[:, c - 1] = 1.0
        in_maps.append(dict(
            xin=xin, wqT=wqT, wkT=wkT, wvT=wvT, woT=woT, gwT=gwT, biasT=biasT,
            fc1T=fc1T, fc2T=fc2T, w1T=w1T, w2T=w2T, supT=supT, sdnT=sdnT,
            cosT=cosA, sinT=sinA, maskT=maskU, identT=ident, ehotT=ehot,
            flagsT=flags))
    return in_maps


def _host_prep(inputs):
    f32 = np.float32
    C = np.ascontiguousarray
    wqT = C(np.asarray(inputs["L0_attn_Wq"], f32).T)
    wkT = C(np.asarray(inputs["L0_attn_Wk"], f32).T)
    wvT = C(np.asarray(inputs["L0_attn_Wv"], f32).T)
    woT = C(np.asarray(inputs["L0_attn_Wo"], f32).T)
    gwT = C(np.asarray(inputs["L0_ffn_gate_w"], f32).T)
    biasT = C(np.broadcast_to(np.asarray(inputs["L0_ffn_gate_bias"], f32), (128, NE)))
    fc1T = C(np.asarray(inputs["L0_ffn_fc1_w"], f32).T)
    fc2T = C(np.asarray(inputs["L0_ffn_fc2_w"], f32).T)
    w1T = C(np.asarray(inputs["L0_ffn_w1"], f32).transpose(0, 2, 1).reshape(NE * LAT, INTER))
    w2T = C(np.asarray(inputs["L0_ffn_w2"], f32).transpose(0, 2, 1).reshape(NE * INTER, LAT))
    supT = C(np.asarray(inputs["L0_ffn_shared_up_w"], f32).T)
    sdnT = C(np.asarray(inputs["L0_ffn_shared_down_w"], f32).T)
    maskU = np.triu(np.ones((128, 128), np.float32)).astype(bfdt)
    ident = np.eye(128, dtype=np.float32).astype(bfdt)
    ehot = np.zeros((NE, NE * 128), np.float32)
    for e in range(NE):
        ehot[e, e * 128:(e + 1) * 128] = 1.0
    ehot = ehot.astype(bfdt)
    x = np.asarray(inputs["L0_attn_x"], f32)
    return _in_maps(x, wqT, wkT, wvT, woT, gwT, biasT, fc1T, fc2T, w1T, w2T,
                    supT, sdnT, maskU, ident, ehot)


# version 03
# speedup vs baseline: 1.0493x; 1.0493x over previous
import numpy as np
import ml_dtypes
from contextlib import ExitStack

import concourse.bass as bass
import concourse.mybir as mybir
from concourse.bass_utils import run_bass_kernel_spmd

F32 = mybir.dt.float32
BF16 = mybir.dt.bfloat16
AF = mybir.ActivationFunctionType
ALU = mybir.AluOpType
AX = mybir.AxisListType

DM = 2048
TOK = 2048
NB = 16
LAT = 1024
INTER = 1024
NE = 32
SH = 4096
NKV = 4
HD = 128
EPS_LN = 1e-5
EPS_AT = 1e-6
NCORES = 8
KB = 1024
ARENA_KB = 120
BIG = 1e30

bfdt = ml_dtypes.bfloat16


class Sched:
    ENGS = ("tensor", "vector", "scalar", "gpsimd", "sync")

    def __init__(self):
        self.ops = []
        self.cnt = {}

    def op(self, eng, fn, waits=(), inc=None):
        ws = [(s, v) for (s, v) in waits if v and v > 0]
        self.ops.append((eng, fn, ws, inc))
        if inc is not None:
            self.cnt[inc[0]] = self.cnt.get(inc[0], 0) + inc[1]
            return self.cnt[inc[0]]
        return None

    def now(self, sem):
        return self.cnt.get(sem, 0)

    def emit(self, block, sems):
        for name in self.ENGS:
            def emitter(eng, eng_name=name):
                last = {}
                for (e, fn, ws, inc) in self.ops:
                    if e != eng_name:
                        continue
                    for (s, v) in ws:
                        if last.get(s, -1) >= v:
                            continue
                        eng.wait_ge(sems[s], v)
                        last[s] = v
                    instr = fn(eng)
                    if inc is not None:
                        instr.then_inc(sems[inc[0]], inc[1])
            getattr(block, name)(emitter)


class MK:
    def __init__(self, dbg=(), stop_after=None):
        self.nc = bass.Bass("TRN2", target_bir_lowering=False)
        self.sc = Sched()
        self.dbg_names = set(dbg)
        self.dbg_t = {}
        self.stop_after = stop_after
        self._dram()

    def _dram(self):
        nc = self.nc

        def din(n, s):
            return nc.dram_tensor(n, s, F32, kind="ExternalInput").ap()

        def dbin(n, s):
            return nc.dram_tensor(n, s, BF16, kind="ExternalInput").ap()

        self.xin = din("xin", [DM, TOK])
        self.wqT = din("wqT", [DM, DM])
        self.wkT = din("wkT", [DM, NKV * HD])
        self.wvT = din("wvT", [DM, NKV * HD])
        self.woT = din("woT", [DM, DM])
        self.gwT = din("gwT", [DM, NE])
        self.biasT = din("biasT", [128, NE])
        self.fc1T = din("fc1T", [DM, LAT])
        self.fc2T = din("fc2T", [LAT, DM])
        self.w1T = din("w1T", [NE * LAT, INTER])
        self.w2T = din("w2T", [NE * INTER, LAT])
        self.supT = din("supT", [DM, SH])
        self.sdnT = din("sdnT", [SH, DM])
        self.cosT = dbin("cosT", [128, NB * HD])
        self.sinT = dbin("sinT", [128, NB * HD])
        self.maskT = dbin("maskT", [128, 128])
        self.identT = dbin("identT", [128, 128])
        self.ehotT = dbin("ehotT", [NE, NE * 128])
        self.flagsT = din("flagsT", [128, NCORES])
        self.xout = nc.dram_tensor("xout", [DM, TOK], F32, kind="ExternalOutput").ap()
        self.cc_in = [nc.dram_tensor(f"ccin{g}", [128, 132], F32).ap() for g in range(NKV)]
        self.otd = nc.dram_tensor("otd", [DM, TOK], BF16).ap()
        self.cc_out = [nc.dram_tensor(f"ccout{g}", [NCORES * 128, 132], F32).ap() for g in range(NKV)]

    def t(self, fn, waits=(), inc=None):
        return self.sc.op("tensor", fn, waits, inc)

    def v(self, fn, waits=(), inc=None):
        return self.sc.op("vector", fn, waits, inc)

    def s(self, fn, waits=(), inc=None):
        return self.sc.op("scalar", fn, waits, inc)

    def g(self, fn, waits=(), inc=None):
        return self.sc.op("gpsimd", fn, waits, inc)

    def q(self, fn, waits=(), inc=None):
        return self.sc.op("sync", fn, waits, inc)

    def vd(self):
        self.sc.op("vector", lambda e: e.drain())

    def sd(self):
        self.sc.op("scalar", lambda e: e.drain())

    def av(self, off_kb, shape, dtype=BF16, part0=None):
        n = int(np.prod(shape))
        it = {BF16: 2, F32: 4}[dtype]
        base = self.arena if part0 is None else self.arena[0:part0]
        a = base[:, off_kb * KB: off_kb * KB + n * it].bitcast(dtype)
        if len(shape) == 1:
            return a
        kw = {f"d{i}": shape[i] for i in range(len(shape))}
        pat = "p (" + " ".join(f"d{i}" for i in range(len(shape))) + ") -> p " + \
              " ".join(f"d{i}" for i in range(len(shape)))
        return a.rearrange(pat, **kw)

    def all_now(self):
        sc = self.sc
        return [(s, sc.now(s)) for s in ("t", "v", "s", "p", "qw", "qio", "cc")]

    def tap(self, name, ap_fn, shape, dtype=F32):
        if name not in self.dbg_names:
            return
        ap = self.nc.dram_tensor("dbg_" + name, shape, dtype, kind="ExternalOutput").ap()
        self.q(lambda e, ap=ap, f=ap_fn: e.dma_start(out=ap[:], in_=f()),
               waits=self.all_now(), inc=("qio", 16))
        self.dbg_t[name] = ap

    def bank_wait(self, name):
        return [self.bank.get(name, ("v", 0))]

    def bank_set(self, name, sem="v"):
        self.bank[name] = (sem, self.sc.now(sem))

    def build(self):
        nc, sc = self.nc, self.sc
        with ExitStack() as st:
            E = st.enter_context
            self.x = E(nc.sbuf_tensor("x", [128, NB * TOK], BF16))
            self.msk = E(nc.sbuf_tensor("msk", [128, 128], BF16))
            self.idn = E(nc.sbuf_tensor("idn", [128, 128], BF16))
            self.eh = E(nc.sbuf_tensor("eh", [NE, NE * 128], BF16))
            self.bias = E(nc.sbuf_tensor("bias", [128, NE], F32))
            self.flg = E(nc.sbuf_tensor("flg", [128, NCORES], F32))
            self.ones_r = E(nc.sbuf_tensor("ones_r", [1, 128], BF16))
            self.ones_c = E(nc.sbuf_tensor("ones_c", [128, 1], BF16))
            self.rrow = E(nc.sbuf_tensor("rrow", [1, TOK], BF16))
            self.murow = E(nc.sbuf_tensor("murow", [1, TOK], BF16))
            self.r2row = E(nc.sbuf_tensor("r2row", [1, TOK], BF16))
            self.rcol = E(nc.sbuf_tensor("rcol", [128, NB], F32))
            self.mucol = E(nc.sbuf_tensor("mucol", [128, NB], F32))
            self.w1s_g = E(nc.sbuf_tensor("w1s_g", [32, 1], F32))
            self.w1s_f1 = E(nc.sbuf_tensor("w1s_f1", [128, 8], F32))
            self.w1s_sup = E(nc.sbuf_tensor("w1s_sup", [128, 32], F32))
            self.wsrow = E(nc.sbuf_tensor("wsrow", [1, 1024], BF16))
            self.cst = E(nc.sbuf_tensor("cst", [128, 1], F32))
            self.arena = E(nc.sbuf_tensor("arena", [128, ARENA_KB * KB], mybir.dt.uint8))
            self.P2 = [E(nc.psum_tensor(f"P2{i}", [128, 1024], F32)) for i in range(2)]
            self.P1 = [E(nc.psum_tensor(f"P1{i}", [128, 512], F32)) for i in range(3)]
            self.PT = E(nc.psum_tensor("PT", [128, 1024], BF16))
            sems = {}
            for sname in ("t", "v", "s", "p", "qw", "qio", "cc"):
                sems[sname] = E(nc.semaphore("sem_" + sname))
            block = E(nc.Block())

            self.bank = {}
            self.w1s_done = False
            self.program()
            sc.emit(block, sems)
        return self.nc

    def program(self):
        st = self.stop_after
        self.init_load()
        if st is not None and st.startswith("ONLYAT"):
            self.attn(stop=st[6:] or None)
            self.store_out()
            return
        if st != "init":
            self.ffn(0, stop=st)
            if st is None or not st.startswith("f0"):
                self.attn(stop=st)
                if st is None or not st.startswith("at"):
                    self.ffn(1, stop=st)
        self.store_out()

    def init_load(self):
        for dst, src in ((self.msk, self.maskT), (self.idn, self.identT),
                         (self.eh, self.ehotT), (self.bias, self.biasT),
                         (self.flg, self.flagsT)):
            self.q(lambda e, d=dst, s=src: e.dma_start(out=d[:], in_=s[:]), inc=("qio", 16))
        self.g(lambda e: e.dma_start(
            out=self.x[:].rearrange("p (c t) -> p c t", c=NB),
            in_=self.xin[:].rearrange("(c p) t -> p c t", p=128)), inc=("qw", 16))
        self.g(lambda e: e.memset(self.cst[:], EPS_LN))
        self.g(lambda e: e.memset(self.ones_r[:], 1.0))
        self.g(lambda e: e.memset(self.ones_c[:], 1.0), inc=("p", 1))

    def stats(self):
        sc = self.sc
        xsq = [self.av(72 + 2 * i, [1024], BF16) for i in range(2)]
        row = [self.av(56 + 4 * i, [1024], F32)[0:1, :] for i in range(4)]
        start_w = self.all_now()
        P2 = self.P2
        xsq_t = [0, 0]
        for half in range(2):
            sum_row = P2[0][0:1, :]
            sq_row = P2[1][0:1, :]
            w_ps = self.bank_wait("P2a") + self.bank_wait("P2b")
            for c in range(NB):
                xb = self.x[:, c * TOK + half * 1024: c * TOK + (half + 1) * 1024]
                sl = xsq[c % 2]
                wv = ([("t", xsq_t[c % 2])] if (half * NB + c) >= 2 else []) + \
                     (start_w if (half == 0 and c < 2) else [])
                self.v(lambda e, xb=xb, sl=sl: e.tensor_tensor(sl[:], xb, xb, ALU.mult),
                       waits=wv, inc=("v", 1))
                vm = sc.now("v")
                for n in range(2):
                    self.t(lambda e, xb=xb, n=n, c=c, sum_row=sum_row: e.matmul(
                        sum_row[:, n * 512:(n + 1) * 512], self.ones_c[:],
                        xb[:, n * 512:(n + 1) * 512], start=(c == 0), stop=(c == NB - 1)),
                        waits=(w_ps + start_w) if (c == 0 and n == 0) else [])
                for n in range(2):
                    xsq_t[c % 2] = self.t(lambda e, sl=sl, n=n, c=c, sq_row=sq_row: e.matmul(
                        sq_row[:, n * 512:(n + 1) * 512], self.ones_c[:],
                        sl[:, n * 512:(n + 1) * 512], start=(c == 0), stop=(c == NB - 1)),
                        waits=[("v", vm)] if n == 0 else [], inc=("t", 1))
            tsum = sc.now("t")
            hs = slice(half * 1024, (half + 1) * 1024)
            self.v(lambda e, sum_row=sum_row: e.tensor_scalar(
                row[0][:], sum_row, 1.0 / DM, None, ALU.mult, ALU.bypass),
                waits=[("t", tsum)], inc=("v", 1))
            self.v(lambda e, sq_row=sq_row: e.tensor_scalar(
                row[1][:], sq_row, 1.0 / DM, None, ALU.mult, ALU.bypass), inc=("v", 1))
            self.vd()
            self.v(lambda e: e.tensor_tensor(row[2][:], row[0][:], row[0][:], ALU.mult), inc=("v", 1))
            self.vd()
            self.v(lambda e: e.tensor_tensor(row[3][:], row[1][:], row[2][:], ALU.subtract), inc=("v", 1))
            self.vd()
            vv = sc.now("v")
            self.s(lambda e: e.activation(row[1][:], row[3][:], AF.Ln, bias=self.cst[0:1, 0:1]),
                   waits=[("v", vv)], inc=("s", 1))
            self.sd()
            self.s(lambda e: e.activation(row[2][:], row[1][:], AF.Exp, scale=-0.5), inc=("s", 1))
            sv = sc.now("s")
            self.v(lambda e, hs=hs: e.tensor_copy(self.rrow[0:1, hs], row[2][:]),
                   waits=[("s", sv)], inc=("v", 1))
            self.v(lambda e, hs=hs: e.tensor_copy(self.murow[0:1, hs], row[0][:]), inc=("v", 1))
            self.v(lambda e: e.tensor_tensor(row[3][:], row[2][:], row[2][:], ALU.mult), inc=("v", 1))
            self.vd()
            self.v(lambda e, hs=hs: e.tensor_copy(self.r2row[0:1, hs], row[3][:]), inc=("v", 1))
            self.bank_set("P2a")
            self.bank_set("P2b")
        vrows = sc.now("v")
        P = self.P1[2]
        for tb in range(NB):
            wb = self.bank_wait("P1c") + ([("v", vrows)] if tb == 0 else [])
            self.t(lambda e, tb=tb: e.matmul(
                P[:, 0:1],
                self.rrow[0:1, tb * 128:(tb + 1) * 128], self.ones_r[0:1, 0:1],
                start=True, stop=True), waits=wb)
            self.t(lambda e, tb=tb: e.matmul(
                P[:, 4:5],
                self.murow[0:1, tb * 128:(tb + 1) * 128], self.ones_r[0:1, 0:1],
                start=True, stop=True), inc=("t", 1))
            tm = sc.now("t")
            self.v(lambda e, tb=tb: e.tensor_copy(
                self.rcol[:, tb:tb + 1], P[:, 0:1]), waits=[("t", tm)])
            self.v(lambda e, tb=tb: e.tensor_copy(
                self.mucol[:, tb:tb + 1], P[:, 4:5]), inc=("v", 1))
            self.bank_set("P1c")
        self.stats_v = sc.now("v")
        self.tap("rrow", lambda: self.rrow[:], [1, TOK], BF16)
        self.tap("murow", lambda: self.murow[:], [1, TOK], BF16)
        self.tap("rcol", lambda: self.rcol[:], [128, NB], F32)

    def bcast_rows(self, rows_offs):
        sc = self.sc
        P = self.P1[2]
        outs = []
        for (srow, off) in rows_offs:
            dst = self.av(off, [TOK], BF16)
            outs.append(dst)
            for n in range(4):
                wb = self.bank_wait("P1c")
                self.t(lambda e, srow=srow, n=n: e.matmul(
                    P[:], self.ones_r[:], srow[0:1, n * 512:(n + 1) * 512],
                    start=True, stop=True),
                    waits=wb + [("v", self.stats_v)], inc=("t", 1))
                tm = sc.now("t")
                self.v(lambda e, dst=dst, n=n: e.tensor_copy(dst[:, n * 512:(n + 1) * 512], P[:]),
                       waits=[("t", tm)], inc=("v", 1))
                self.bank_set("P1c")
        return outs

    def w1s_from_slot(self, wslot, nchunk, dst_col, mb, row_off, qwait):
        sc = self.sc
        P = self.P1[2]
        wb = self.bank_wait("P1c")
        for c in range(nchunk):
            self.t(lambda e, c=c: e.matmul(
                P[0:1, 0:128], self.ones_c[:], wslot[:, c, :],
                start=(c == 0), stop=(c == nchunk - 1)),
                waits=(wb + [("qw", qwait)]) if c == 0 else [],
                inc=("t", 1) if c == nchunk - 1 else None)
        tm = sc.now("t")
        ro = self.wsrow[0:1, row_off:row_off + 128]
        self.v(lambda e, ro=ro: e.tensor_copy(ro, P[0:1, 0:128]), waits=[("t", tm)], inc=("v", 1))
        vm = sc.now("v")
        self.bank_set("P1c")
        wb2 = self.bank_wait("P1c")
        self.t(lambda e, ro=ro: e.matmul(P[:, 256:257], ro, self.ones_r[0:1, 0:1],
                                         start=True, stop=True),
               waits=[("v", vm)] + wb2, inc=("t", 1))
        tm2 = sc.now("t")
        self.v(lambda e, mb=mb: e.tensor_copy(dst_col[:, mb:mb + 1], P[:, 256:257]),
               waits=[("t", tm2)], inc=("v", 1))
        self.bank_set("P1c")

    def ffn(self, idx, stop=None):
        pre = f"f{idx}"
        self.stats()
        if stop == pre + "stats":
            return
        Rb, Mub, R2b = self.bcast_rows([(self.rrow, 36), (self.murow, 40), (self.r2row, 44)])
        self.Rb, self.Mub, self.R2b = Rb, Mub, R2b
        if stop == pre + "bcast":
            return
        self.gate_and_route(idx)
        if stop == pre + "gate":
            return
        self.fc1(idx)
        if stop == pre + "fc1":
            return
        self.shared(idx)
        if stop == pre + "shared":
            return
        self.experts(idx)
        if stop == pre + "exp":
            return
        self.fc2(idx)
        self.w1s_done = True

    def gate_and_route(self, idx):
        sc = self.sc
        Rb, Mub = self.Rb, self.Mub
        gw = self.av(48, [NB, NE], BF16)
        lg = self.av(50, [NB, NE], F32)
        scr = self.av(52, [NB, NE], F32)
        sbm = self.av(54, [NB, NE], F32)
        cwa = self.av(56, [NB, NE], F32)
        sm0 = self.av(58, [NB, 4], F32)
        sm1 = self.av(59, [NB, 4], F32)
        sm2 = self.av(60, [NB, NE], F32)
        tg32 = self.av(62, [512], F32, part0=32)
        lgT = self.av(64, [TOK], BF16, part0=32)
        self.g(lambda e: e.dma_start(
            out=gw[:], in_=self.gwT[:].rearrange("(c p) n -> p c n", p=128)), inc=("qw", 16))
        qn = sc.now("qw")
        if not self.w1s_done:
            Pg = self.P1[2]
            wb = self.bank_wait("P1c")
            for c in range(NB):
                self.t(lambda e, c=c, Pg=Pg: e.matmul(
                    Pg[0:1, 0:32], self.ones_c[:], gw[:, c, :],
                    start=(c == 0), stop=(c == NB - 1)),
                    waits=(wb + [("qw", qn)]) if c == 0 else [],
                    inc=("t", 1) if c == NB - 1 else None)
            tm = sc.now("t")
            self.v(lambda e, Pg=Pg: e.tensor_copy(self.wsrow[0:1, 0:32], Pg[0:1, 0:32]),
                   waits=[("t", tm)], inc=("v", 1))
            vm = sc.now("v")
            self.bank_set("P1c")
            wb = self.bank_wait("P1c")
            self.t(lambda e, Pg=Pg: e.matmul(Pg[0:32, 256:257], self.wsrow[0:1, 0:32],
                                             self.ones_r[0:1, 0:1], start=True, stop=True),
                   waits=[("v", vm)] + wb, inc=("t", 1))
            tm = sc.now("t")
            self.v(lambda e, Pg=Pg: e.tensor_copy(self.w1s_g[:], Pg[0:32, 256:257]),
                   waits=[("t", tm)], inc=("v", 1))
            self.bank_set("P1c")
        for n in range(4):
            bk = "P1a" if n % 2 == 0 else "P1b"
            P = self.P1[n % 2]
            wb = self.bank_wait(bk)
            for c in range(NB):
                self.t(lambda e, n=n, c=c, P=P: e.matmul(
                    P[0:32, :], gw[:, c, :],
                    self.x[:, c * TOK + n * 512: c * TOK + (n + 1) * 512],
                    start=(c == 0), stop=(c == NB - 1)),
                    waits=(wb + [("qw", qn)]) if c == 0 else [],
                    inc=("t", 1) if c == NB - 1 else None)
            tm = sc.now("t")
            nsl = slice(n * 512, (n + 1) * 512)
            self.v(lambda e, P=P, nsl=nsl: e.scalar_tensor_tensor(
                tg32[:], Mub[0:32, nsl], self.w1s_g[:], P[0:32, :], ALU.mult, ALU.subtract),
                waits=[("t", tm)], inc=("v", 1))
            self.vd()
            self.v(lambda e, nsl=nsl: e.scalar_tensor_tensor(
                lgT[:, nsl], tg32[:], -1.0, Rb[0:32, nsl], ALU.mult, ALU.mult), inc=("v", 1))
            self.bank_set(bk)
        vv = sc.now("v")
        for half in range(2):
            wpt = self.bank_wait("PT")
            for j in range(8):
                tb = half * 8 + j
                self.t(lambda e, tb=tb, j=j: e.transpose(
                    self.PT[:, j * 128: j * 128 + 32], lgT[:, tb * 128:(tb + 1) * 128],
                    self.idn[0:32, 0:32]),
                    waits=(wpt + [("v", vv)]) if j == 0 else [],
                    inc=("t", 1) if j == 7 else None)
            tm = sc.now("t")
            self.v(lambda e, half=half: e.tensor_copy(
                lg[:, half * 8:(half + 1) * 8, :],
                self.PT[:].rearrange("p (a b) -> p a b", a=8)[:, :, 0:32]),
                waits=[("t", tm)], inc=("v", 1))
            self.bank_set("PT")
        vv = sc.now("v")
        self.s(lambda e: e.activation(scr[:], lg[:], AF.Sigmoid), waits=[("v", vv)], inc=("s", 1))
        sv = sc.now("s")
        V, VD = self.v, self.vd
        r3 = lambda a: a[:].rearrange("p a b -> p a b")
        g4s = sbm[:].rearrange("p a (g k) -> p a g k", g=4)
        M64 = self.av(58, [64], F32)
        aux = self.av(59, [96], F32)
        M64b, m16, m16b = aux[:, 0:64], aux[:, 64:80], aux[:, 80:96]
        EXP = lg[:].rearrange("p a b -> p (a b)")
        sbf = sbm[:].rearrange("p a b -> p (a b)")
        scf = scr[:].rearrange("p a b -> p (a b)")
        cwf = cwa[:].rearrange("p a b -> p (a b)")
        smf = sm2[:].rearrange("p a b -> p (a b)")
        V(lambda e: e.tensor_tensor(sbm[:], scr[:],
                                    self.bias[:].unsqueeze(1).broadcast_to([128, NB, NE]), ALU.add),
          waits=[("s", sv)])
        VD()
        V(lambda e: e.tensor_reduce(M64.rearrange("p (a g) -> p a g", a=NB), g4s, axis=AX.X, op=ALU.max))
        VD()
        for hh in range(2):
            V(lambda e, hh=hh: e.tensor_copy(
                EXP.rearrange("p (a k) -> p a k", a=64)[:, hh * 32:(hh + 1) * 32, :],
                M64[:, hh * 32:(hh + 1) * 32].unsqueeze(2).broadcast_to([128, 32, 8])))
        VD()
        V(lambda e: e.tensor_tensor(smf, sbf, EXP, ALU.is_ge))
        VD()
        V(lambda e: e.tensor_scalar(smf, smf, -BIG, None, ALU.mult, ALU.bypass))
        VD()
        V(lambda e: e.tensor_tensor(smf, smf, sbf, ALU.add))
        VD()
        V(lambda e: e.tensor_reduce(M64b.rearrange("p (a g) -> p a g", a=NB),
                                    sm2[:].rearrange("p a (g k) -> p a g k", g=4), axis=AX.X, op=ALU.max))
        VD()
        V(lambda e: e.tensor_tensor(M64, M64, M64b, ALU.add))
        VD()
        V(lambda e: e.tensor_reduce(m16.rearrange("p (a g) -> p a g", a=NB, g=1).squeeze(2),
                                    M64.rearrange("p (a g) -> p a g", a=NB), axis=AX.X, op=ALU.max))
        VD()
        V(lambda e: e.tensor_copy(M64b.rearrange("p (a g) -> p a g", a=NB),
                                  m16.unsqueeze(2).broadcast_to([128, 16, 4])))
        VD()
        V(lambda e: e.tensor_tensor(smf[:, 0:64], M64, M64b, ALU.is_ge))
        VD()
        V(lambda e: e.tensor_scalar(smf[:, 0:64], smf[:, 0:64], -BIG, None, ALU.mult, ALU.bypass))
        VD()
        V(lambda e: e.tensor_tensor(smf[:, 0:64], smf[:, 0:64], M64, ALU.add))
        VD()
        V(lambda e: e.tensor_reduce(m16b.rearrange("p (a g) -> p a g", a=NB, g=1).squeeze(2),
                                    smf[:, 0:64].rearrange("p (a g) -> p a g", a=NB), axis=AX.X, op=ALU.max))
        VD()
        V(lambda e: e.tensor_copy(M64b.rearrange("p (a g) -> p a g", a=NB),
                                  m16b.unsqueeze(2).broadcast_to([128, 16, 4])))
        VD()
        V(lambda e: e.tensor_tensor(smf[:, 0:64], M64, M64b, ALU.is_ge))
        VD()
        V(lambda e: e.tensor_scalar(smf[:, 0:64], smf[:, 0:64], 1.0, BIG, ALU.subtract, ALU.mult))
        VD()
        for hh in range(2):
            V(lambda e, hh=hh: e.tensor_copy(
                EXP.rearrange("p (a k) -> p a k", a=64)[:, hh * 32:(hh + 1) * 32, :],
                smf[:, hh * 32:(hh + 1) * 32].unsqueeze(2).broadcast_to([128, 32, 8])))
        VD()
        V(lambda e: e.tensor_tensor(sbf, sbf, EXP, ALU.add))
        VD()
        V(lambda e: e.memset(cwa[:], 0.0))
        VD()
        for it in range(8):
            V(lambda e: e.tensor_reduce(m16.rearrange("p (a g) -> p a g", a=NB, g=1).squeeze(2),
                                        r3(sbm), axis=AX.X, op=ALU.max))
            VD()
            for hh in range(2):
                V(lambda e, hh=hh: e.tensor_copy(
                    EXP.rearrange("p (a k) -> p a k", a=NB)[:, hh * 8:(hh + 1) * 8, :],
                    m16[:, hh * 8:(hh + 1) * 8].unsqueeze(2).broadcast_to([128, 8, 32])))
            VD()
            V(lambda e: e.tensor_tensor(smf, sbf, EXP, ALU.is_ge))
            VD()
            V(lambda e: e.scalar_tensor_tensor(EXP, smf, -BIG, sbf, ALU.mult, ALU.add))
            VD()
            V(lambda e: e.tensor_tensor(smf, smf, scf, ALU.mult))
            VD()
            V(lambda e: e.tensor_tensor(cwf, cwf, smf, ALU.add))
            VD()
            V(lambda e: e.tensor_copy(sbf, EXP))
            VD()
        V(lambda e: e.tensor_reduce(m16.rearrange("p (a g) -> p a g", a=NB, g=1).squeeze(2),
                                    r3(cwa), axis=AX.X, op=ALU.add))
        VD()
        V(lambda e: e.reciprocal(m16b, m16))
        VD()
        for hh in range(2):
            V(lambda e, hh=hh: e.tensor_copy(
                EXP.rearrange("p (a k) -> p a k", a=NB)[:, hh * 8:(hh + 1) * 8, :],
                m16b[:, hh * 8:(hh + 1) * 8].unsqueeze(2).broadcast_to([128, 8, 32])))
        VD()
        V(lambda e: e.tensor_scalar(cwf, cwf, 2.5, None, ALU.mult, ALU.bypass))
        VD()
        V(lambda e: e.tensor_tensor(cwf, cwf, EXP, ALU.mult), inc=("v", 1))
        self.vd()
        vv = sc.now("v")
        self.tap(f"cw{idx}", lambda: cwa[:].rearrange("p a b -> p (a b)"), [128, 512])
        cwb = self.av(52, [NB, NE], BF16)
        V(lambda e: e.tensor_copy(cwb[:], cwa[:]), waits=[("v", vv)], inc=("v", 1))
        vv = sc.now("v")
        cwT = self.av(32, [TOK], BF16, part0=32)
        for half in range(2):
            wpt = self.bank_wait("PT")
            for j in range(8):
                tb = half * 8 + j
                self.t(lambda e, tb=tb, j=j: e.transpose(
                    self.PT[0:32, j * 128:(j + 1) * 128],
                    cwb[:, tb, :], self.idn[:]),
                    waits=(wpt + [("v", vv)]) if j == 0 else [],
                    inc=("t", 1) if j == 7 else None)
            tm = sc.now("t")
            self.v(lambda e, half=half: e.tensor_copy(
                cwT[:, half * 1024:(half + 1) * 1024].rearrange("p (a b) -> p a b", a=8),
                self.PT[0:32].rearrange("p (a b) -> p a b", a=8)),
                waits=[("t", tm)], inc=("v", 1))
            self.bank_set("PT")
        self.cwT = cwT
        self.cwT_v = sc.now("v")

    def shared(self, idx):
        sc = self.sc
        Mub, R2b = self.Mub, self.R2b
        h_sh = self.av(56, [32, 512], BF16)
        supw = [self.av(88 + 8 * i, [NB, 256], BF16) for i in range(2)]
        sdnw = [self.av(104 + 8 * i, [32, 128], BF16) for i in range(2)]
        tf = [self.av(48 + 2 * i, [512], F32) for i in range(2)]
        rb = [self.av(52 + i, [512], BF16) for i in range(2)]
        pw = [("t", sc.now("t")), ("v", sc.now("v")), ("s", sc.now("s"))]
        self._pw = {0: pw, 1: pw}
        sup_t = [0, 0]
        sdn_t = [0, 0]
        relu_s = [0, 0]
        for grp in range(4):
            gsl = slice(grp * 512, (grp + 1) * 512)
            hshmark = [0] * 32
            for mbp in range(16):
                slp = mbp % 2
                self.g(lambda e, mbp=mbp, slp=slp: e.dma_start(
                    out=supw[slp][:],
                    in_=self.supT[:, mbp * 256:(mbp + 1) * 256].rearrange("(c p) n -> p c n", p=128)),
                    waits=([("t", sup_t[slp])] if (grp, mbp) >= (0, 2) else pw), inc=("qw", 16))
                qn = sc.now("qw")
                for sub in range(2):
                  mb = 2 * mbp + sub
                  sl = mb % 2
                  if True:
                    if not self.w1s_done and grp == 0:
                        self.w1s_from_slot(
                            supw[slp][:].rearrange("p a b -> p a b")[:, :, sub * 128:(sub + 1) * 128],
                            NB, self.w1s_sup, mb, 128, qn)
                    bk = "P1a" if sl == 0 else "P1b"
                    P = self.P1[sl]
                    wb = self.bank_wait(bk)
                    for c in range(NB):
                        sup_t[slp] = self.t(lambda e, c=c, P=P, grp=grp, slp=slp, sub=sub: e.matmul(
                            P[:], supw[slp][:, c, sub * 128:(sub + 1) * 128],
                            self.x[:, c * TOK + grp * 512: c * TOK + (grp + 1) * 512],
                            start=(c == 0), stop=(c == NB - 1)),
                            waits=(wb + [("qw", qn), ("v", self.sc.now("v"))]) if c == 0 else [],
                            inc=("t", 1) if c == NB - 1 else None) or sup_t[slp]
                    tm = sc.now("t")
                    self.v(lambda e, mb=mb, P=P, gsl=gsl, sl=sl: e.scalar_tensor_tensor(
                        tf[sl][:], Mub[:, gsl], self.w1s_sup[:, mb:mb + 1], P[:],
                        ALU.mult, ALU.subtract), waits=[("t", tm), ("s", relu_s[sl])], inc=("v", 1))
                    vv = sc.now("v")
                    self.bank_set(bk)
                    relu_s[sl] = self.s(lambda e, sl=sl: e.activation(rb[sl][:], tf[sl][:], AF.Relu, scale=-1.0),
                           waits=[("v", vv)], inc=("s", 1))
                    sv = sc.now("s")
                    self.v(lambda e, sl=sl: e.tensor_tensor(tf[sl][:], rb[sl][:], rb[sl][:], ALU.mult),
                           waits=[("s", sv)])
                    self.vd()
                    self.v(lambda e, mb=mb, sl=sl, gsl=gsl: e.tensor_tensor(
                        h_sh[:, mb, :], tf[sl][:], R2b[:, gsl], ALU.mult), inc=("v", 1))
                    hshmark[mb] = sc.now("v")
            hv = sc.now("v")
            if grp == 0:
                self.tap("hsh", lambda: h_sh[:].rearrange("p a b -> p (a b)"), [128, 32 * 512], BF16)
                self.tap("w1s_sup", lambda: self.w1s_sup[:], [128, 32], F32)
            for mb in range(NB):
                sl = mb % 2
                self.g(lambda e, mb=mb, sl=sl: e.dma_start(
                    out=sdnw[sl][:],
                    in_=self.sdnT[:, mb * 128:(mb + 1) * 128].rearrange("(c p) n -> p c n", p=128)),
                    waits=([("t", sdn_t[sl])] if (grp, mb) >= (0, 2) else pw), inc=("qw", 16))
                qn = sc.now("qw")
                bk = "P2a" if sl == 0 else "P2b"
                P = self.P2[sl][:, 0:512]
                wb = self.bank_wait(bk)
                for c in range(32):
                    ww = (wb + [("qw", qn)]) if c == 0 else []
                    if mb == 0:
                        ww = ww + [("v", hshmark[c])]
                    sdn_t[sl] = self.t(lambda e, c=c, P=P, sl=sl: e.matmul(
                        P, sdnw[sl][:, c, :], h_sh[:, c, :],
                        start=(c == 0), stop=(c == 31)),
                        waits=ww,
                        inc=("t", 1) if c == 31 else None) or sdn_t[sl]
                tm = sc.now("t")
                xsl = self.x[:, mb * TOK + grp * 512: mb * TOK + (grp + 1) * 512]
                self.v(lambda e, P=P, xsl=xsl: e.scalar_tensor_tensor(
                    xsl, P, 0.5, xsl, ALU.mult, ALU.add), waits=[("t", tm)], inc=("v", 1))
                self.bank_set(bk)

    def fc1(self, idx):
        sc = self.sc
        Rb, Mub = self.Rb, self.Mub
        xlat = self.av(0, [8, TOK], BF16)
        fw = [self.av(88 + 4 * i, [NB, 128], BF16) for i in range(2)]
        tf = [self.av(112 + 4 * i, [1024], F32) for i in range(2)]
        pw = [("t", sc.now("t")), ("v", sc.now("v")), ("s", sc.now("s"))]
        f_t = [0, 0]
        for mb in range(8):
            sl = mb % 2
            self.g(lambda e, mb=mb, sl=sl: e.dma_start(
                out=fw[sl][:],
                in_=self.fc1T[:, mb * 128:(mb + 1) * 128].rearrange("(c p) n -> p c n", p=128)),
                waits=([("t", f_t[sl])] if mb >= 2 else pw), inc=("qw", 16))
            qn = sc.now("qw")
            if not self.w1s_done:
                self.w1s_from_slot(fw[sl], NB, self.w1s_f1, mb, 256, qn)
            for nh in range(2):
                bk = "P2a" if sl == 0 else "P2b"
                P = self.P2[sl]
                wb = self.bank_wait(bk) if nh == 0 else [("v", sc.now("v"))]
                for half512 in range(2):
                    off = nh * 1024 + half512 * 512
                    for c in range(NB):
                        f_t[sl] = self.t(lambda e, c=c, off=off, P=P, sl=sl, half512=half512: e.matmul(
                            P[:, half512 * 512:(half512 + 1) * 512],
                            fw[sl][:, c, :], self.x[:, c * TOK + off: c * TOK + off + 512],
                            start=(c == 0), stop=(c == NB - 1)),
                            waits=(wb + [("qw", qn)]) if (c == 0 and half512 == 0) else [],
                            inc=("t", 1) if c == NB - 1 else None) or f_t[sl]
                tm = sc.now("t")
                tsl = slice(nh * 1024, (nh + 1) * 1024)
                self.v(lambda e, mb=mb, P=P, tsl=tsl, sl=sl: e.scalar_tensor_tensor(
                    tf[sl][:], Mub[:, tsl], self.w1s_f1[:, mb:mb + 1], P[:], ALU.mult, ALU.subtract),
                    waits=[("t", tm)], inc=("v", 1))
                self.vd()
                self.v(lambda e, mb=mb, tsl=tsl, sl=sl: e.scalar_tensor_tensor(
                    xlat[:, mb, tsl], tf[sl][:], -1.0, Rb[:, tsl], ALU.mult, ALU.mult),
                    inc=("v", 1))
                if nh == 1:
                    self.bank_set(bk)
        self.xlat_v = sc.now("v")
        self.tap(f"xlat{idx}", lambda: xlat[:].rearrange("p a b -> p (a b)"), [128, 8 * TOK], BF16)

    def experts(self, idx):
        sc = self.sc
        xlat = self.av(0, [8, TOK], BF16)
        cwT = self.cwT
        wA = [self.av(36, [8, 1024], BF16), self.av(104, [8, 1024], BF16)]
        Be = self.av(52, [TOK], BF16)
        y = self.av(56, [8, TOK], BF16)
        h = self.av(88, [8, 1024], BF16)
        pw = [("t", sc.now("t")), ("v", sc.now("v")), ("s", sc.now("s"))]
        w1_t = 0
        w2_t = 0
        yv = 0
        w1q = w2q = 0
        for e_i in range(NE):
            self.g(lambda e, e_i=e_i: e.dma_start(
                out=wA[0][:],
                in_=self.w1T[e_i * LAT:(e_i + 1) * LAT, :].rearrange("(c p) n -> p c n", p=128)),
                waits=([("t", w1_t)] if e_i > 0 else pw), inc=("qw", 16))
            w1q = sc.now("qw")
            self.g(lambda e, e_i=e_i: e.dma_start(
                out=wA[1][:],
                in_=self.w2T[e_i * INTER:(e_i + 1) * INTER, :].rearrange("(c p) n -> p c n", p=128)),
                waits=([("t", w2_t)] if e_i > 0 else pw), inc=("qw", 16))
            w2q = sc.now("qw")
            for n in range(4):
                wb = self.bank_wait("P1a")
                self.t(lambda e, e_i=e_i, n=n: e.matmul(
                    self.P1[0][:], self.eh[:, e_i * 128:(e_i + 1) * 128],
                    cwT[:, n * 512:(n + 1) * 512], start=True, stop=True),
                    waits=wb + ([("v", max(yv, self.cwT_v))] if n == 0 else []), inc=("t", 1))
                tm = sc.now("t")
                self.v(lambda e, n=n: e.tensor_copy(Be[:, n * 512:(n + 1) * 512], self.P1[0][:]),
                       waits=[("t", tm)], inc=("v", 1))
                self.bank_set("P1a")
            for half in range(2):
                hsl = slice(half * 1024, (half + 1) * 1024)
                hmark = [0] * 8
                for ib in range(8):
                    bk = "P2a" if ib % 2 == 0 else "P2b"
                    P = self.P2[ib % 2]
                    wb = self.bank_wait(bk)
                    for kc in range(8):
                        for n2 in range(2):
                            w1_t = self.t(lambda e, kc=kc, ib=ib, n2=n2, P=P, half=half: e.matmul(
                                P[:, n2 * 512:(n2 + 1) * 512],
                                wA[0][:, kc, ib * 128:(ib + 1) * 128],
                                xlat[:, kc, half * 1024 + n2 * 512: half * 1024 + (n2 + 1) * 512],
                                start=(kc == 0), stop=(kc == 7)),
                                waits=(wb + [("qw", w1q), ("v", self.xlat_v)]) if (kc == 0 and n2 == 0) else [],
                                inc=("t", 1) if (kc == 7 and n2 == 1) else None) or w1_t
                    tm = sc.now("t")
                    self.s(lambda e, ib=ib, P=P: e.activation(h[:, ib, :], P[:], AF.Relu),
                           waits=[("t", tm)], inc=("s", 1))
                    sv = sc.now("s")
                    self.v(lambda e, ib=ib: e.tensor_tensor(h[:, ib, :], h[:, ib, :], h[:, ib, :],
                                                            ALU.mult), waits=[("s", sv)])
                    self.vd()
                    self.v(lambda e, ib=ib, hsl=hsl: e.tensor_tensor(
                        h[:, ib, :], h[:, ib, :], Be[:, hsl], ALU.mult), inc=("v", 1))
                    hmark[ib] = sc.now("v")
                    self.bank_set(bk, "s")
                hv = sc.now("v")
                for mb in range(8):
                    bk = "P2a" if mb % 2 == 0 else "P2b"
                    P = self.P2[mb % 2]
                    wb = self.bank_wait(bk)
                    for kc in range(8):
                        for n2 in range(2):
                            ww = list(wb) + [("qw", w2q)] if (kc == 0 and n2 == 0) else []
                            if mb == 0 and n2 == 0:
                                ww = ww + [("v", hmark[kc])]
                            w2_t = self.t(lambda e, kc=kc, mb=mb, n2=n2, P=P: e.matmul(
                                P[:, n2 * 512:(n2 + 1) * 512],
                                wA[1][:, kc, mb * 128:(mb + 1) * 128],
                                h[:, kc, n2 * 512:(n2 + 1) * 512],
                                start=(kc == 0), stop=(kc == 7)),
                                waits=ww,
                                inc=("t", 1) if (kc == 7 and n2 == 1) else None) or w2_t
                    tm = sc.now("t")
                    if e_i == 0:
                        yv = self.v(lambda e, mb=mb, P=P, hsl=hsl: e.tensor_copy(
                            y[:, mb, hsl], P[:]), waits=[("t", tm)], inc=("v", 1))
                    else:
                        yv = self.v(lambda e, mb=mb, P=P, hsl=hsl: e.tensor_tensor(
                            y[:, mb, hsl], P[:], y[:, mb, hsl], ALU.add),
                            waits=[("t", tm)], inc=("v", 1))
                    self.bank_set(bk)
        self.y_done = sc.now("v")
        self.tap(f"ylat{idx}", lambda: y[:].rearrange("p a b -> p (a b)"), [128, 8 * TOK], BF16)

    def fc2(self, idx):
        import os as _os
        NMB = int(_os.environ.get("FC2_MB", NB))
        sc = self.sc
        y = self.av(56, [8, TOK], BF16)
        fw = [self.av(88 + 2 * i, [8, 128], BF16) for i in range(2)]
        pw = [("t", sc.now("t")), ("v", sc.now("v")), ("s", sc.now("s"))]
        f_t = [0, 0]
        for mb in range(NMB):
            sl = mb % 2
            self.g(lambda e, mb=mb, sl=sl: e.dma_start(
                out=fw[sl][:],
                in_=self.fc2T[:, mb * 128:(mb + 1) * 128].rearrange("(c p) n -> p c n", p=128)),
                waits=([("t", f_t[sl])] if mb >= 2 else pw), inc=("qw", 16))
            qn = sc.now("qw")
            for nh in range(2):
                bk = "P2a" if sl == 0 else "P2b"
                P = self.P2[sl]
                wb = self.bank_wait(bk) if nh == 0 else [("v", sc.now("v"))]
                for n2 in range(2):
                    for kc in range(8):
                        f_t[sl] = self.t(lambda e, kc=kc, nh=nh, n2=n2, P=P, sl=sl: e.matmul(
                            P[:, n2 * 512:(n2 + 1) * 512],
                            fw[sl][:, kc, :], y[:, kc, nh * 1024 + n2 * 512: nh * 1024 + (n2 + 1) * 512],
                            start=(kc == 0), stop=(kc == 7)),
                            waits=(wb + ([("qw", qn), ("v", self.y_done)] if nh == 0 else [])) if (kc == 0 and n2 == 0) else [],
                            inc=("t", 1) if kc == 7 else None) or f_t[sl]
                tm = sc.now("t")
                xsl = self.x[:, mb * TOK + nh * 1024: mb * TOK + (nh + 1) * 1024]
                self.v(lambda e, P=P, xsl=xsl: e.scalar_tensor_tensor(
                    xsl, P[:], 0.5, xsl, ALU.mult, ALU.add), waits=[("t", tm)], inc=("v", 1))
                if nh == 1:
                    self.bank_set(bk)

    def attn(self, stop=None):
        sc = self.sc
        self.stats()
        if stop == 'atstats':
            return
        cosA = self.av(112, [NB, HD], BF16)
        sinA = self.av(116, [NB, HD], BF16)
        self.q(lambda e: e.dma_start(out=cosA[:].rearrange("p a b -> p (a b)"), in_=self.cosT[:]),
               waits=[("v", sc.now("v")), ("t", sc.now("t")), ("s", sc.now("s"))], inc=("qio", 16))
        self.q(lambda e: e.dma_start(out=sinA[:].rearrange("p a b -> p (a b)"), in_=self.sinT[:]),
               inc=("qio", 16))
        tabq = sc.now("qio")
        wq = self.av(0, [NB, 512], BF16)
        wk = self.av(16, [NB, 128], BF16)
        wv = self.av(20, [NB, 128], BF16)
        fx0 = self.av(24, [768], F32)
        fx1 = self.av(27, [640], F32)
        fx2 = self.av(30, [640], F32)
        qtok = self.av(33, [NB, 512], BF16)
        OT = self.av(16, [4, TOK], BF16)
        qfT = self.av(49, [4, TOK], BF16)
        ktok = self.av(65, [NB, 128], BF16)
        kT = self.av(69, [TOK], BF16)
        vaug = self.av(73, [NB, 132], BF16)
        osb = self.av(78, [NB, 4, 129], BF16)
        sg = self.av(95, [NCORES, 132], F32)
        sfin = self.av(100, [132], F32)
        ssb = self.av(101, [132], BF16)
        sinb = self.av(102, [132], BF16)
        acc = [self.av(103, [132], F32), self.av(104, [132], F32)]
        amT = [self.av(105, [512], BF16), self.av(106, [512], BF16)]
        W1Sb = self.av(107, [768], BF16)
        ft = self.av(109, [4, 129], F32)
        den = self.av(104, [8], F32)
        woT_s = self.av(0, [4, TOK], BF16)
        OHEAD = [0, 256, 512, 768]

        prev_corr_v = 0
        for grp in range(NKV):
            pwA = [("t", sc.now("t")), ("v", sc.now("v")), ("s", sc.now("s")),
                   ("qio", sc.now("qio"))]
            wqw = self.g(lambda e, grp=grp: e.dma_start(
                out=wq[:], in_=self.wqT[:, grp * 512:(grp + 1) * 512].rearrange("(c p) n -> p c n", p=128)),
                waits=pwA, inc=("qw", 16))
            self.g(lambda e, grp=grp: e.dma_start(
                out=wk[:], in_=self.wkT[:, grp * 128:(grp + 1) * 128].rearrange("(c p) n -> p c n", p=128)),
                inc=("qw", 16))
            self.g(lambda e, grp=grp: e.dma_start(
                out=wv[:], in_=self.wvT[:, grp * 128:(grp + 1) * 128].rearrange("(c p) n -> p c n", p=128)),
                inc=("qw", 16))
            qn = sc.now("qw")
            Pw = self.P2[0]
            wb = self.bank_wait("P2a")
            for c in range(NB):
                self.t(lambda e, c=c, Pw=Pw: e.matmul(Pw[0:1, 0:512], self.ones_c[:], wq[:, c, :],
                                                      start=(c == 0), stop=(c == NB - 1)),
                       waits=(wb + [("qw", qn), ("v", prev_corr_v)]) if c == 0 else [])
            for c in range(NB):
                self.t(lambda e, c=c, Pw=Pw: e.matmul(Pw[0:1, 512:640], self.ones_c[:], wk[:, c, :],
                                                      start=(c == 0), stop=(c == NB - 1)))
            for c in range(NB):
                self.t(lambda e, c=c, Pw=Pw: e.matmul(Pw[0:1, 640:768], self.ones_c[:], wv[:, c, :],
                                                      start=(c == 0), stop=(c == NB - 1)),
                       inc=("t", 1) if c == NB - 1 else None)
            tm = sc.now("t")
            self.v(lambda e, Pw=Pw: e.tensor_copy(self.wsrow[0:1, 0:768], Pw[0:1, 0:768]),
                   waits=[("t", tm)], inc=("v", 1))
            vm = sc.now("v")
            self.bank_set("P2a")
            for (o0, n0) in ((0, 512), (512, 256)):
                wb = self.bank_wait("P1a")
                self.t(lambda e, o0=o0, n0=n0: e.matmul(
                    self.P1[0][:, 0:n0], self.ones_r[:], self.wsrow[0:1, o0:o0 + n0],
                    start=True, stop=True), waits=wb + ([("v", vm)] if o0 == 0 else []),
                    inc=("t", 1))
                tm = sc.now("t")
                self.v(lambda e, o0=o0, n0=n0: e.tensor_copy(W1Sb[:, o0:o0 + n0], self.P1[0][:, 0:n0]),
                       waits=[("t", tm)], inc=("v", 1))
                self.bank_set("P1a")
            w1sv = sc.now("v")
            self.g(lambda e: e.memset(vaug[:, :, 128:129], 1.0),
                   waits=[("v", sc.now("v")), ("t", sc.now("t"))], inc=("p", 1))
            pmem = sc.now("p")
            for tb in range(NB):
                bk = "P2a" if tb % 2 == 0 else "P2b"
                P = self.P2[tb % 2]
                wb = self.bank_wait(bk)
                for c in range(NB):
                    self.t(lambda e, c=c, tb=tb, P=P: e.matmul(
                        P[:, 0:512], self.x[:, c * TOK + tb * 128: c * TOK + (tb + 1) * 128],
                        wq[:, c, :], start=(c == 0), stop=(c == NB - 1)),
                        waits=(wb + [("qw", qn), ("qio", tabq)]) if c == 0 else [])
                for c in range(NB):
                    self.t(lambda e, c=c, tb=tb, P=P: e.matmul(
                        P[:, 512:640], self.x[:, c * TOK + tb * 128: c * TOK + (tb + 1) * 128],
                        wk[:, c, :], start=(c == 0), stop=(c == NB - 1)))
                for c in range(NB):
                    self.t(lambda e, c=c, tb=tb, P=P: e.matmul(
                        P[:, 640:768], self.x[:, c * TOK + tb * 128: c * TOK + (tb + 1) * 128],
                        wv[:, c, :], start=(c == 0), stop=(c == NB - 1)),
                        inc=("t", 1) if c == NB - 1 else None)
                tm = sc.now("t")
                self.v(lambda e, tb=tb: e.tensor_scalar(
                    fx0[:], W1Sb[:], self.mucol[:, tb:tb + 1], None, ALU.mult, ALU.bypass),
                    waits=[("t", tm), ("v", w1sv), ("p", pmem), ("qio", tabq)], inc=("v", 1))
                self.vd()
                self.v(lambda e, tb=tb, P=P: e.scalar_tensor_tensor(
                    fx0[:], P[:, 0:768], self.rcol[:, tb:tb + 1], fx0[:], ALU.mult, ALU.subtract),
                    inc=("v", 1))
                self.vd()
                self.bank_set(bk)
                self.v(lambda e, tb=tb: e.tensor_copy(vaug[:, tb, 0:128], fx0[:, 640:768]), inc=("v", 1))
                qk5 = fx0[:, 0:640].rearrange("p (h d) -> p h d", h=5)
                f15 = fx1[:].rearrange("p (h d) -> p h d", h=5)
                f25 = fx2[:].rearrange("p (h d) -> p h d", h=5)
                self.v(lambda e, tb=tb, qk5=qk5, f15=f15: e.tensor_tensor(
                    f15, qk5, cosA[:, tb, :].unsqueeze(1).broadcast_to([128, 5, 128]), ALU.mult))
                self.v(lambda e, tb=tb, qk5=qk5, f25=f25: e.tensor_tensor(
                    f25[:, :, 0:64], qk5[:, :, 64:128],
                    sinA[:, tb, 0:64].unsqueeze(1).broadcast_to([128, 5, 64]), ALU.mult))
                self.v(lambda e, tb=tb, qk5=qk5, f25=f25: e.tensor_tensor(
                    f25[:, :, 64:128], qk5[:, :, 0:64],
                    sinA[:, tb, 64:128].unsqueeze(1).broadcast_to([128, 5, 64]), ALU.mult))
                self.vd()
                self.v(lambda e: e.tensor_tensor(fx1[:], fx1[:], fx2[:], ALU.add), inc=("v", 1))
                self.vd()
                self.v(lambda e: e.tensor_scalar(fx0[:, 0:640], fx1[:], 0.0, None, ALU.min, ALU.bypass),
                       inc=("v", 1))
                vv = sc.now("v")
                self.s(lambda e: e.activation(fx2[:], fx0[:, 0:640], AF.Exp), waits=[("v", vv)],
                       inc=("s", 1))
                sv = sc.now("s")
                self.v(lambda e: e.tensor_scalar(fx0[:, 0:640], fx1[:], 0.0, None, ALU.max, ALU.bypass),
                       waits=[("s", sv)])
                self.vd()
                self.v(lambda e, tb=tb: e.tensor_tensor(qtok[:, tb, :], fx2[:, 0:512],
                                                        fx0[:, 0:512], ALU.add))
                self.v(lambda e, tb=tb: e.tensor_tensor(ktok[:, tb, :], fx2[:, 512:640],
                                                        fx0[:, 512:640], ALU.add), inc=("v", 1))
                self.vd()
            projv = sc.now("v")
            self.bank["wq_t"] = ("t", sc.now("t"))
            if grp == 1:
                self.tap("qtok", lambda: qtok[:].rearrange("p a b -> p (a b)"), [128, NB * 512], BF16)
                self.tap("ktok", lambda: ktok[:].rearrange("p a b -> p (a b)"), [128, NB * 128], BF16)
                self.tap("vaug", lambda: vaug[:].rearrange("p a b -> p (a b)"), [128, NB * 132], BF16)
            if stop == f"atproj{grp}":
                return
            def tr_batch(srcs, dst, n_in, waitv):
                wpt = self.bank_wait("PT")
                for j, sap in enumerate(srcs):
                    self.t(lambda e, j=j, sap=sap: e.transpose(
                        self.PT[:, j * 128:(j + 1) * 128], sap, self.idn[:]),
                        waits=(wpt + [("v", waitv)]) if j == 0 else [],
                        inc=("t", 1) if j == len(srcs) - 1 else None)
                tm = sc.now("t")
                self.v(lambda e, dst=dst, n=len(srcs): e.tensor_copy(
                    dst, self.PT[:, 0:n * 128]), waits=[("t", tm)], inc=("v", 1))
                self.bank_set("PT")
            for hh in range(4):
                for half in range(2):
                    tr_batch([qtok[:, half * 8 + j, hh * 128:(hh + 1) * 128] for j in range(8)],
                             qfT[:, hh, half * 1024:(half + 1) * 1024], 8, projv)
            for half in range(2):
                tr_batch([ktok[:, half * 8 + j, :] for j in range(8)],
                         kT[:, half * 1024:(half + 1) * 1024], 8, projv)
            trv = sc.now("v")
            psS = self.P1[2][:, 0:132]
            self.g(lambda e: e.memset(ssb[:], 0.0), waits=[("t", sc.now("t"))], inc=("p", 1))
            pz = sc.now("p")
            ssb_v = 0
            st_t = 0
            for cchunk in range(NB):
                bkA = "P1a" if cchunk % 2 == 0 else "P1b"
                PA = self.P1[cchunk % 2]
                wbA = self.bank_wait(bkA)
                for hh in range(4):
                    self.t(lambda e, cchunk=cchunk, hh=hh, PA=PA: e.matmul(
                        PA[:, hh * 128:(hh + 1) * 128], kT[:, cchunk * 128:(cchunk + 1) * 128],
                        qfT[:, hh, cchunk * 128:(cchunk + 1) * 128], start=True, stop=True),
                        waits=(wbA + [("v", trv)]) if hh == 0 else [],
                        inc=("t", 1) if hh == 3 else None)
                tmA = sc.now("t")
                am = amT[cchunk % 2]
                self.v(lambda e, PA=PA, am=am: e.tensor_tensor(
                    am[:].rearrange("p (h d) -> p h d", h=4),
                    PA[:].rearrange("p (h d) -> p h d", h=4),
                    self.msk[:].unsqueeze(1).broadcast_to([128, 4, 128]), ALU.mult),
                    waits=[("t", tmA)], inc=("v", 1))
                amv = sc.now("v")
                self.bank_set(bkA)
                bkO = "P2a" if cchunk % 2 == 0 else "P2b"
                PO = self.P2[cchunk % 2]
                wbO = self.bank_wait(bkO)
                for hh in range(4):
                    self.t(lambda e, cchunk=cchunk, hh=hh, PO=PO, am=am: e.matmul(
                        PO[:, OHEAD[hh]:OHEAD[hh] + 129], am[:, hh * 128:(hh + 1) * 128],
                        vaug[:, cchunk, 0:129], start=True, stop=False),
                        waits=(wbO + [("v", amv), ("v", ssb_v), ("p", pz)]) if hh == 0 else [])
                    self.t(lambda e, cchunk=cchunk, hh=hh, PO=PO: e.matmul(
                        PO[:, OHEAD[hh]:OHEAD[hh] + 129], qfT[:, hh, cchunk * 128:(cchunk + 1) * 128],
                        ssb[:, 0:129], start=False, stop=True),
                        inc=("t", 1) if hh == 3 else None)
                tmO = sc.now("t")
                st_t = self.t(lambda e, cchunk=cchunk: e.matmul(
                    psS[:, 0:129], ktok[:, cchunk, :], vaug[:, cchunk, 0:129],
                    start=(cchunk == 0), stop=True, skip_group_check=True),
                    waits=[("v", ssb_v)], inc=("t", 1))
                self.v(lambda e: e.tensor_copy(ssb[:, 0:129], psS[:, 0:129]),
                       waits=[("t", st_t)], inc=("v", 1))
                ssb_v = sc.now("v")
                self.v(lambda e, cchunk=cchunk, PO=PO: e.tensor_copy(
                    osb[:, cchunk, :, :],
                    PO[:, 0:1024].rearrange("p (a b) -> p a b", a=4)[:, :, 0:129]),
                    waits=[("t", tmO)], inc=("v", 1))
                self.bank_set(bkO)
            scan_t = sc.now("t")
            if grp == 1:
                self.tap("osb", lambda: osb[:].rearrange("p a b c -> p (a b c)"), [128, NB * 4 * 129], BF16)
            if stop == f"atscan{grp}":
                self.v(lambda e: e.tensor_copy(sfin[:, 0:129], psS[:, 0:129]),
                       waits=[("t", scan_t)], inc=("v", 1))
                return
            self.v(lambda e: e.tensor_copy(sfin[:, 0:129], psS[:, 0:129]),
                   waits=[("t", scan_t)], inc=("v", 1))
            self.v(lambda e: e.memset(sfin[:, 129:132], 0.0), inc=("v", 1))
            sfv = sc.now("v")
            self.g(lambda e, grp=grp: e.dma_start(out=self.cc_in[grp][:], in_=sfin[:]),
                   waits=[("v", sfv)], inc=("qw", 16))
            qcc = sc.now("qw")
            self.g(lambda e, grp=grp: e.collective_compute(
                "AllGather", ALU.bypass, replica_groups=[list(range(NCORES))],
                ins=[self.cc_in[grp][:]], outs=[self.cc_out[grp][:]]),
                waits=[("qw", qcc)], inc=("cc", 1))
            ccv = sc.now("cc")
            self.g(lambda e, grp=grp: e.dma_start(
                out=sg[:].rearrange("p a b -> p a b"),
                in_=self.cc_out[grp][:].rearrange("(r p) n -> p r n", p=128)),
                waits=[("cc", ccv)], inc=("qw", 16))
            qsg = sc.now("qw")
            self.v(lambda e: e.tensor_scalar(acc[0][:], sg[:, 0, :], self.flg[:, 0:1], None,
                                             ALU.mult, ALU.bypass), waits=[("qw", qsg)], inc=("v", 1))
            for r in range(1, NCORES):
                self.vd()
                self.v(lambda e, r=r: e.scalar_tensor_tensor(
                    acc[r % 2][:], sg[:, r, :], self.flg[:, r:r + 1], acc[(r - 1) % 2][:],
                    ALU.mult, ALU.add), inc=("v", 1))
            self.vd()
            self.v(lambda e: e.tensor_copy(sinb[:], acc[(NCORES - 1) % 2][:]), inc=("v", 1))
            sinv = sc.now("v")
            ofin = qtok
            for cchunk in range(NB):
                bkO = "P2a" if cchunk % 2 == 0 else "P2b"
                PO = self.P2[cchunk % 2]
                wbO = self.bank_wait(bkO)
                for hh in range(4):
                    self.t(lambda e, cchunk=cchunk, hh=hh, PO=PO: e.matmul(
                        PO[:, OHEAD[hh]:OHEAD[hh] + 129], qfT[:, hh, cchunk * 128:(cchunk + 1) * 128],
                        sinb[:, 0:129], start=True, stop=True),
                        waits=(wbO + [("v", sinv)]) if hh == 0 else [],
                        inc=("t", 1) if hh == 3 else None)
                tm = sc.now("t")
                self.v(lambda e, cchunk=cchunk, PO=PO: e.tensor_tensor(
                    ft[:],
                    PO[:, 0:1024].rearrange("p (a b) -> p a b", a=4)[:, :, 0:129],
                    osb[:, cchunk, :, :], ALU.add), waits=[("t", tm)], inc=("v", 1))
                self.vd()
                self.bank_set(bkO)
                self.v(lambda e: e.tensor_scalar(den[:, 0:4], ft[:, :, 128], EPS_AT, None,
                                                 ALU.add, ALU.bypass), inc=("v", 1))
                self.vd()
                self.v(lambda e: e.reciprocal(den[:, 4:8], den[:, 0:4]), inc=("v", 1))
                self.vd()
                for hh in range(4):
                    self.v(lambda e, cchunk=cchunk, hh=hh: e.tensor_scalar(
                        ofin[:, cchunk, hh * 128:(hh + 1) * 128], ft[:, hh, 0:128],
                        den[:, 4 + hh:5 + hh], None, ALU.mult, ALU.bypass),
                        inc=("v", 1) if hh == 3 else None)
                self.vd()
            corrv = sc.now("v")
            for hh in range(4):
                for half in range(2):
                    tr_batch([ofin[:, half * 8 + j, hh * 128:(hh + 1) * 128] for j in range(8)],
                             OT[:, hh, half * 1024:(half + 1) * 1024], 8, corrv)
            otv = sc.now("v")
            self.q(lambda e, grp=grp: e.dma_start(
                out=self.otd[grp * 512:(grp + 1) * 512, :].rearrange("(c p) n -> p c n", p=128),
                in_=OT[:]), waits=[("v", otv)], inc=("qio", 16))
            self.ot_spill_q = sc.now("qio")
            self.bank["wo_t"] = ("t", sc.now("t"))
            prev_corr_v = sc.now("v")
            if stop == f"atg{grp}":
                return
        otf = self.av(0, [NB, TOK], BF16)
        wos = [self.av(64 + 4 * i, [NB, 128], BF16) for i in range(2)]
        self.q(lambda e: e.dma_start(
            out=otf[:], in_=self.otd[:].rearrange("(c p) n -> p c n", p=128)),
            waits=[("qio", sc.now("qio")), ("t", sc.now("t"))], inc=("qio", 16))
        qotf = sc.now("qio")
        wo_t = [0, 0]
        for mb in range(NB):
            sl = mb % 2
            self.g(lambda e, mb=mb, sl=sl: e.dma_start(
                out=wos[sl][:],
                in_=self.woT[:, mb * 128:(mb + 1) * 128].rearrange("(c p) n -> p c n", p=128)),
                waits=([("t", wo_t[sl])] if mb >= 2 else
                       [("t", sc.now("t")), ("v", sc.now("v")), ("s", sc.now("s"))]),
                inc=("qw", 16))
            qn = sc.now("qw")
            for n in range(4):
                bk = "P1a" if sl == 0 else "P1b"
                P = self.P1[sl]
                wb = self.bank_wait(bk) if n == 0 else [("v", sc.now("v"))]
                for kc in range(NB):
                    wo_t[sl] = self.t(lambda e, mb=mb, n=n, kc=kc, P=P, sl=sl: e.matmul(
                        P[:], wos[sl][:, kc, :], otf[:, kc, n * 512:(n + 1) * 512],
                        start=(kc == 0), stop=(kc == NB - 1)),
                        waits=(wb + [("qw", qn), ("qio", qotf)]) if kc == 0 else [],
                        inc=("t", 1) if kc == NB - 1 else None) or wo_t[sl]
                tm = sc.now("t")
                xsl = self.x[:, mb * TOK + n * 512: mb * TOK + (n + 1) * 512]
                self.v(lambda e, P=P, xsl=xsl: e.tensor_tensor(xsl, P[:], xsl, ALU.add),
                       waits=[("t", tm)], inc=("v", 1))
                self.bank_set(bk)

    def store_out(self):
        sc = self.sc
        self.g(lambda e: e.dma_start(
            out=self.xout[:].rearrange("(c p) t -> p c t", p=128),
            in_=self.x[:].rearrange("p (c t) -> p c t", c=NB)),
            waits=self.all_now(), inc=("qw", 16))
        fin = sc.now("qw")
        self.g(lambda e: e.nop(), waits=[("qw", fin)])


def _host_tables(core):
    b, half = core // 2, core % 2
    inv = 1.0 / (10000.0 ** (np.arange(0, HD, 2, dtype=np.float64) / HD))
    t = np.arange(S_FULL := 4096, dtype=np.float64)
    fr = np.outer(t, inv)
    emb = np.concatenate([fr, fr], axis=-1)
    cos = np.cos(emb).astype(np.float32)
    sin = np.sin(emb).astype(np.float32)
    sgn = np.ones((1, HD), np.float32)
    sgn[0, :64] = -1.0
    sinS = sin * sgn
    sl = slice(half * TOK, (half + 1) * TOK)
    def fold(tab):
        return np.ascontiguousarray(
            tab[sl].reshape(NB, 128, HD).transpose(1, 0, 2).reshape(128, NB * HD)).astype(bfdt)
    return fold(cos), fold(sinS)


def kernel(L0_attn_x, L0_attn_Wq, L0_attn_Wk, L0_attn_Wv, L0_attn_Wo,
           L0_ffn_gate_w, L0_ffn_gate_bias, L0_ffn_fc1_w, L0_ffn_fc2_w,
           L0_ffn_w1, L0_ffn_w2, L0_ffn_shared_up_w, L0_ffn_shared_down_w):
    f32 = np.float32
    C = np.ascontiguousarray
    wqT = C(np.asarray(L0_attn_Wq, f32).T)
    wkT = C(np.asarray(L0_attn_Wk, f32).T)
    wvT = C(np.asarray(L0_attn_Wv, f32).T)
    woT = C(np.asarray(L0_attn_Wo, f32).T)
    gwT = C(np.asarray(L0_ffn_gate_w, f32).T)
    biasT = C(np.broadcast_to(np.asarray(L0_ffn_gate_bias, f32), (128, NE)))
    fc1T = C(np.asarray(L0_ffn_fc1_w, f32).T)
    fc2T = C(np.asarray(L0_ffn_fc2_w, f32).T)
    w1T = C(np.asarray(L0_ffn_w1, f32).transpose(0, 2, 1).reshape(NE * LAT, INTER))
    w2T = C(np.asarray(L0_ffn_w2, f32).transpose(0, 2, 1).reshape(NE * INTER, LAT))
    supT = C(np.asarray(L0_ffn_shared_up_w, f32).T)
    sdnT = C(np.asarray(L0_ffn_shared_down_w, f32).T)
    maskU = np.triu(np.ones((128, 128), np.float32)).astype(bfdt)
    ident = np.eye(128, dtype=np.float32).astype(bfdt)
    ehot = np.zeros((NE, NE * 128), np.float32)
    for e in range(NE):
        ehot[e, e * 128:(e + 1) * 128] = 1.0
    ehot = ehot.astype(bfdt)
    x = np.asarray(L0_attn_x, f32)

    import os
    mk = MK(stop_after=os.environ.get("MK_STOP") or None)
    nc = mk.build()
    in_maps = _in_maps(x, wqT, wkT, wvT, woT, gwT, biasT, fc1T, fc2T, w1T, w2T,
                       supT, sdnT, maskU, ident, ehot)
    res = run_bass_kernel_spmd(nc, in_maps, core_ids=list(range(NCORES)))
    out = np.empty((4, 4096, DM), f32)
    for c in range(NCORES):
        b, half = c // 2, c % 2
        out[b, half * TOK:(half + 1) * TOK, :] = np.asarray(res.results[c]["xout"]).T
    return out


def _in_maps(x, wqT, wkT, wvT, woT, gwT, biasT, fc1T, fc2T, w1T, w2T,
             supT, sdnT, maskU, ident, ehot):
    f32 = np.float32
    C = np.ascontiguousarray
    in_maps = []
    for c in range(NCORES):
        b, half = c // 2, c % 2
        xin = C(x[b, half * TOK:(half + 1) * TOK, :].T)
        cosA, sinA = _host_tables(c)
        flags = np.zeros((128, NCORES), f32)
        if half == 1:
            flags[:, c - 1] = 1.0
        in_maps.append(dict(
            xin=xin, wqT=wqT, wkT=wkT, wvT=wvT, woT=woT, gwT=gwT, biasT=biasT,
            fc1T=fc1T, fc2T=fc2T, w1T=w1T, w2T=w2T, supT=supT, sdnT=sdnT,
            cosT=cosA, sinT=sinA, maskT=maskU, identT=ident, ehotT=ehot,
            flagsT=flags))
    return in_maps


def _host_prep(inputs):
    f32 = np.float32
    C = np.ascontiguousarray
    wqT = C(np.asarray(inputs["L0_attn_Wq"], f32).T)
    wkT = C(np.asarray(inputs["L0_attn_Wk"], f32).T)
    wvT = C(np.asarray(inputs["L0_attn_Wv"], f32).T)
    woT = C(np.asarray(inputs["L0_attn_Wo"], f32).T)
    gwT = C(np.asarray(inputs["L0_ffn_gate_w"], f32).T)
    biasT = C(np.broadcast_to(np.asarray(inputs["L0_ffn_gate_bias"], f32), (128, NE)))
    fc1T = C(np.asarray(inputs["L0_ffn_fc1_w"], f32).T)
    fc2T = C(np.asarray(inputs["L0_ffn_fc2_w"], f32).T)
    w1T = C(np.asarray(inputs["L0_ffn_w1"], f32).transpose(0, 2, 1).reshape(NE * LAT, INTER))
    w2T = C(np.asarray(inputs["L0_ffn_w2"], f32).transpose(0, 2, 1).reshape(NE * INTER, LAT))
    supT = C(np.asarray(inputs["L0_ffn_shared_up_w"], f32).T)
    sdnT = C(np.asarray(inputs["L0_ffn_shared_down_w"], f32).T)
    maskU = np.triu(np.ones((128, 128), np.float32)).astype(bfdt)
    ident = np.eye(128, dtype=np.float32).astype(bfdt)
    ehot = np.zeros((NE, NE * 128), np.float32)
    for e in range(NE):
        ehot[e, e * 128:(e + 1) * 128] = 1.0
    ehot = ehot.astype(bfdt)
    x = np.asarray(inputs["L0_attn_x"], f32)
    return _in_maps(x, wqT, wkT, wvT, woT, gwT, biasT, fc1T, fc2T, w1T, w2T,
                    supT, sdnT, maskU, ident, ehot)


# version 07
# speedup vs baseline: 1.0879x; 1.0368x over previous
import numpy as np
import ml_dtypes
from contextlib import ExitStack

import concourse.bass as bass
import concourse.mybir as mybir
from concourse.bass_utils import run_bass_kernel_spmd

F32 = mybir.dt.float32
BF16 = mybir.dt.bfloat16
AF = mybir.ActivationFunctionType
ALU = mybir.AluOpType
AX = mybir.AxisListType

DM = 2048
TOK = 2048
NB = 16
LAT = 1024
INTER = 1024
NE = 32
SH = 4096
NKV = 4
HD = 128
EPS_LN = 1e-5
EPS_AT = 1e-6
NCORES = 8
KB = 1024
ARENA_KB = 120
BIG = 1e30

bfdt = ml_dtypes.bfloat16


class Sched:
    ENGS = ("tensor", "vector", "scalar", "gpsimd", "sync")

    def __init__(self):
        self.ops = []
        self.cnt = {}

    def op(self, eng, fn, waits=(), inc=None):
        ws = [(s, v) for (s, v) in waits if v and v > 0]
        self.ops.append((eng, fn, ws, inc))
        if inc is not None:
            self.cnt[inc[0]] = self.cnt.get(inc[0], 0) + inc[1]
            return self.cnt[inc[0]]
        return None

    def now(self, sem):
        return self.cnt.get(sem, 0)

    def emit(self, block, sems):
        for name in self.ENGS:
            def emitter(eng, eng_name=name):
                last = {}
                for (e, fn, ws, inc) in self.ops:
                    if e != eng_name:
                        continue
                    for (s, v) in ws:
                        if last.get(s, -1) >= v:
                            continue
                        eng.wait_ge(sems[s], v)
                        last[s] = v
                    instr = fn(eng)
                    if inc is not None:
                        instr.then_inc(sems[inc[0]], inc[1])
            getattr(block, name)(emitter)


class MK:
    def __init__(self, dbg=(), stop_after=None):
        self.nc = bass.Bass("TRN2", target_bir_lowering=False)
        self.sc = Sched()
        self.dbg_names = set(dbg)
        self.dbg_t = {}
        self.stop_after = stop_after
        self._dram()

    def _dram(self):
        nc = self.nc

        def din(n, s):
            return nc.dram_tensor(n, s, F32, kind="ExternalInput").ap()

        def dbin(n, s):
            return nc.dram_tensor(n, s, BF16, kind="ExternalInput").ap()

        self.xin = din("xin", [DM, TOK])
        self.wqT = din("wqT", [DM, DM])
        self.wkT = din("wkT", [DM, NKV * HD])
        self.wvT = din("wvT", [DM, NKV * HD])
        self.woT = din("woT", [DM, DM])
        self.gwT = din("gwT", [DM, NE])
        self.biasT = din("biasT", [128, NE])
        self.fc1T = din("fc1T", [DM, LAT])
        self.fc2T = din("fc2T", [LAT, DM])
        self.w1T = din("w1T", [NE * LAT, INTER])
        self.w2T = din("w2T", [NE * INTER, LAT])
        self.supT = din("supT", [DM, SH])
        self.sdnT = din("sdnT", [SH, DM])
        self.cosT = dbin("cosT", [128, NB * HD])
        self.sinT = dbin("sinT", [128, NB * HD])
        self.maskT = dbin("maskT", [128, 128])
        self.identT = dbin("identT", [128, 128])
        self.ehotT = dbin("ehotT", [NE, NE * 128])
        self.flagsT = din("flagsT", [128, NCORES])
        self.xout = nc.dram_tensor("xout", [DM, TOK], F32, kind="ExternalOutput").ap()
        self.cc_in = [nc.dram_tensor(f"ccin{g}", [128, 132], F32).ap() for g in range(NKV)]
        self.otd = nc.dram_tensor("otd", [DM, TOK], BF16).ap()
        self.cc_out = [nc.dram_tensor(f"ccout{g}", [NCORES * 128, 132], F32).ap() for g in range(NKV)]

    def t(self, fn, waits=(), inc=None):
        return self.sc.op("tensor", fn, waits, inc)

    def v(self, fn, waits=(), inc=None):
        return self.sc.op("vector", fn, waits, inc)

    def s(self, fn, waits=(), inc=None):
        return self.sc.op("scalar", fn, waits, inc)

    def g(self, fn, waits=(), inc=None):
        return self.sc.op("gpsimd", fn, waits, inc)

    def q(self, fn, waits=(), inc=None):
        return self.sc.op("sync", fn, waits, inc)

    def vd(self):
        self.sc.op("vector", lambda e: e.drain())

    def sd(self):
        self.sc.op("scalar", lambda e: e.drain())

    def av(self, off_kb, shape, dtype=BF16, part0=None):
        n = int(np.prod(shape))
        it = {BF16: 2, F32: 4}[dtype]
        base = self.arena if part0 is None else self.arena[0:part0]
        a = base[:, off_kb * KB: off_kb * KB + n * it].bitcast(dtype)
        if len(shape) == 1:
            return a
        kw = {f"d{i}": shape[i] for i in range(len(shape))}
        pat = "p (" + " ".join(f"d{i}" for i in range(len(shape))) + ") -> p " + \
              " ".join(f"d{i}" for i in range(len(shape)))
        return a.rearrange(pat, **kw)

    def all_now(self):
        sc = self.sc
        return [(s, sc.now(s)) for s in ("t", "v", "s", "p", "qw", "qio", "cc")]

    def tap(self, name, ap_fn, shape, dtype=F32):
        if name not in self.dbg_names:
            return
        ap = self.nc.dram_tensor("dbg_" + name, shape, dtype, kind="ExternalOutput").ap()
        self.q(lambda e, ap=ap, f=ap_fn: e.dma_start(out=ap[:], in_=f()),
               waits=self.all_now(), inc=("qio", 16))
        self.dbg_t[name] = ap

    def bank_wait(self, name):
        return [self.bank.get(name, ("v", 0))]

    def bank_set(self, name, sem="v"):
        self.bank[name] = (sem, self.sc.now(sem))

    def build(self):
        nc, sc = self.nc, self.sc
        with ExitStack() as st:
            E = st.enter_context
            self.x = E(nc.sbuf_tensor("x", [128, NB * TOK], BF16))
            self.msk = E(nc.sbuf_tensor("msk", [128, 128], BF16))
            self.idn = E(nc.sbuf_tensor("idn", [128, 128], BF16))
            self.eh = E(nc.sbuf_tensor("eh", [NE, NE * 128], BF16))
            self.bias = E(nc.sbuf_tensor("bias", [128, NE], F32))
            self.flg = E(nc.sbuf_tensor("flg", [128, NCORES], F32))
            self.ones_r = E(nc.sbuf_tensor("ones_r", [1, 128], BF16))
            self.ones_c = E(nc.sbuf_tensor("ones_c", [128, 1], BF16))
            self.rrow = E(nc.sbuf_tensor("rrow", [1, TOK], BF16))
            self.murow = E(nc.sbuf_tensor("murow", [1, TOK], BF16))
            self.r2row = E(nc.sbuf_tensor("r2row", [1, TOK], BF16))
            self.rcol = E(nc.sbuf_tensor("rcol", [128, NB], F32))
            self.mucol = E(nc.sbuf_tensor("mucol", [128, NB], F32))
            self.w1s_g = E(nc.sbuf_tensor("w1s_g", [32, 1], F32))
            self.w1s_f1 = E(nc.sbuf_tensor("w1s_f1", [128, 8], F32))
            self.w1s_sup = E(nc.sbuf_tensor("w1s_sup", [128, 32], F32))
            self.wsrow = E(nc.sbuf_tensor("wsrow", [1, 1024], BF16))
            self.cst = E(nc.sbuf_tensor("cst", [128, 1], F32))
            self.arena = E(nc.sbuf_tensor("arena", [128, ARENA_KB * KB], mybir.dt.uint8))
            self.P2 = [E(nc.psum_tensor(f"P2{i}", [128, 1024], F32)) for i in range(2)]
            self.P1 = [E(nc.psum_tensor(f"P1{i}", [128, 512], F32)) for i in range(3)]
            self.PT = E(nc.psum_tensor("PT", [128, 1024], BF16))
            sems = {}
            for sname in ("t", "v", "s", "p", "qw", "qio", "cc"):
                sems[sname] = E(nc.semaphore("sem_" + sname))
            block = E(nc.Block())

            self.bank = {}
            self.w1s_done = False
            self.program()
            sc.emit(block, sems)
        return self.nc

    def program(self):
        st = self.stop_after
        self.init_load()
        if st is not None and st.startswith("ONLYAT"):
            self.attn(stop=st[6:] or None)
            self.store_out()
            return
        if st != "init":
            self.ffn(0, stop=st)
            if st is None or not st.startswith("f0"):
                self.attn(stop=st)
                if st is None or not st.startswith("at"):
                    self.ffn(1, stop=st)
        self.store_out()

    def init_load(self):
        for dst, src in ((self.msk, self.maskT), (self.idn, self.identT),
                         (self.eh, self.ehotT), (self.bias, self.biasT),
                         (self.flg, self.flagsT)):
            self.q(lambda e, d=dst, s=src: e.dma_start(out=d[:], in_=s[:]), inc=("qio", 16))
        self.g(lambda e: e.dma_start(
            out=self.x[:].rearrange("p (c t) -> p c t", c=NB),
            in_=self.xin[:].rearrange("(c p) t -> p c t", p=128)), inc=("qw", 16))
        self.g(lambda e: e.memset(self.cst[:], EPS_LN))
        self.g(lambda e: e.memset(self.ones_r[:], 1.0))
        self.g(lambda e: e.memset(self.ones_c[:], 1.0), inc=("p", 1))

    def stats(self):
        sc = self.sc
        xsq = [self.av(72 + 2 * i, [1024], BF16) for i in range(2)]
        row = [self.av(56 + 4 * i, [1024], F32)[0:1, :] for i in range(4)]
        start_w = self.all_now()
        P2 = self.P2
        xsq_t = [0, 0]
        for half in range(2):
            sum_row = P2[0][0:1, :]
            sq_row = P2[1][0:1, :]
            w_ps = self.bank_wait("P2a") + self.bank_wait("P2b")
            for c in range(NB):
                xb = self.x[:, c * TOK + half * 1024: c * TOK + (half + 1) * 1024]
                sl = xsq[c % 2]
                wv = ([("t", xsq_t[c % 2])] if (half * NB + c) >= 2 else []) + \
                     (start_w if (half == 0 and c < 2) else [])
                self.v(lambda e, xb=xb, sl=sl: e.tensor_tensor(sl[:], xb, xb, ALU.mult),
                       waits=wv, inc=("v", 1))
                vm = sc.now("v")
                for n in range(2):
                    self.t(lambda e, xb=xb, n=n, c=c, sum_row=sum_row: e.matmul(
                        sum_row[:, n * 512:(n + 1) * 512], self.ones_c[:],
                        xb[:, n * 512:(n + 1) * 512], start=(c == 0), stop=(c == NB - 1)),
                        waits=(w_ps + start_w) if (c == 0 and n == 0) else [])
                for n in range(2):
                    xsq_t[c % 2] = self.t(lambda e, sl=sl, n=n, c=c, sq_row=sq_row: e.matmul(
                        sq_row[:, n * 512:(n + 1) * 512], self.ones_c[:],
                        sl[:, n * 512:(n + 1) * 512], start=(c == 0), stop=(c == NB - 1)),
                        waits=[("v", vm)] if n == 0 else [], inc=("t", 1))
            tsum = sc.now("t")
            hs = slice(half * 1024, (half + 1) * 1024)
            self.v(lambda e, sum_row=sum_row: e.tensor_scalar(
                row[0][:], sum_row, 1.0 / DM, None, ALU.mult, ALU.bypass),
                waits=[("t", tsum)], inc=("v", 1))
            self.v(lambda e, sq_row=sq_row: e.tensor_scalar(
                row[1][:], sq_row, 1.0 / DM, None, ALU.mult, ALU.bypass), inc=("v", 1))
            self.vd()
            self.v(lambda e: e.tensor_tensor(row[2][:], row[0][:], row[0][:], ALU.mult), inc=("v", 1))
            self.vd()
            self.v(lambda e: e.tensor_tensor(row[3][:], row[1][:], row[2][:], ALU.subtract), inc=("v", 1))
            self.vd()
            vv = sc.now("v")
            self.s(lambda e: e.activation(row[1][:], row[3][:], AF.Ln, bias=self.cst[0:1, 0:1]),
                   waits=[("v", vv)], inc=("s", 1))
            self.sd()
            self.s(lambda e: e.activation(row[2][:], row[1][:], AF.Exp, scale=-0.5), inc=("s", 1))
            sv = sc.now("s")
            self.v(lambda e, hs=hs: e.tensor_copy(self.rrow[0:1, hs], row[2][:]),
                   waits=[("s", sv)], inc=("v", 1))
            self.v(lambda e, hs=hs: e.tensor_copy(self.murow[0:1, hs], row[0][:]), inc=("v", 1))
            self.v(lambda e: e.tensor_tensor(row[3][:], row[2][:], row[2][:], ALU.mult), inc=("v", 1))
            self.vd()
            self.v(lambda e, hs=hs: e.tensor_copy(self.r2row[0:1, hs], row[3][:]), inc=("v", 1))
            self.bank_set("P2a")
            self.bank_set("P2b")
        vrows = sc.now("v")
        P = self.P1[2]
        for tb in range(NB):
            wb = self.bank_wait("P1c") + ([("v", vrows)] if tb == 0 else [])
            self.t(lambda e, tb=tb: e.matmul(
                P[:, 0:1],
                self.rrow[0:1, tb * 128:(tb + 1) * 128], self.ones_r[0:1, 0:1],
                start=True, stop=True), waits=wb)
            self.t(lambda e, tb=tb: e.matmul(
                P[:, 4:5],
                self.murow[0:1, tb * 128:(tb + 1) * 128], self.ones_r[0:1, 0:1],
                start=True, stop=True), inc=("t", 1))
            tm = sc.now("t")
            self.v(lambda e, tb=tb: e.tensor_copy(
                self.rcol[:, tb:tb + 1], P[:, 0:1]), waits=[("t", tm)])
            self.v(lambda e, tb=tb: e.tensor_copy(
                self.mucol[:, tb:tb + 1], P[:, 4:5]), inc=("v", 1))
            self.bank_set("P1c")
        self.stats_v = sc.now("v")
        self.tap("rrow", lambda: self.rrow[:], [1, TOK], BF16)
        self.tap("murow", lambda: self.murow[:], [1, TOK], BF16)
        self.tap("rcol", lambda: self.rcol[:], [128, NB], F32)

    def bcast_rows(self, rows_offs):
        sc = self.sc
        P = self.P1[2]
        outs = []
        for (srow, off) in rows_offs:
            dst = self.av(off, [TOK], BF16)
            outs.append(dst)
            for n in range(4):
                wb = self.bank_wait("P1c")
                self.t(lambda e, srow=srow, n=n: e.matmul(
                    P[:], self.ones_r[:], srow[0:1, n * 512:(n + 1) * 512],
                    start=True, stop=True),
                    waits=wb + [("v", self.stats_v)], inc=("t", 1))
                tm = sc.now("t")
                self.v(lambda e, dst=dst, n=n: e.tensor_copy(dst[:, n * 512:(n + 1) * 512], P[:]),
                       waits=[("t", tm)], inc=("v", 1))
                self.bank_set("P1c")
        return outs

    def w1s_from_slot(self, wslot, nchunk, dst_col, mb, row_off, qwait):
        sc = self.sc
        P = self.P1[2]
        wb = self.bank_wait("P1c")
        for c in range(nchunk):
            self.t(lambda e, c=c: e.matmul(
                P[0:1, 0:128], self.ones_c[:], wslot[:, c, :],
                start=(c == 0), stop=(c == nchunk - 1)),
                waits=(wb + [("qw", qwait)]) if c == 0 else [],
                inc=("t", 1) if c == nchunk - 1 else None)
        tm = sc.now("t")
        ro = self.wsrow[0:1, row_off:row_off + 128]
        self.v(lambda e, ro=ro: e.tensor_copy(ro, P[0:1, 0:128]), waits=[("t", tm)], inc=("v", 1))
        vm = sc.now("v")
        self.bank_set("P1c")
        wb2 = self.bank_wait("P1c")
        self.t(lambda e, ro=ro: e.matmul(P[:, 256:257], ro, self.ones_r[0:1, 0:1],
                                         start=True, stop=True),
               waits=[("v", vm)] + wb2, inc=("t", 1))
        tm2 = sc.now("t")
        self.v(lambda e, mb=mb: e.tensor_copy(dst_col[:, mb:mb + 1], P[:, 256:257]),
               waits=[("t", tm2)], inc=("v", 1))
        self.bank_set("P1c")

    def ffn(self, idx, stop=None):
        pre = f"f{idx}"
        self.stats()
        if stop == pre + "stats":
            return
        Rb, Mub, R2b = self.bcast_rows([(self.rrow, 36), (self.murow, 40), (self.r2row, 44)])
        self.Rb, self.Mub, self.R2b = Rb, Mub, R2b
        if stop == pre + "bcast":
            return
        self.gate_and_route(idx)
        if stop == pre + "gate":
            return
        self.fc1(idx)
        if stop == pre + "fc1":
            return
        self.shared(idx)
        if stop == pre + "shared":
            return
        self.experts(idx)
        if stop == pre + "exp":
            return
        self.fc2(idx)
        self.w1s_done = True

    def gate_and_route(self, idx):
        sc = self.sc
        Rb, Mub = self.Rb, self.Mub
        gw = self.av(48, [NB, NE], BF16)
        lg = self.av(50, [NB, NE], F32)
        scr = self.av(52, [NB, NE], F32)
        sbm = self.av(54, [NB, NE], F32)
        cwa = self.av(56, [NB, NE], F32)
        sm0 = self.av(58, [NB, 4], F32)
        sm1 = self.av(59, [NB, 4], F32)
        sm2 = self.av(60, [NB, NE], F32)
        tg32 = self.av(62, [512], F32, part0=32)
        lgT = self.av(64, [TOK], BF16, part0=32)
        self.g(lambda e: e.dma_start(
            out=gw[:], in_=self.gwT[:].rearrange("(c p) n -> p c n", p=128)), inc=("qw", 16))
        qn = sc.now("qw")
        if not self.w1s_done:
            Pg = self.P1[2]
            wb = self.bank_wait("P1c")
            for c in range(NB):
                self.t(lambda e, c=c, Pg=Pg: e.matmul(
                    Pg[0:1, 0:32], self.ones_c[:], gw[:, c, :],
                    start=(c == 0), stop=(c == NB - 1)),
                    waits=(wb + [("qw", qn)]) if c == 0 else [],
                    inc=("t", 1) if c == NB - 1 else None)
            tm = sc.now("t")
            self.v(lambda e, Pg=Pg: e.tensor_copy(self.wsrow[0:1, 0:32], Pg[0:1, 0:32]),
                   waits=[("t", tm)], inc=("v", 1))
            vm = sc.now("v")
            self.bank_set("P1c")
            wb = self.bank_wait("P1c")
            self.t(lambda e, Pg=Pg: e.matmul(Pg[0:32, 256:257], self.wsrow[0:1, 0:32],
                                             self.ones_r[0:1, 0:1], start=True, stop=True),
                   waits=[("v", vm)] + wb, inc=("t", 1))
            tm = sc.now("t")
            self.v(lambda e, Pg=Pg: e.tensor_copy(self.w1s_g[:], Pg[0:32, 256:257]),
                   waits=[("t", tm)], inc=("v", 1))
            self.bank_set("P1c")
        for n in range(4):
            bk = "P1a" if n % 2 == 0 else "P1b"
            P = self.P1[n % 2]
            wb = self.bank_wait(bk)
            for c in range(NB):
                self.t(lambda e, n=n, c=c, P=P: e.matmul(
                    P[0:32, :], gw[:, c, :],
                    self.x[:, c * TOK + n * 512: c * TOK + (n + 1) * 512],
                    start=(c == 0), stop=(c == NB - 1)),
                    waits=(wb + [("qw", qn)]) if c == 0 else [],
                    inc=("t", 1) if c == NB - 1 else None)
            tm = sc.now("t")
            nsl = slice(n * 512, (n + 1) * 512)
            self.v(lambda e, P=P, nsl=nsl: e.scalar_tensor_tensor(
                tg32[:], Mub[0:32, nsl], self.w1s_g[:], P[0:32, :], ALU.mult, ALU.subtract),
                waits=[("t", tm)], inc=("v", 1))
            self.vd()
            self.v(lambda e, nsl=nsl: e.scalar_tensor_tensor(
                lgT[:, nsl], tg32[:], -1.0, Rb[0:32, nsl], ALU.mult, ALU.mult), inc=("v", 1))
            self.bank_set(bk)
        vv = sc.now("v")
        for half in range(2):
            wpt = self.bank_wait("PT")
            for j in range(8):
                tb = half * 8 + j
                self.t(lambda e, tb=tb, j=j: e.transpose(
                    self.PT[:, j * 128: j * 128 + 32], lgT[:, tb * 128:(tb + 1) * 128],
                    self.idn[0:32, 0:32]),
                    waits=(wpt + [("v", vv)]) if j == 0 else [],
                    inc=("t", 1) if j == 7 else None)
            tm = sc.now("t")
            self.v(lambda e, half=half: e.tensor_copy(
                lg[:, half * 8:(half + 1) * 8, :],
                self.PT[:].rearrange("p (a b) -> p a b", a=8)[:, :, 0:32]),
                waits=[("t", tm)], inc=("v", 1))
            self.bank_set("PT")
        vv = sc.now("v")
        self.s(lambda e: e.activation(scr[:], lg[:], AF.Sigmoid), waits=[("v", vv)], inc=("s", 1))
        sv = sc.now("s")
        V, VD = self.v, self.vd
        r3 = lambda a: a[:].rearrange("p a b -> p a b")
        g4s = sbm[:].rearrange("p a (g k) -> p a g k", g=4)
        M64 = self.av(58, [64], F32)
        aux = self.av(59, [96], F32)
        M64b, m16, m16b = aux[:, 0:64], aux[:, 64:80], aux[:, 80:96]
        EXP = lg[:].rearrange("p a b -> p (a b)")
        sbf = sbm[:].rearrange("p a b -> p (a b)")
        scf = scr[:].rearrange("p a b -> p (a b)")
        cwf = cwa[:].rearrange("p a b -> p (a b)")
        smf = sm2[:].rearrange("p a b -> p (a b)")
        V(lambda e: e.tensor_tensor(sbm[:], scr[:],
                                    self.bias[:].unsqueeze(1).broadcast_to([128, NB, NE]), ALU.add),
          waits=[("s", sv)])
        VD()
        V(lambda e: e.tensor_reduce(M64.rearrange("p (a g) -> p a g", a=NB), g4s, axis=AX.X, op=ALU.max))
        VD()
        for hh in range(2):
            V(lambda e, hh=hh: e.tensor_copy(
                EXP.rearrange("p (a k) -> p a k", a=64)[:, hh * 32:(hh + 1) * 32, :],
                M64[:, hh * 32:(hh + 1) * 32].unsqueeze(2).broadcast_to([128, 32, 8])))
        VD()
        V(lambda e: e.tensor_tensor(smf, sbf, EXP, ALU.is_ge))
        VD()
        V(lambda e: e.tensor_scalar(smf, smf, -BIG, None, ALU.mult, ALU.bypass))
        VD()
        V(lambda e: e.tensor_tensor(smf, smf, sbf, ALU.add))
        VD()
        V(lambda e: e.tensor_reduce(M64b.rearrange("p (a g) -> p a g", a=NB),
                                    sm2[:].rearrange("p a (g k) -> p a g k", g=4), axis=AX.X, op=ALU.max))
        VD()
        V(lambda e: e.tensor_tensor(M64, M64, M64b, ALU.add))
        VD()
        V(lambda e: e.tensor_reduce(m16.rearrange("p (a g) -> p a g", a=NB, g=1).squeeze(2),
                                    M64.rearrange("p (a g) -> p a g", a=NB), axis=AX.X, op=ALU.max))
        VD()
        V(lambda e: e.tensor_copy(M64b.rearrange("p (a g) -> p a g", a=NB),
                                  m16.unsqueeze(2).broadcast_to([128, 16, 4])))
        VD()
        V(lambda e: e.tensor_tensor(smf[:, 0:64], M64, M64b, ALU.is_ge))
        VD()
        V(lambda e: e.tensor_scalar(smf[:, 0:64], smf[:, 0:64], -BIG, None, ALU.mult, ALU.bypass))
        VD()
        V(lambda e: e.tensor_tensor(smf[:, 0:64], smf[:, 0:64], M64, ALU.add))
        VD()
        V(lambda e: e.tensor_reduce(m16b.rearrange("p (a g) -> p a g", a=NB, g=1).squeeze(2),
                                    smf[:, 0:64].rearrange("p (a g) -> p a g", a=NB), axis=AX.X, op=ALU.max))
        VD()
        V(lambda e: e.tensor_copy(M64b.rearrange("p (a g) -> p a g", a=NB),
                                  m16b.unsqueeze(2).broadcast_to([128, 16, 4])))
        VD()
        V(lambda e: e.tensor_tensor(smf[:, 0:64], M64, M64b, ALU.is_ge))
        VD()
        V(lambda e: e.tensor_scalar(smf[:, 0:64], smf[:, 0:64], 1.0, BIG, ALU.subtract, ALU.mult))
        VD()
        for hh in range(2):
            V(lambda e, hh=hh: e.tensor_copy(
                EXP.rearrange("p (a k) -> p a k", a=64)[:, hh * 32:(hh + 1) * 32, :],
                smf[:, hh * 32:(hh + 1) * 32].unsqueeze(2).broadcast_to([128, 32, 8])))
        VD()
        V(lambda e: e.tensor_tensor(sbf, sbf, EXP, ALU.add))
        VD()
        V(lambda e: e.memset(cwa[:], 0.0))
        VD()
        for it in range(8):
            V(lambda e: e.tensor_reduce(m16.rearrange("p (a g) -> p a g", a=NB, g=1).squeeze(2),
                                        r3(sbm), axis=AX.X, op=ALU.max))
            VD()
            for hh in range(2):
                V(lambda e, hh=hh: e.tensor_copy(
                    EXP.rearrange("p (a k) -> p a k", a=NB)[:, hh * 8:(hh + 1) * 8, :],
                    m16[:, hh * 8:(hh + 1) * 8].unsqueeze(2).broadcast_to([128, 8, 32])))
            VD()
            V(lambda e: e.tensor_tensor(smf, sbf, EXP, ALU.is_ge))
            VD()
            V(lambda e: e.scalar_tensor_tensor(EXP, smf, -BIG, sbf, ALU.mult, ALU.add))
            VD()
            V(lambda e: e.tensor_tensor(smf, smf, scf, ALU.mult))
            VD()
            V(lambda e: e.tensor_tensor(cwf, cwf, smf, ALU.add))
            VD()
            V(lambda e: e.tensor_copy(sbf, EXP))
            VD()
        V(lambda e: e.tensor_reduce(m16.rearrange("p (a g) -> p a g", a=NB, g=1).squeeze(2),
                                    r3(cwa), axis=AX.X, op=ALU.add))
        VD()
        V(lambda e: e.reciprocal(m16b, m16))
        VD()
        for hh in range(2):
            V(lambda e, hh=hh: e.tensor_copy(
                EXP.rearrange("p (a k) -> p a k", a=NB)[:, hh * 8:(hh + 1) * 8, :],
                m16b[:, hh * 8:(hh + 1) * 8].unsqueeze(2).broadcast_to([128, 8, 32])))
        VD()
        V(lambda e: e.tensor_scalar(cwf, cwf, 2.5, None, ALU.mult, ALU.bypass))
        VD()
        V(lambda e: e.tensor_tensor(cwf, cwf, EXP, ALU.mult), inc=("v", 1))
        self.vd()
        vv = sc.now("v")
        self.tap(f"cw{idx}", lambda: cwa[:].rearrange("p a b -> p (a b)"), [128, 512])
        cwb = self.av(52, [NB, NE], BF16)
        V(lambda e: e.tensor_copy(cwb[:], cwa[:]), waits=[("v", vv)], inc=("v", 1))
        vv = sc.now("v")
        cwT = self.av(32, [TOK], BF16, part0=32)
        for half in range(2):
            wpt = self.bank_wait("PT")
            for j in range(8):
                tb = half * 8 + j
                self.t(lambda e, tb=tb, j=j: e.transpose(
                    self.PT[0:32, j * 128:(j + 1) * 128],
                    cwb[:, tb, :], self.idn[:]),
                    waits=(wpt + [("v", vv)]) if j == 0 else [],
                    inc=("t", 1) if j == 7 else None)
            tm = sc.now("t")
            self.v(lambda e, half=half: e.tensor_copy(
                cwT[:, half * 1024:(half + 1) * 1024].rearrange("p (a b) -> p a b", a=8),
                self.PT[0:32].rearrange("p (a b) -> p a b", a=8)),
                waits=[("t", tm)], inc=("v", 1))
            self.bank_set("PT")
        self.cwT = cwT
        self.cwT_v = sc.now("v")

    def shared(self, idx):
        sc = self.sc
        Mub, R2b = self.Mub, self.R2b
        h_sh = self.av(56, [32, 512], BF16)
        supw = [self.av(88 + 8 * i, [NB, 256], BF16) for i in range(2)]
        sdnw = [self.av(104 + 8 * i, [32, 128], BF16) for i in range(2)]
        tf = [self.av(48 + 2 * i, [512], F32) for i in range(2)]
        rb = [self.av(52 + i, [512], BF16) for i in range(2)]
        pw = [("t", sc.now("t")), ("v", sc.now("v")), ("s", sc.now("s"))]
        self._pw = {0: pw, 1: pw}
        sup_t = [0, 0]
        sdn_t = [0, 0]
        relu_s = [0, 0]
        for grp in range(4):
            gsl = slice(grp * 512, (grp + 1) * 512)
            hshmark = [0] * 32
            for mbp in range(16):
                slp = mbp % 2
                self.g(lambda e, mbp=mbp, slp=slp: e.dma_start(
                    out=supw[slp][:],
                    in_=self.supT[:, mbp * 256:(mbp + 1) * 256].rearrange("(c p) n -> p c n", p=128)),
                    waits=([("t", sup_t[slp])] if (grp, mbp) >= (0, 2) else pw), inc=("qw", 16))
                qn = sc.now("qw")
                for sub in range(2):
                  mb = 2 * mbp + sub
                  sl = mb % 2
                  if True:
                    if not self.w1s_done and grp == 0:
                        self.w1s_from_slot(
                            supw[slp][:].rearrange("p a b -> p a b")[:, :, sub * 128:(sub + 1) * 128],
                            NB, self.w1s_sup, mb, 128, qn)
                    bk = "P1a" if sl == 0 else "P1b"
                    P = self.P1[sl]
                    wb = self.bank_wait(bk)
                    for c in range(NB):
                        sup_t[slp] = self.t(lambda e, c=c, P=P, grp=grp, slp=slp, sub=sub: e.matmul(
                            P[:], supw[slp][:, c, sub * 128:(sub + 1) * 128],
                            self.x[:, c * TOK + grp * 512: c * TOK + (grp + 1) * 512],
                            start=(c == 0), stop=(c == NB - 1)),
                            waits=(wb + [("qw", qn), ("v", self.sc.now("v"))]) if c == 0 else [],
                            inc=("t", 1) if c == NB - 1 else None) or sup_t[slp]
                    tm = sc.now("t")
                    self.v(lambda e, mb=mb, P=P, gsl=gsl, sl=sl: e.scalar_tensor_tensor(
                        tf[sl][:], Mub[:, gsl], self.w1s_sup[:, mb:mb + 1], P[:],
                        ALU.mult, ALU.subtract), waits=[("t", tm), ("s", relu_s[sl])], inc=("v", 1))
                    vv = sc.now("v")
                    self.bank_set(bk)
                    relu_s[sl] = self.s(lambda e, sl=sl: e.activation(rb[sl][:], tf[sl][:], AF.Relu, scale=-1.0),
                           waits=[("v", vv)], inc=("s", 1))
                    sv = sc.now("s")
                    self.v(lambda e, sl=sl: e.tensor_tensor(tf[sl][:], rb[sl][:], rb[sl][:], ALU.mult),
                           waits=[("s", sv)])
                    self.vd()
                    self.v(lambda e, mb=mb, sl=sl, gsl=gsl: e.tensor_tensor(
                        h_sh[:, mb, :], tf[sl][:], R2b[:, gsl], ALU.mult), inc=("v", 1))
                    hshmark[mb] = sc.now("v")
            hv = sc.now("v")
            if grp == 0:
                self.tap("hsh", lambda: h_sh[:].rearrange("p a b -> p (a b)"), [128, 32 * 512], BF16)
                self.tap("w1s_sup", lambda: self.w1s_sup[:], [128, 32], F32)
            for mb in range(NB):
                sl = mb % 2
                self.g(lambda e, mb=mb, sl=sl: e.dma_start(
                    out=sdnw[sl][:],
                    in_=self.sdnT[:, mb * 128:(mb + 1) * 128].rearrange("(c p) n -> p c n", p=128)),
                    waits=([("t", sdn_t[sl])] if (grp, mb) >= (0, 2) else pw), inc=("qw", 16))
                qn = sc.now("qw")
                bk = "P2a" if sl == 0 else "P2b"
                P = self.P2[sl][:, 0:512]
                wb = self.bank_wait(bk)
                for c in range(32):
                    ww = (wb + [("qw", qn)]) if c == 0 else []
                    if mb == 0:
                        ww = ww + [("v", hshmark[c])]
                    sdn_t[sl] = self.t(lambda e, c=c, P=P, sl=sl: e.matmul(
                        P, sdnw[sl][:, c, :], h_sh[:, c, :],
                        start=(c == 0), stop=(c == 31)),
                        waits=ww,
                        inc=("t", 1) if c == 31 else None) or sdn_t[sl]
                tm = sc.now("t")
                xsl = self.x[:, mb * TOK + grp * 512: mb * TOK + (grp + 1) * 512]
                self.v(lambda e, P=P, xsl=xsl: e.scalar_tensor_tensor(
                    xsl, P, 0.5, xsl, ALU.mult, ALU.add), waits=[("t", tm)], inc=("v", 1))
                self.bank_set(bk)

    def fc1(self, idx):
        sc = self.sc
        Rb, Mub = self.Rb, self.Mub
        xlat = self.av(0, [8, TOK], BF16)
        fw = [self.av(88 + 4 * i, [NB, 128], BF16) for i in range(2)]
        tf = [self.av(112 + 4 * i, [1024], F32) for i in range(2)]
        pw = [("t", sc.now("t")), ("v", sc.now("v")), ("s", sc.now("s"))]
        f_t = [0, 0]
        for mb in range(8):
            sl = mb % 2
            self.g(lambda e, mb=mb, sl=sl: e.dma_start(
                out=fw[sl][:],
                in_=self.fc1T[:, mb * 128:(mb + 1) * 128].rearrange("(c p) n -> p c n", p=128)),
                waits=([("t", f_t[sl])] if mb >= 2 else pw), inc=("qw", 16))
            qn = sc.now("qw")
            if not self.w1s_done:
                self.w1s_from_slot(fw[sl], NB, self.w1s_f1, mb, 256, qn)
            for nh in range(2):
                bk = "P2a" if sl == 0 else "P2b"
                P = self.P2[sl]
                wb = self.bank_wait(bk) if nh == 0 else [("v", sc.now("v"))]
                for half512 in range(2):
                    off = nh * 1024 + half512 * 512
                    for c in range(NB):
                        f_t[sl] = self.t(lambda e, c=c, off=off, P=P, sl=sl, half512=half512: e.matmul(
                            P[:, half512 * 512:(half512 + 1) * 512],
                            fw[sl][:, c, :], self.x[:, c * TOK + off: c * TOK + off + 512],
                            start=(c == 0), stop=(c == NB - 1)),
                            waits=(wb + [("qw", qn)]) if (c == 0 and half512 == 0) else [],
                            inc=("t", 1) if c == NB - 1 else None) or f_t[sl]
                tm = sc.now("t")
                tsl = slice(nh * 1024, (nh + 1) * 1024)
                self.v(lambda e, mb=mb, P=P, tsl=tsl, sl=sl: e.scalar_tensor_tensor(
                    tf[sl][:], Mub[:, tsl], self.w1s_f1[:, mb:mb + 1], P[:], ALU.mult, ALU.subtract),
                    waits=[("t", tm)], inc=("v", 1))
                self.vd()
                self.v(lambda e, mb=mb, tsl=tsl, sl=sl: e.scalar_tensor_tensor(
                    xlat[:, mb, tsl], tf[sl][:], -1.0, Rb[:, tsl], ALU.mult, ALU.mult),
                    inc=("v", 1))
                if nh == 1:
                    self.bank_set(bk)
        self.xlat_v = sc.now("v")
        self.tap(f"xlat{idx}", lambda: xlat[:].rearrange("p a b -> p (a b)"), [128, 8 * TOK], BF16)

    def experts(self, idx):
        sc = self.sc
        xlat = self.av(0, [8, TOK], BF16)
        cwT = self.cwT
        wA = [self.av(36, [8, 1024], BF16), self.av(104, [8, 1024], BF16)]
        BeS = [self.av(52, [TOK], BF16), self.av(52, [TOK], BF16)]
        y = self.av(56, [8, TOK], BF16)
        h = self.av(88, [8, 1024], BF16)
        pw = [("t", sc.now("t")), ("v", sc.now("v")), ("s", sc.now("s"))]
        w1_t = 0
        w2_t = 0
        yv = 0
        w1q = w2q = 0
        self.be_war = 0

        def build_Be(eb):
            banks = ["P1a", "P1b", "P1c"]

            def bmm(n, extra):
                wb = self.bank_wait(banks[n % 3])
                self.t(lambda e, eb=eb, n=n: e.matmul(
                    self.P1[n % 3][:], self.eh[:, eb * 128:(eb + 1) * 128],
                    cwT[:, n * 512:(n + 1) * 512], start=True, stop=True),
                    waits=wb + extra, inc=("t", 1))
                return sc.now("t")

            def bcp(n):
                tm = tmv[n]
                self.v(lambda e, n=n: e.tensor_copy(
                    BeS[0][:, n * 512:(n + 1) * 512], self.P1[n % 3][:]),
                    waits=[("t", tm)], inc=("v", 1))
                self.bank_set(banks[n % 3])

            tmv = {}
            w0 = [("v", max(self.be_war, self.cwT_v))]
            tmv[0] = bmm(0, w0)
            tmv[1] = bmm(1, [])
            tmv[2] = bmm(2, [])
            bcp(0)
            bcp(1)
            bcp(2)
            tmv[3] = bmm(3, [])
            bcp(3)

        build_Be(0)
        for e_i in range(NE):
            Be = BeS[0]
            if e_i > 0:
                self.be_war = sc.now("v")
                build_Be(e_i)
            self.g(lambda e, e_i=e_i: e.dma_start(
                out=wA[0][:],
                in_=self.w1T[e_i * LAT:(e_i + 1) * LAT, :].rearrange("(c p) n -> p c n", p=128)),
                waits=([("t", w1_t)] if e_i > 0 else pw), inc=("qw", 16))
            w1q = sc.now("qw")
            self.g(lambda e, e_i=e_i: e.dma_start(
                out=wA[1][:],
                in_=self.w2T[e_i * INTER:(e_i + 1) * INTER, :].rearrange("(c p) n -> p c n", p=128)),
                waits=([("t", w2_t)] if e_i > 0 else pw), inc=("qw", 16))
            w2q = sc.now("qw")
            for half in range(2):
                hsl = slice(half * 1024, (half + 1) * 1024)
                hmark = [0] * 8
                for ib in range(8):
                    bk = "P2a" if ib % 2 == 0 else "P2b"
                    P = self.P2[ib % 2]
                    wb = self.bank_wait(bk)
                    for kc in range(8):
                        for n2 in range(2):
                            w1_t = self.t(lambda e, kc=kc, ib=ib, n2=n2, P=P, half=half: e.matmul(
                                P[:, n2 * 512:(n2 + 1) * 512],
                                wA[0][:, kc, ib * 128:(ib + 1) * 128],
                                xlat[:, kc, half * 1024 + n2 * 512: half * 1024 + (n2 + 1) * 512],
                                start=(kc == 0), stop=(kc == 7)),
                                waits=(wb + [("qw", w1q), ("v", self.xlat_v)]) if (kc == 0 and n2 == 0) else [],
                                inc=("t", 1) if (kc == 7 and n2 == 1) else None) or w1_t
                    tm = sc.now("t")
                    self.s(lambda e, ib=ib, P=P: e.activation(h[:, ib, :], P[:], AF.Relu),
                           waits=[("t", tm)], inc=("s", 1))
                    sv = sc.now("s")
                    self.v(lambda e, ib=ib: e.tensor_tensor(h[:, ib, :], h[:, ib, :], h[:, ib, :],
                                                            ALU.mult), waits=[("s", sv)])
                    self.vd()
                    self.v(lambda e, ib=ib, hsl=hsl, Be=Be: e.tensor_tensor(
                        h[:, ib, :], h[:, ib, :], Be[:, hsl], ALU.mult), inc=("v", 1))
                    hmark[ib] = sc.now("v")
                    self.bank_set(bk, "s")
                hv = sc.now("v")
                for mb in range(8):
                    bk = "P2a" if mb % 2 == 0 else "P2b"
                    P = self.P2[mb % 2]
                    wb = self.bank_wait(bk)
                    for kc in range(8):
                        for n2 in range(2):
                            ww = list(wb) + [("qw", w2q)] if (kc == 0 and n2 == 0) else []
                            if mb == 0 and n2 == 0:
                                ww = ww + [("v", hmark[kc])]
                            w2_t = self.t(lambda e, kc=kc, mb=mb, n2=n2, P=P: e.matmul(
                                P[:, n2 * 512:(n2 + 1) * 512],
                                wA[1][:, kc, mb * 128:(mb + 1) * 128],
                                h[:, kc, n2 * 512:(n2 + 1) * 512],
                                start=(kc == 0), stop=(kc == 7)),
                                waits=ww,
                                inc=("t", 1) if (kc == 7 and n2 == 1) else None) or w2_t
                    tm = sc.now("t")
                    if e_i == 0:
                        yv = self.v(lambda e, mb=mb, P=P, hsl=hsl: e.tensor_copy(
                            y[:, mb, hsl], P[:]), waits=[("t", tm)], inc=("v", 1))
                    else:
                        yv = self.v(lambda e, mb=mb, P=P, hsl=hsl: e.tensor_tensor(
                            y[:, mb, hsl], P[:], y[:, mb, hsl], ALU.add),
                            waits=[("t", tm)], inc=("v", 1))
                    self.bank_set(bk)
        self.y_done = sc.now("v")
        self.tap(f"ylat{idx}", lambda: y[:].rearrange("p a b -> p (a b)"), [128, 8 * TOK], BF16)

    def fc2(self, idx):
        import os as _os
        NMB = int(_os.environ.get("FC2_MB", NB))
        sc = self.sc
        y = self.av(56, [8, TOK], BF16)
        fw = [self.av(88 + 2 * i, [8, 128], BF16) for i in range(2)]
        pw = [("t", sc.now("t")), ("v", sc.now("v")), ("s", sc.now("s"))]
        f_t = [0, 0]
        for mb in range(NMB):
            sl = mb % 2
            self.g(lambda e, mb=mb, sl=sl: e.dma_start(
                out=fw[sl][:],
                in_=self.fc2T[:, mb * 128:(mb + 1) * 128].rearrange("(c p) n -> p c n", p=128)),
                waits=([("t", f_t[sl])] if mb >= 2 else pw), inc=("qw", 16))
            qn = sc.now("qw")
            for nh in range(2):
                bk = "P2a" if sl == 0 else "P2b"
                P = self.P2[sl]
                wb = self.bank_wait(bk) if nh == 0 else [("v", sc.now("v"))]
                for n2 in range(2):
                    for kc in range(8):
                        f_t[sl] = self.t(lambda e, kc=kc, nh=nh, n2=n2, P=P, sl=sl: e.matmul(
                            P[:, n2 * 512:(n2 + 1) * 512],
                            fw[sl][:, kc, :], y[:, kc, nh * 1024 + n2 * 512: nh * 1024 + (n2 + 1) * 512],
                            start=(kc == 0), stop=(kc == 7)),
                            waits=(wb + ([("qw", qn), ("v", self.y_done)] if nh == 0 else [])) if (kc == 0 and n2 == 0) else [],
                            inc=("t", 1) if kc == 7 else None) or f_t[sl]
                tm = sc.now("t")
                xsl = self.x[:, mb * TOK + nh * 1024: mb * TOK + (nh + 1) * 1024]
                self.v(lambda e, P=P, xsl=xsl: e.scalar_tensor_tensor(
                    xsl, P[:], 0.5, xsl, ALU.mult, ALU.add), waits=[("t", tm)], inc=("v", 1))
                if nh == 1:
                    self.bank_set(bk)

    def attn(self, stop=None):
        sc = self.sc
        self.stats()
        if stop == 'atstats':
            return
        cosA = self.av(112, [NB, HD], BF16)
        sinA = self.av(116, [NB, HD], BF16)
        self.q(lambda e: e.dma_start(out=cosA[:].rearrange("p a b -> p (a b)"), in_=self.cosT[:]),
               waits=[("v", sc.now("v")), ("t", sc.now("t")), ("s", sc.now("s"))], inc=("qio", 16))
        self.q(lambda e: e.dma_start(out=sinA[:].rearrange("p a b -> p (a b)"), in_=self.sinT[:]),
               inc=("qio", 16))
        tabq = sc.now("qio")
        wq = self.av(0, [NB, 512], BF16)
        wk = self.av(16, [NB, 128], BF16)
        wv = self.av(20, [NB, 128], BF16)
        fx0 = self.av(24, [768], F32)
        fx1 = self.av(27, [640], F32)
        fx2 = self.av(30, [640], F32)
        qtok = self.av(33, [NB, 512], BF16)
        OT = self.av(16, [4, TOK], BF16)
        qfT = self.av(49, [4, TOK], BF16)
        ktok = self.av(65, [NB, 128], BF16)
        kT = self.av(69, [TOK], BF16)
        vaug = self.av(73, [NB, 132], BF16)
        osb = self.av(78, [NB, 4, 129], BF16)
        sg = self.av(95, [NCORES, 132], F32)
        sfin = self.av(100, [132], F32)
        ssb = self.av(101, [132], BF16)
        sinb = self.av(102, [132], BF16)
        acc = [self.av(103, [132], F32), self.av(104, [132], F32)]
        amT = [self.av(105, [512], BF16), self.av(106, [512], BF16)]
        W1Sb = self.av(107, [768], BF16)
        ft = self.av(109, [4, 129], F32)
        den = self.av(104, [8], F32)
        woT_s = self.av(0, [4, TOK], BF16)
        OHEAD = [0, 256, 512, 768]

        prev_corr_v = 0
        for grp in range(NKV):
            pwA = [("t", sc.now("t")), ("v", sc.now("v")), ("s", sc.now("s")),
                   ("qio", sc.now("qio"))]
            wqw = self.g(lambda e, grp=grp: e.dma_start(
                out=wq[:], in_=self.wqT[:, grp * 512:(grp + 1) * 512].rearrange("(c p) n -> p c n", p=128)),
                waits=pwA, inc=("qw", 16))
            self.g(lambda e, grp=grp: e.dma_start(
                out=wk[:], in_=self.wkT[:, grp * 128:(grp + 1) * 128].rearrange("(c p) n -> p c n", p=128)),
                inc=("qw", 16))
            self.g(lambda e, grp=grp: e.dma_start(
                out=wv[:], in_=self.wvT[:, grp * 128:(grp + 1) * 128].rearrange("(c p) n -> p c n", p=128)),
                inc=("qw", 16))
            qn = sc.now("qw")
            Pw = self.P2[0]
            wb = self.bank_wait("P2a")
            for c in range(NB):
                self.t(lambda e, c=c, Pw=Pw: e.matmul(Pw[0:1, 0:512], self.ones_c[:], wq[:, c, :],
                                                      start=(c == 0), stop=(c == NB - 1)),
                       waits=(wb + [("qw", qn), ("v", prev_corr_v)]) if c == 0 else [])
            for c in range(NB):
                self.t(lambda e, c=c, Pw=Pw: e.matmul(Pw[0:1, 512:640], self.ones_c[:], wk[:, c, :],
                                                      start=(c == 0), stop=(c == NB - 1)))
            for c in range(NB):
                self.t(lambda e, c=c, Pw=Pw: e.matmul(Pw[0:1, 640:768], self.ones_c[:], wv[:, c, :],
                                                      start=(c == 0), stop=(c == NB - 1)),
                       inc=("t", 1) if c == NB - 1 else None)
            tm = sc.now("t")
            self.v(lambda e, Pw=Pw: e.tensor_copy(self.wsrow[0:1, 0:768], Pw[0:1, 0:768]),
                   waits=[("t", tm)], inc=("v", 1))
            vm = sc.now("v")
            self.bank_set("P2a")
            for (o0, n0) in ((0, 512), (512, 256)):
                wb = self.bank_wait("P1a")
                self.t(lambda e, o0=o0, n0=n0: e.matmul(
                    self.P1[0][:, 0:n0], self.ones_r[:], self.wsrow[0:1, o0:o0 + n0],
                    start=True, stop=True), waits=wb + ([("v", vm)] if o0 == 0 else []),
                    inc=("t", 1))
                tm = sc.now("t")
                self.v(lambda e, o0=o0, n0=n0: e.tensor_copy(W1Sb[:, o0:o0 + n0], self.P1[0][:, 0:n0]),
                       waits=[("t", tm)], inc=("v", 1))
                self.bank_set("P1a")
            w1sv = sc.now("v")
            self.g(lambda e: e.memset(vaug[:, :, 128:129], 1.0),
                   waits=[("v", sc.now("v")), ("t", sc.now("t"))], inc=("p", 1))
            pmem = sc.now("p")
            for tb in range(NB):
                bk = "P2a" if tb % 2 == 0 else "P2b"
                P = self.P2[tb % 2]
                wb = self.bank_wait(bk)
                for c in range(NB):
                    self.t(lambda e, c=c, tb=tb, P=P: e.matmul(
                        P[:, 0:512], self.x[:, c * TOK + tb * 128: c * TOK + (tb + 1) * 128],
                        wq[:, c, :], start=(c == 0), stop=(c == NB - 1)),
                        waits=(wb + [("qw", qn), ("qio", tabq)]) if c == 0 else [])
                for c in range(NB):
                    self.t(lambda e, c=c, tb=tb, P=P: e.matmul(
                        P[:, 512:640], self.x[:, c * TOK + tb * 128: c * TOK + (tb + 1) * 128],
                        wk[:, c, :], start=(c == 0), stop=(c == NB - 1)))
                for c in range(NB):
                    self.t(lambda e, c=c, tb=tb, P=P: e.matmul(
                        P[:, 640:768], self.x[:, c * TOK + tb * 128: c * TOK + (tb + 1) * 128],
                        wv[:, c, :], start=(c == 0), stop=(c == NB - 1)),
                        inc=("t", 1) if c == NB - 1 else None)
                tm = sc.now("t")
                self.v(lambda e, tb=tb: e.tensor_scalar(
                    fx0[:], W1Sb[:], self.mucol[:, tb:tb + 1], None, ALU.mult, ALU.bypass),
                    waits=[("t", tm), ("v", w1sv), ("p", pmem), ("qio", tabq)], inc=("v", 1))
                self.vd()
                self.v(lambda e, tb=tb, P=P: e.scalar_tensor_tensor(
                    fx0[:], P[:, 0:768], self.rcol[:, tb:tb + 1], fx0[:], ALU.mult, ALU.subtract),
                    inc=("v", 1))
                self.vd()
                self.bank_set(bk)
                self.v(lambda e, tb=tb: e.tensor_copy(vaug[:, tb, 0:128], fx0[:, 640:768]), inc=("v", 1))
                qk5 = fx0[:, 0:640].rearrange("p (h d) -> p h d", h=5)
                f15 = fx1[:].rearrange("p (h d) -> p h d", h=5)
                f25 = fx2[:].rearrange("p (h d) -> p h d", h=5)
                self.v(lambda e, tb=tb, qk5=qk5, f15=f15: e.tensor_tensor(
                    f15, qk5, cosA[:, tb, :].unsqueeze(1).broadcast_to([128, 5, 128]), ALU.mult))
                self.v(lambda e, tb=tb, qk5=qk5, f25=f25: e.tensor_tensor(
                    f25[:, :, 0:64], qk5[:, :, 64:128],
                    sinA[:, tb, 0:64].unsqueeze(1).broadcast_to([128, 5, 64]), ALU.mult))
                self.v(lambda e, tb=tb, qk5=qk5, f25=f25: e.tensor_tensor(
                    f25[:, :, 64:128], qk5[:, :, 0:64],
                    sinA[:, tb, 64:128].unsqueeze(1).broadcast_to([128, 5, 64]), ALU.mult))
                self.vd()
                self.v(lambda e: e.tensor_tensor(fx1[:], fx1[:], fx2[:], ALU.add), inc=("v", 1))
                self.vd()
                self.v(lambda e: e.tensor_scalar(fx0[:, 0:640], fx1[:], 0.0, None, ALU.min, ALU.bypass),
                       inc=("v", 1))
                vv = sc.now("v")
                self.s(lambda e: e.activation(fx2[:], fx0[:, 0:640], AF.Exp), waits=[("v", vv)],
                       inc=("s", 1))
                sv = sc.now("s")
                self.v(lambda e: e.tensor_scalar(fx0[:, 0:640], fx1[:], 0.0, None, ALU.max, ALU.bypass),
                       waits=[("s", sv)])
                self.vd()
                self.v(lambda e, tb=tb: e.tensor_tensor(qtok[:, tb, :], fx2[:, 0:512],
                                                        fx0[:, 0:512], ALU.add))
                self.v(lambda e, tb=tb: e.tensor_tensor(ktok[:, tb, :], fx2[:, 512:640],
                                                        fx0[:, 512:640], ALU.add), inc=("v", 1))
                self.vd()
            projv = sc.now("v")
            self.bank["wq_t"] = ("t", sc.now("t"))
            if grp == 1:
                self.tap("qtok", lambda: qtok[:].rearrange("p a b -> p (a b)"), [128, NB * 512], BF16)
                self.tap("ktok", lambda: ktok[:].rearrange("p a b -> p (a b)"), [128, NB * 128], BF16)
                self.tap("vaug", lambda: vaug[:].rearrange("p a b -> p (a b)"), [128, NB * 132], BF16)
            if stop == f"atproj{grp}":
                return
            def tr_batch(srcs, dst, n_in, waitv):
                wpt = self.bank_wait("PT")
                for j, sap in enumerate(srcs):
                    self.t(lambda e, j=j, sap=sap: e.transpose(
                        self.PT[:, j * 128:(j + 1) * 128], sap, self.idn[:]),
                        waits=(wpt + [("v", waitv)]) if j == 0 else [],
                        inc=("t", 1) if j == len(srcs) - 1 else None)
                tm = sc.now("t")
                self.v(lambda e, dst=dst, n=len(srcs): e.tensor_copy(
                    dst, self.PT[:, 0:n * 128]), waits=[("t", tm)], inc=("v", 1))
                self.bank_set("PT")
            for hh in range(4):
                for half in range(2):
                    tr_batch([qtok[:, half * 8 + j, hh * 128:(hh + 1) * 128] for j in range(8)],
                             qfT[:, hh, half * 1024:(half + 1) * 1024], 8, projv)
            for half in range(2):
                tr_batch([ktok[:, half * 8 + j, :] for j in range(8)],
                         kT[:, half * 1024:(half + 1) * 1024], 8, projv)
            trv = sc.now("v")
            psS = self.P1[2][:, 0:132]
            self.g(lambda e: e.memset(ssb[:], 0.0), waits=[("t", sc.now("t"))], inc=("p", 1))
            pz = sc.now("p")
            ssb_v = 0
            st_t = 0
            for cchunk in range(NB):
                bkA = "P1a" if cchunk % 2 == 0 else "P1b"
                PA = self.P1[cchunk % 2]
                wbA = self.bank_wait(bkA)
                for hh in range(4):
                    self.t(lambda e, cchunk=cchunk, hh=hh, PA=PA: e.matmul(
                        PA[:, hh * 128:(hh + 1) * 128], kT[:, cchunk * 128:(cchunk + 1) * 128],
                        qfT[:, hh, cchunk * 128:(cchunk + 1) * 128], start=True, stop=True),
                        waits=(wbA + [("v", trv)]) if hh == 0 else [],
                        inc=("t", 1) if hh == 3 else None)
                tmA = sc.now("t")
                am = amT[cchunk % 2]
                self.v(lambda e, PA=PA, am=am: e.tensor_tensor(
                    am[:].rearrange("p (h d) -> p h d", h=4),
                    PA[:].rearrange("p (h d) -> p h d", h=4),
                    self.msk[:].unsqueeze(1).broadcast_to([128, 4, 128]), ALU.mult),
                    waits=[("t", tmA)], inc=("v", 1))
                amv = sc.now("v")
                self.bank_set(bkA)
                bkO = "P2a" if cchunk % 2 == 0 else "P2b"
                PO = self.P2[cchunk % 2]
                wbO = self.bank_wait(bkO)
                for hh in range(4):
                    self.t(lambda e, cchunk=cchunk, hh=hh, PO=PO, am=am: e.matmul(
                        PO[:, OHEAD[hh]:OHEAD[hh] + 129], am[:, hh * 128:(hh + 1) * 128],
                        vaug[:, cchunk, 0:129], start=True, stop=False),
                        waits=(wbO + [("v", amv), ("v", ssb_v), ("p", pz)]) if hh == 0 else [])
                    self.t(lambda e, cchunk=cchunk, hh=hh, PO=PO: e.matmul(
                        PO[:, OHEAD[hh]:OHEAD[hh] + 129], qfT[:, hh, cchunk * 128:(cchunk + 1) * 128],
                        ssb[:, 0:129], start=False, stop=True),
                        inc=("t", 1) if hh == 3 else None)
                tmO = sc.now("t")
                st_t = self.t(lambda e, cchunk=cchunk: e.matmul(
                    psS[:, 0:129], ktok[:, cchunk, :], vaug[:, cchunk, 0:129],
                    start=(cchunk == 0), stop=True, skip_group_check=True),
                    waits=[("v", ssb_v)], inc=("t", 1))
                self.v(lambda e: e.tensor_copy(ssb[:, 0:129], psS[:, 0:129]),
                       waits=[("t", st_t)], inc=("v", 1))
                ssb_v = sc.now("v")
                self.v(lambda e, cchunk=cchunk, PO=PO: e.tensor_copy(
                    osb[:, cchunk, :, :],
                    PO[:, 0:1024].rearrange("p (a b) -> p a b", a=4)[:, :, 0:129]),
                    waits=[("t", tmO)], inc=("v", 1))
                self.bank_set(bkO)
            scan_t = sc.now("t")
            if grp == 1:
                self.tap("osb", lambda: osb[:].rearrange("p a b c -> p (a b c)"), [128, NB * 4 * 129], BF16)
            if stop == f"atscan{grp}":
                self.v(lambda e: e.tensor_copy(sfin[:, 0:129], psS[:, 0:129]),
                       waits=[("t", scan_t)], inc=("v", 1))
                return
            self.v(lambda e: e.tensor_copy(sfin[:, 0:129], psS[:, 0:129]),
                   waits=[("t", scan_t)], inc=("v", 1))
            self.v(lambda e: e.memset(sfin[:, 129:132], 0.0), inc=("v", 1))
            sfv = sc.now("v")
            self.g(lambda e, grp=grp: e.dma_start(out=self.cc_in[grp][:], in_=sfin[:]),
                   waits=[("v", sfv)], inc=("qw", 16))
            qcc = sc.now("qw")
            self.g(lambda e, grp=grp: e.collective_compute(
                "AllGather", ALU.bypass, replica_groups=[list(range(NCORES))],
                ins=[self.cc_in[grp][:]], outs=[self.cc_out[grp][:]]),
                waits=[("qw", qcc)], inc=("cc", 1))
            ccv = sc.now("cc")
            self.g(lambda e, grp=grp: e.dma_start(
                out=sg[:].rearrange("p a b -> p a b"),
                in_=self.cc_out[grp][:].rearrange("(r p) n -> p r n", p=128)),
                waits=[("cc", ccv)], inc=("qw", 16))
            qsg = sc.now("qw")
            self.v(lambda e: e.tensor_scalar(acc[0][:], sg[:, 0, :], self.flg[:, 0:1], None,
                                             ALU.mult, ALU.bypass), waits=[("qw", qsg)], inc=("v", 1))
            for r in range(1, NCORES):
                self.vd()
                self.v(lambda e, r=r: e.scalar_tensor_tensor(
                    acc[r % 2][:], sg[:, r, :], self.flg[:, r:r + 1], acc[(r - 1) % 2][:],
                    ALU.mult, ALU.add), inc=("v", 1))
            self.vd()
            self.v(lambda e: e.tensor_copy(sinb[:], acc[(NCORES - 1) % 2][:]), inc=("v", 1))
            sinv = sc.now("v")
            ofin = qtok
            for cchunk in range(NB):
                bkO = "P2a" if cchunk % 2 == 0 else "P2b"
                PO = self.P2[cchunk % 2]
                wbO = self.bank_wait(bkO)
                for hh in range(4):
                    self.t(lambda e, cchunk=cchunk, hh=hh, PO=PO: e.matmul(
                        PO[:, OHEAD[hh]:OHEAD[hh] + 129], qfT[:, hh, cchunk * 128:(cchunk + 1) * 128],
                        sinb[:, 0:129], start=True, stop=True),
                        waits=(wbO + [("v", sinv)]) if hh == 0 else [],
                        inc=("t", 1) if hh == 3 else None)
                tm = sc.now("t")
                self.v(lambda e, cchunk=cchunk, PO=PO: e.tensor_tensor(
                    ft[:],
                    PO[:, 0:1024].rearrange("p (a b) -> p a b", a=4)[:, :, 0:129],
                    osb[:, cchunk, :, :], ALU.add), waits=[("t", tm)], inc=("v", 1))
                self.vd()
                self.bank_set(bkO)
                self.v(lambda e: e.tensor_scalar(den[:, 0:4], ft[:, :, 128], EPS_AT, None,
                                                 ALU.add, ALU.bypass), inc=("v", 1))
                self.vd()
                self.v(lambda e: e.reciprocal(den[:, 4:8], den[:, 0:4]), inc=("v", 1))
                self.vd()
                for hh in range(4):
                    self.v(lambda e, cchunk=cchunk, hh=hh: e.tensor_scalar(
                        ofin[:, cchunk, hh * 128:(hh + 1) * 128], ft[:, hh, 0:128],
                        den[:, 4 + hh:5 + hh], None, ALU.mult, ALU.bypass),
                        inc=("v", 1) if hh == 3 else None)
                self.vd()
            corrv = sc.now("v")
            for hh in range(4):
                for half in range(2):
                    tr_batch([ofin[:, half * 8 + j, hh * 128:(hh + 1) * 128] for j in range(8)],
                             OT[:, hh, half * 1024:(half + 1) * 1024], 8, corrv)
            otv = sc.now("v")
            self.q(lambda e, grp=grp: e.dma_start(
                out=self.otd[grp * 512:(grp + 1) * 512, :].rearrange("(c p) n -> p c n", p=128),
                in_=OT[:]), waits=[("v", otv)], inc=("qio", 16))
            self.ot_spill_q = sc.now("qio")
            self.bank["wo_t"] = ("t", sc.now("t"))
            prev_corr_v = sc.now("v")
            if stop == f"atg{grp}":
                return
        otf = self.av(0, [NB, TOK], BF16)
        wos = [self.av(64 + 4 * i, [NB, 128], BF16) for i in range(2)]
        self.q(lambda e: e.dma_start(
            out=otf[:], in_=self.otd[:].rearrange("(c p) n -> p c n", p=128)),
            waits=[("qio", sc.now("qio")), ("t", sc.now("t"))], inc=("qio", 16))
        qotf = sc.now("qio")
        wo_t = [0, 0]
        for mb in range(NB):
            sl = mb % 2
            self.g(lambda e, mb=mb, sl=sl: e.dma_start(
                out=wos[sl][:],
                in_=self.woT[:, mb * 128:(mb + 1) * 128].rearrange("(c p) n -> p c n", p=128)),
                waits=([("t", wo_t[sl])] if mb >= 2 else
                       [("t", sc.now("t")), ("v", sc.now("v")), ("s", sc.now("s"))]),
                inc=("qw", 16))
            qn = sc.now("qw")
            for n in range(4):
                bk = "P1a" if sl == 0 else "P1b"
                P = self.P1[sl]
                wb = self.bank_wait(bk) if n == 0 else [("v", sc.now("v"))]
                for kc in range(NB):
                    wo_t[sl] = self.t(lambda e, mb=mb, n=n, kc=kc, P=P, sl=sl: e.matmul(
                        P[:], wos[sl][:, kc, :], otf[:, kc, n * 512:(n + 1) * 512],
                        start=(kc == 0), stop=(kc == NB - 1)),
                        waits=(wb + [("qw", qn), ("qio", qotf)]) if kc == 0 else [],
                        inc=("t", 1) if kc == NB - 1 else None) or wo_t[sl]
                tm = sc.now("t")
                xsl = self.x[:, mb * TOK + n * 512: mb * TOK + (n + 1) * 512]
                self.v(lambda e, P=P, xsl=xsl: e.tensor_tensor(xsl, P[:], xsl, ALU.add),
                       waits=[("t", tm)], inc=("v", 1))
                self.bank_set(bk)

    def store_out(self):
        sc = self.sc
        self.g(lambda e: e.dma_start(
            out=self.xout[:].rearrange("(c p) t -> p c t", p=128),
            in_=self.x[:].rearrange("p (c t) -> p c t", c=NB)),
            waits=self.all_now(), inc=("qw", 16))
        fin = sc.now("qw")
        self.g(lambda e: e.nop(), waits=[("qw", fin)])


def _host_tables(core):
    b, half = core // 2, core % 2
    inv = 1.0 / (10000.0 ** (np.arange(0, HD, 2, dtype=np.float64) / HD))
    t = np.arange(S_FULL := 4096, dtype=np.float64)
    fr = np.outer(t, inv)
    emb = np.concatenate([fr, fr], axis=-1)
    cos = np.cos(emb).astype(np.float32)
    sin = np.sin(emb).astype(np.float32)
    sgn = np.ones((1, HD), np.float32)
    sgn[0, :64] = -1.0
    sinS = sin * sgn
    sl = slice(half * TOK, (half + 1) * TOK)
    def fold(tab):
        return np.ascontiguousarray(
            tab[sl].reshape(NB, 128, HD).transpose(1, 0, 2).reshape(128, NB * HD)).astype(bfdt)
    return fold(cos), fold(sinS)


def kernel(L0_attn_x, L0_attn_Wq, L0_attn_Wk, L0_attn_Wv, L0_attn_Wo,
           L0_ffn_gate_w, L0_ffn_gate_bias, L0_ffn_fc1_w, L0_ffn_fc2_w,
           L0_ffn_w1, L0_ffn_w2, L0_ffn_shared_up_w, L0_ffn_shared_down_w):
    f32 = np.float32
    C = np.ascontiguousarray
    wqT = C(np.asarray(L0_attn_Wq, f32).T)
    wkT = C(np.asarray(L0_attn_Wk, f32).T)
    wvT = C(np.asarray(L0_attn_Wv, f32).T)
    woT = C(np.asarray(L0_attn_Wo, f32).T)
    gwT = C(np.asarray(L0_ffn_gate_w, f32).T)
    biasT = C(np.broadcast_to(np.asarray(L0_ffn_gate_bias, f32), (128, NE)))
    fc1T = C(np.asarray(L0_ffn_fc1_w, f32).T)
    fc2T = C(np.asarray(L0_ffn_fc2_w, f32).T)
    w1T = C(np.asarray(L0_ffn_w1, f32).transpose(0, 2, 1).reshape(NE * LAT, INTER))
    w2T = C(np.asarray(L0_ffn_w2, f32).transpose(0, 2, 1).reshape(NE * INTER, LAT))
    supT = C(np.asarray(L0_ffn_shared_up_w, f32).T)
    sdnT = C(np.asarray(L0_ffn_shared_down_w, f32).T)
    maskU = np.triu(np.ones((128, 128), np.float32)).astype(bfdt)
    ident = np.eye(128, dtype=np.float32).astype(bfdt)
    ehot = np.zeros((NE, NE * 128), np.float32)
    for e in range(NE):
        ehot[e, e * 128:(e + 1) * 128] = 1.0
    ehot = ehot.astype(bfdt)
    x = np.asarray(L0_attn_x, f32)

    import os
    mk = MK(stop_after=os.environ.get("MK_STOP") or None)
    nc = mk.build()
    in_maps = _in_maps(x, wqT, wkT, wvT, woT, gwT, biasT, fc1T, fc2T, w1T, w2T,
                       supT, sdnT, maskU, ident, ehot)
    res = run_bass_kernel_spmd(nc, in_maps, core_ids=list(range(NCORES)))
    out = np.empty((4, 4096, DM), f32)
    for c in range(NCORES):
        b, half = c // 2, c % 2
        out[b, half * TOK:(half + 1) * TOK, :] = np.asarray(res.results[c]["xout"]).T
    return out


def _in_maps(x, wqT, wkT, wvT, woT, gwT, biasT, fc1T, fc2T, w1T, w2T,
             supT, sdnT, maskU, ident, ehot):
    f32 = np.float32
    C = np.ascontiguousarray
    in_maps = []
    for c in range(NCORES):
        b, half = c // 2, c % 2
        xin = C(x[b, half * TOK:(half + 1) * TOK, :].T)
        cosA, sinA = _host_tables(c)
        flags = np.zeros((128, NCORES), f32)
        if half == 1:
            flags[:, c - 1] = 1.0
        in_maps.append(dict(
            xin=xin, wqT=wqT, wkT=wkT, wvT=wvT, woT=woT, gwT=gwT, biasT=biasT,
            fc1T=fc1T, fc2T=fc2T, w1T=w1T, w2T=w2T, supT=supT, sdnT=sdnT,
            cosT=cosA, sinT=sinA, maskT=maskU, identT=ident, ehotT=ehot,
            flagsT=flags))
    return in_maps


def _host_prep(inputs):
    f32 = np.float32
    C = np.ascontiguousarray
    wqT = C(np.asarray(inputs["L0_attn_Wq"], f32).T)
    wkT = C(np.asarray(inputs["L0_attn_Wk"], f32).T)
    wvT = C(np.asarray(inputs["L0_attn_Wv"], f32).T)
    woT = C(np.asarray(inputs["L0_attn_Wo"], f32).T)
    gwT = C(np.asarray(inputs["L0_ffn_gate_w"], f32).T)
    biasT = C(np.broadcast_to(np.asarray(inputs["L0_ffn_gate_bias"], f32), (128, NE)))
    fc1T = C(np.asarray(inputs["L0_ffn_fc1_w"], f32).T)
    fc2T = C(np.asarray(inputs["L0_ffn_fc2_w"], f32).T)
    w1T = C(np.asarray(inputs["L0_ffn_w1"], f32).transpose(0, 2, 1).reshape(NE * LAT, INTER))
    w2T = C(np.asarray(inputs["L0_ffn_w2"], f32).transpose(0, 2, 1).reshape(NE * INTER, LAT))
    supT = C(np.asarray(inputs["L0_ffn_shared_up_w"], f32).T)
    sdnT = C(np.asarray(inputs["L0_ffn_shared_down_w"], f32).T)
    maskU = np.triu(np.ones((128, 128), np.float32)).astype(bfdt)
    ident = np.eye(128, dtype=np.float32).astype(bfdt)
    ehot = np.zeros((NE, NE * 128), np.float32)
    for e in range(NE):
        ehot[e, e * 128:(e + 1) * 128] = 1.0
    ehot = ehot.astype(bfdt)
    x = np.asarray(inputs["L0_attn_x"], f32)
    return _in_maps(x, wqT, wkT, wvT, woT, gwT, biasT, fc1T, fc2T, w1T, w2T,
                    supT, sdnT, maskU, ident, ehot)


# version 08
# speedup vs baseline: 1.1081x; 1.0186x over previous
import numpy as np
import ml_dtypes
from contextlib import ExitStack

import concourse.bass as bass
import concourse.mybir as mybir
from concourse.bass_utils import run_bass_kernel_spmd

F32 = mybir.dt.float32
BF16 = mybir.dt.bfloat16
AF = mybir.ActivationFunctionType
ALU = mybir.AluOpType
AX = mybir.AxisListType

DM = 2048
TOK = 2048
NB = 16
LAT = 1024
INTER = 1024
NE = 32
SH = 4096
NKV = 4
HD = 128
EPS_LN = 1e-5
EPS_AT = 1e-6
NCORES = 8
KB = 1024
ARENA_KB = 120
BIG = 1e30

bfdt = ml_dtypes.bfloat16


class Sched:
    ENGS = ("tensor", "vector", "scalar", "gpsimd", "sync")

    def __init__(self):
        self.ops = []
        self.cnt = {}

    def op(self, eng, fn, waits=(), inc=None):
        ws = [(s, v) for (s, v) in waits if v and v > 0]
        self.ops.append((eng, fn, ws, inc))
        if inc is not None:
            self.cnt[inc[0]] = self.cnt.get(inc[0], 0) + inc[1]
            return self.cnt[inc[0]]
        return None

    def now(self, sem):
        return self.cnt.get(sem, 0)

    def emit(self, block, sems):
        for name in self.ENGS:
            def emitter(eng, eng_name=name):
                last = {}
                for (e, fn, ws, inc) in self.ops:
                    if e != eng_name:
                        continue
                    for (s, v) in ws:
                        if last.get(s, -1) >= v:
                            continue
                        eng.wait_ge(sems[s], v)
                        last[s] = v
                    instr = fn(eng)
                    if inc is not None:
                        instr.then_inc(sems[inc[0]], inc[1])
            getattr(block, name)(emitter)


class MK:
    def __init__(self, dbg=(), stop_after=None):
        self.nc = bass.Bass("TRN2", target_bir_lowering=False)
        self.sc = Sched()
        self.dbg_names = set(dbg)
        self.dbg_t = {}
        self.stop_after = stop_after
        self._dram()

    def _dram(self):
        nc = self.nc

        def din(n, s):
            return nc.dram_tensor(n, s, F32, kind="ExternalInput").ap()

        def dbin(n, s):
            return nc.dram_tensor(n, s, BF16, kind="ExternalInput").ap()

        self.xin = din("xin", [DM, TOK])
        self.wqT = din("wqT", [DM, DM])
        self.wkT = din("wkT", [DM, NKV * HD])
        self.wvT = din("wvT", [DM, NKV * HD])
        self.woT = din("woT", [DM, DM])
        self.gwT = din("gwT", [DM, NE])
        self.biasT = din("biasT", [128, NE])
        self.fc1T = din("fc1T", [DM, LAT])
        self.fc2T = din("fc2T", [LAT, DM])
        self.w1T = din("w1T", [NE * LAT, INTER])
        self.w2T = din("w2T", [NE * INTER, LAT])
        self.supT = din("supT", [DM, SH])
        self.sdnT = din("sdnT", [SH, DM])
        self.cosT = dbin("cosT", [128, NB * HD])
        self.sinT = dbin("sinT", [128, NB * HD])
        self.maskT = dbin("maskT", [128, 128])
        self.identT = dbin("identT", [128, 128])
        self.ehotT = dbin("ehotT", [NE, NE * 128])
        self.flagsT = din("flagsT", [128, NCORES])
        self.xout = nc.dram_tensor("xout", [DM, TOK], F32, kind="ExternalOutput").ap()
        self.cc_in = [nc.dram_tensor(f"ccin{g}", [128, 132], F32).ap() for g in range(NKV)]
        self.otd = nc.dram_tensor("otd", [DM, TOK], BF16).ap()
        self.cc_out = [nc.dram_tensor(f"ccout{g}", [NCORES * 128, 132], F32).ap() for g in range(NKV)]

    def t(self, fn, waits=(), inc=None):
        return self.sc.op("tensor", fn, waits, inc)

    def v(self, fn, waits=(), inc=None):
        return self.sc.op("vector", fn, waits, inc)

    def s(self, fn, waits=(), inc=None):
        return self.sc.op("scalar", fn, waits, inc)

    def g(self, fn, waits=(), inc=None):
        return self.sc.op("gpsimd", fn, waits, inc)

    def q(self, fn, waits=(), inc=None):
        return self.sc.op("sync", fn, waits, inc)

    def vd(self):
        self.sc.op("vector", lambda e: e.drain())

    def sd(self):
        self.sc.op("scalar", lambda e: e.drain())

    def av(self, off_kb, shape, dtype=BF16, part0=None):
        n = int(np.prod(shape))
        it = {BF16: 2, F32: 4}[dtype]
        base = self.arena if part0 is None else self.arena[0:part0]
        a = base[:, off_kb * KB: off_kb * KB + n * it].bitcast(dtype)
        if len(shape) == 1:
            return a
        kw = {f"d{i}": shape[i] for i in range(len(shape))}
        pat = "p (" + " ".join(f"d{i}" for i in range(len(shape))) + ") -> p " + \
              " ".join(f"d{i}" for i in range(len(shape)))
        return a.rearrange(pat, **kw)

    def all_now(self):
        sc = self.sc
        return [(s, sc.now(s)) for s in ("t", "v", "s", "p", "qw", "qio", "cc")]

    def tap(self, name, ap_fn, shape, dtype=F32):
        if name not in self.dbg_names:
            return
        ap = self.nc.dram_tensor("dbg_" + name, shape, dtype, kind="ExternalOutput").ap()
        self.q(lambda e, ap=ap, f=ap_fn: e.dma_start(out=ap[:], in_=f()),
               waits=self.all_now(), inc=("qio", 16))
        self.dbg_t[name] = ap

    def bank_wait(self, name):
        return [self.bank.get(name, ("v", 0))]

    def bank_set(self, name, sem="v"):
        self.bank[name] = (sem, self.sc.now(sem))

    def build(self):
        nc, sc = self.nc, self.sc
        with ExitStack() as st:
            E = st.enter_context
            self.x = E(nc.sbuf_tensor("x", [128, NB * TOK], BF16))
            self.msk = E(nc.sbuf_tensor("msk", [128, 128], BF16))
            self.idn = E(nc.sbuf_tensor("idn", [128, 128], BF16))
            self.eh = E(nc.sbuf_tensor("eh", [NE, NE * 128], BF16))
            self.bias = E(nc.sbuf_tensor("bias", [128, NE], F32))
            self.flg = E(nc.sbuf_tensor("flg", [128, NCORES], F32))
            self.ones_r = E(nc.sbuf_tensor("ones_r", [1, 128], BF16))
            self.ones_c = E(nc.sbuf_tensor("ones_c", [128, 1], BF16))
            self.rrow = E(nc.sbuf_tensor("rrow", [1, TOK], BF16))
            self.murow = E(nc.sbuf_tensor("murow", [1, TOK], BF16))
            self.r2row = E(nc.sbuf_tensor("r2row", [1, TOK], BF16))
            self.rcol = E(nc.sbuf_tensor("rcol", [128, NB], F32))
            self.mucol = E(nc.sbuf_tensor("mucol", [128, NB], F32))
            self.w1s_g = E(nc.sbuf_tensor("w1s_g", [32, 1], F32))
            self.w1s_f1 = E(nc.sbuf_tensor("w1s_f1", [128, 8], F32))
            self.w1s_sup = E(nc.sbuf_tensor("w1s_sup", [128, 32], F32))
            self.wsrow = E(nc.sbuf_tensor("wsrow", [1, 1024], BF16))
            self.cst = E(nc.sbuf_tensor("cst", [128, 1], F32))
            self.arena = E(nc.sbuf_tensor("arena", [128, ARENA_KB * KB], mybir.dt.uint8))
            self.P2 = [E(nc.psum_tensor(f"P2{i}", [128, 1024], F32)) for i in range(2)]
            self.P1 = [E(nc.psum_tensor(f"P1{i}", [128, 512], F32)) for i in range(3)]
            self.PT = E(nc.psum_tensor("PT", [128, 1024], BF16))
            sems = {}
            for sname in ("t", "v", "s", "p", "qw", "qio", "cc"):
                sems[sname] = E(nc.semaphore("sem_" + sname))
            block = E(nc.Block())

            self.bank = {}
            self.w1s_done = False
            self.program()
            sc.emit(block, sems)
        return self.nc

    def program(self):
        st = self.stop_after
        self.init_load()
        if st is not None and st.startswith("ONLYAT"):
            self.attn(stop=st[6:] or None)
            self.store_out()
            return
        if st != "init":
            self.ffn(0, stop=st)
            if st is None or not st.startswith("f0"):
                self.attn(stop=st)
                if st is None or not st.startswith("at"):
                    self.ffn(1, stop=st)
        self.store_out()

    def init_load(self):
        for dst, src in ((self.msk, self.maskT), (self.idn, self.identT),
                         (self.eh, self.ehotT), (self.bias, self.biasT),
                         (self.flg, self.flagsT)):
            self.q(lambda e, d=dst, s=src: e.dma_start(out=d[:], in_=s[:]), inc=("qio", 16))
        self.g(lambda e: e.dma_start(
            out=self.x[:].rearrange("p (c t) -> p c t", c=NB),
            in_=self.xin[:].rearrange("(c p) t -> p c t", p=128)), inc=("qw", 16))
        self.g(lambda e: e.memset(self.cst[:], EPS_LN))
        self.g(lambda e: e.memset(self.ones_r[:], 1.0))
        self.g(lambda e: e.memset(self.ones_c[:], 1.0), inc=("p", 1))

    def stats(self):
        sc = self.sc
        xsq = [self.av(72 + 2 * i, [1024], BF16) for i in range(2)]
        row = [self.av(56 + 4 * i, [1024], F32)[0:1, :] for i in range(4)]
        start_w = self.all_now()
        P2 = self.P2
        xsq_t = [0, 0]
        for half in range(2):
            sum_row = P2[0][0:1, :]
            sq_row = P2[1][0:1, :]
            w_ps = self.bank_wait("P2a") + self.bank_wait("P2b")
            for c in range(NB):
                xb = self.x[:, c * TOK + half * 1024: c * TOK + (half + 1) * 1024]
                sl = xsq[c % 2]
                wv = ([("t", xsq_t[c % 2])] if (half * NB + c) >= 2 else []) + \
                     (start_w if (half == 0 and c < 2) else [])
                self.v(lambda e, xb=xb, sl=sl: e.tensor_tensor(sl[:], xb, xb, ALU.mult),
                       waits=wv, inc=("v", 1))
                vm = sc.now("v")
                for n in range(2):
                    self.t(lambda e, xb=xb, n=n, c=c, sum_row=sum_row: e.matmul(
                        sum_row[:, n * 512:(n + 1) * 512], self.ones_c[:],
                        xb[:, n * 512:(n + 1) * 512], start=(c == 0), stop=(c == NB - 1)),
                        waits=(w_ps + start_w) if (c == 0 and n == 0) else [])
                for n in range(2):
                    xsq_t[c % 2] = self.t(lambda e, sl=sl, n=n, c=c, sq_row=sq_row: e.matmul(
                        sq_row[:, n * 512:(n + 1) * 512], self.ones_c[:],
                        sl[:, n * 512:(n + 1) * 512], start=(c == 0), stop=(c == NB - 1)),
                        waits=[("v", vm)] if n == 0 else [], inc=("t", 1))
            tsum = sc.now("t")
            hs = slice(half * 1024, (half + 1) * 1024)
            self.v(lambda e, sum_row=sum_row: e.tensor_scalar(
                row[0][:], sum_row, 1.0 / DM, None, ALU.mult, ALU.bypass),
                waits=[("t", tsum)], inc=("v", 1))
            self.v(lambda e, sq_row=sq_row: e.tensor_scalar(
                row[1][:], sq_row, 1.0 / DM, None, ALU.mult, ALU.bypass), inc=("v", 1))
            self.vd()
            self.v(lambda e: e.tensor_tensor(row[2][:], row[0][:], row[0][:], ALU.mult), inc=("v", 1))
            self.vd()
            self.v(lambda e: e.tensor_tensor(row[3][:], row[1][:], row[2][:], ALU.subtract), inc=("v", 1))
            self.vd()
            vv = sc.now("v")
            self.s(lambda e: e.activation(row[1][:], row[3][:], AF.Ln, bias=self.cst[0:1, 0:1]),
                   waits=[("v", vv)], inc=("s", 1))
            self.sd()
            self.s(lambda e: e.activation(row[2][:], row[1][:], AF.Exp, scale=-0.5), inc=("s", 1))
            sv = sc.now("s")
            self.v(lambda e, hs=hs: e.tensor_copy(self.rrow[0:1, hs], row[2][:]),
                   waits=[("s", sv)], inc=("v", 1))
            self.v(lambda e, hs=hs: e.tensor_copy(self.murow[0:1, hs], row[0][:]), inc=("v", 1))
            self.v(lambda e: e.tensor_tensor(row[3][:], row[2][:], row[2][:], ALU.mult), inc=("v", 1))
            self.vd()
            self.v(lambda e, hs=hs: e.tensor_copy(self.r2row[0:1, hs], row[3][:]), inc=("v", 1))
            self.bank_set("P2a")
            self.bank_set("P2b")
        vrows = sc.now("v")
        P = self.P1[2]
        for tb in range(NB):
            wb = self.bank_wait("P1c") + ([("v", vrows)] if tb == 0 else [])
            self.t(lambda e, tb=tb: e.matmul(
                P[:, 0:1],
                self.rrow[0:1, tb * 128:(tb + 1) * 128], self.ones_r[0:1, 0:1],
                start=True, stop=True), waits=wb)
            self.t(lambda e, tb=tb: e.matmul(
                P[:, 4:5],
                self.murow[0:1, tb * 128:(tb + 1) * 128], self.ones_r[0:1, 0:1],
                start=True, stop=True), inc=("t", 1))
            tm = sc.now("t")
            self.v(lambda e, tb=tb: e.tensor_copy(
                self.rcol[:, tb:tb + 1], P[:, 0:1]), waits=[("t", tm)])
            self.v(lambda e, tb=tb: e.tensor_copy(
                self.mucol[:, tb:tb + 1], P[:, 4:5]), inc=("v", 1))
            self.bank_set("P1c")
        self.stats_v = sc.now("v")
        self.tap("rrow", lambda: self.rrow[:], [1, TOK], BF16)
        self.tap("murow", lambda: self.murow[:], [1, TOK], BF16)
        self.tap("rcol", lambda: self.rcol[:], [128, NB], F32)

    def bcast_rows(self, rows_offs):
        sc = self.sc
        P = self.P1[2]
        outs = []
        for (srow, off) in rows_offs:
            dst = self.av(off, [TOK], BF16)
            outs.append(dst)
            for n in range(4):
                wb = self.bank_wait("P1c")
                self.t(lambda e, srow=srow, n=n: e.matmul(
                    P[:], self.ones_r[:], srow[0:1, n * 512:(n + 1) * 512],
                    start=True, stop=True),
                    waits=wb + [("v", self.stats_v)], inc=("t", 1))
                tm = sc.now("t")
                self.v(lambda e, dst=dst, n=n: e.tensor_copy(dst[:, n * 512:(n + 1) * 512], P[:]),
                       waits=[("t", tm)], inc=("v", 1))
                self.bank_set("P1c")
        return outs

    def w1s_from_slot(self, wslot, nchunk, dst_col, mb, row_off, qwait):
        sc = self.sc
        P = self.P1[2]
        wb = self.bank_wait("P1c")
        for c in range(nchunk):
            self.t(lambda e, c=c: e.matmul(
                P[0:1, 0:128], self.ones_c[:], wslot[:, c, :],
                start=(c == 0), stop=(c == nchunk - 1)),
                waits=(wb + [("qw", qwait)]) if c == 0 else [],
                inc=("t", 1) if c == nchunk - 1 else None)
        tm = sc.now("t")
        ro = self.wsrow[0:1, row_off:row_off + 128]
        self.v(lambda e, ro=ro: e.tensor_copy(ro, P[0:1, 0:128]), waits=[("t", tm)], inc=("v", 1))
        vm = sc.now("v")
        self.bank_set("P1c")
        wb2 = self.bank_wait("P1c")
        self.t(lambda e, ro=ro: e.matmul(P[:, 256:257], ro, self.ones_r[0:1, 0:1],
                                         start=True, stop=True),
               waits=[("v", vm)] + wb2, inc=("t", 1))
        tm2 = sc.now("t")
        self.v(lambda e, mb=mb: e.tensor_copy(dst_col[:, mb:mb + 1], P[:, 256:257]),
               waits=[("t", tm2)], inc=("v", 1))
        self.bank_set("P1c")

    def ffn(self, idx, stop=None):
        pre = f"f{idx}"
        self.stats()
        if stop == pre + "stats":
            return
        Rb, Mub, R2b = self.bcast_rows([(self.rrow, 36), (self.murow, 40), (self.r2row, 44)])
        self.Rb, self.Mub, self.R2b = Rb, Mub, R2b
        if stop == pre + "bcast":
            return
        self.gate_and_route(idx)
        if stop == pre + "gate":
            return
        self.fc1(idx)
        if stop == pre + "fc1":
            return
        self.shared(idx)
        if stop == pre + "shared":
            return
        self.experts(idx)
        if stop == pre + "exp":
            return
        self.fc2(idx)
        self.w1s_done = True

    def gate_and_route(self, idx):
        sc = self.sc
        Rb, Mub = self.Rb, self.Mub
        gw = self.av(48, [NB, NE], BF16)
        lg = self.av(50, [NB, NE], F32)
        scr = self.av(52, [NB, NE], F32)
        sbm = self.av(54, [NB, NE], F32)
        cwa = self.av(56, [NB, NE], F32)
        sm0 = self.av(58, [NB, 4], F32)
        sm1 = self.av(59, [NB, 4], F32)
        sm2 = self.av(60, [NB, NE], F32)
        tg32 = self.av(62, [512], F32, part0=32)
        lgT = self.av(64, [TOK], BF16, part0=32)
        self.g(lambda e: e.dma_start(
            out=gw[:], in_=self.gwT[:].rearrange("(c p) n -> p c n", p=128)), inc=("qw", 16))
        qn = sc.now("qw")
        if not self.w1s_done:
            Pg = self.P1[2]
            wb = self.bank_wait("P1c")
            for c in range(NB):
                self.t(lambda e, c=c, Pg=Pg: e.matmul(
                    Pg[0:1, 0:32], self.ones_c[:], gw[:, c, :],
                    start=(c == 0), stop=(c == NB - 1)),
                    waits=(wb + [("qw", qn)]) if c == 0 else [],
                    inc=("t", 1) if c == NB - 1 else None)
            tm = sc.now("t")
            self.v(lambda e, Pg=Pg: e.tensor_copy(self.wsrow[0:1, 0:32], Pg[0:1, 0:32]),
                   waits=[("t", tm)], inc=("v", 1))
            vm = sc.now("v")
            self.bank_set("P1c")
            wb = self.bank_wait("P1c")
            self.t(lambda e, Pg=Pg: e.matmul(Pg[0:32, 256:257], self.wsrow[0:1, 0:32],
                                             self.ones_r[0:1, 0:1], start=True, stop=True),
                   waits=[("v", vm)] + wb, inc=("t", 1))
            tm = sc.now("t")
            self.v(lambda e, Pg=Pg: e.tensor_copy(self.w1s_g[:], Pg[0:32, 256:257]),
                   waits=[("t", tm)], inc=("v", 1))
            self.bank_set("P1c")
        for n in range(4):
            bk = "P1a" if n % 2 == 0 else "P1b"
            P = self.P1[n % 2]
            wb = self.bank_wait(bk)
            for c in range(NB):
                self.t(lambda e, n=n, c=c, P=P: e.matmul(
                    P[0:32, :], gw[:, c, :],
                    self.x[:, c * TOK + n * 512: c * TOK + (n + 1) * 512],
                    start=(c == 0), stop=(c == NB - 1)),
                    waits=(wb + [("qw", qn)]) if c == 0 else [],
                    inc=("t", 1) if c == NB - 1 else None)
            tm = sc.now("t")
            nsl = slice(n * 512, (n + 1) * 512)
            self.v(lambda e, P=P, nsl=nsl: e.scalar_tensor_tensor(
                tg32[:], Mub[0:32, nsl], self.w1s_g[:], P[0:32, :], ALU.mult, ALU.subtract),
                waits=[("t", tm)], inc=("v", 1))
            self.vd()
            self.v(lambda e, nsl=nsl: e.scalar_tensor_tensor(
                lgT[:, nsl], tg32[:], -1.0, Rb[0:32, nsl], ALU.mult, ALU.mult), inc=("v", 1))
            self.bank_set(bk)
        vv = sc.now("v")
        for half in range(2):
            wpt = self.bank_wait("PT")
            for j in range(8):
                tb = half * 8 + j
                self.t(lambda e, tb=tb, j=j: e.transpose(
                    self.PT[:, j * 128: j * 128 + 32], lgT[:, tb * 128:(tb + 1) * 128],
                    self.idn[0:32, 0:32]),
                    waits=(wpt + [("v", vv)]) if j == 0 else [],
                    inc=("t", 1) if j == 7 else None)
            tm = sc.now("t")
            self.v(lambda e, half=half: e.tensor_copy(
                lg[:, half * 8:(half + 1) * 8, :],
                self.PT[:].rearrange("p (a b) -> p a b", a=8)[:, :, 0:32]),
                waits=[("t", tm)], inc=("v", 1))
            self.bank_set("PT")
        vv = sc.now("v")
        self.s(lambda e: e.activation(scr[:], lg[:], AF.Sigmoid), waits=[("v", vv)], inc=("s", 1))
        sv = sc.now("s")
        V, VD = self.v, self.vd
        r3 = lambda a: a[:].rearrange("p a b -> p a b")
        g4s = sbm[:].rearrange("p a (g k) -> p a g k", g=4)
        M64 = self.av(58, [64], F32)
        aux = self.av(59, [96], F32)
        M64b, m16, m16b = aux[:, 0:64], aux[:, 64:80], aux[:, 80:96]
        EXP = lg[:].rearrange("p a b -> p (a b)")
        sbf = sbm[:].rearrange("p a b -> p (a b)")
        scf = scr[:].rearrange("p a b -> p (a b)")
        cwf = cwa[:].rearrange("p a b -> p (a b)")
        smf = sm2[:].rearrange("p a b -> p (a b)")
        V(lambda e: e.tensor_tensor(sbm[:], scr[:],
                                    self.bias[:].unsqueeze(1).broadcast_to([128, NB, NE]), ALU.add),
          waits=[("s", sv)])
        VD()
        V(lambda e: e.tensor_reduce(M64.rearrange("p (a g) -> p a g", a=NB), g4s, axis=AX.X, op=ALU.max))
        VD()
        for hh in range(2):
            V(lambda e, hh=hh: e.tensor_copy(
                EXP.rearrange("p (a k) -> p a k", a=64)[:, hh * 32:(hh + 1) * 32, :],
                M64[:, hh * 32:(hh + 1) * 32].unsqueeze(2).broadcast_to([128, 32, 8])))
        VD()
        V(lambda e: e.tensor_tensor(smf, sbf, EXP, ALU.is_ge))
        VD()
        V(lambda e: e.tensor_scalar(smf, smf, -BIG, None, ALU.mult, ALU.bypass))
        VD()
        V(lambda e: e.tensor_tensor(smf, smf, sbf, ALU.add))
        VD()
        V(lambda e: e.tensor_reduce(M64b.rearrange("p (a g) -> p a g", a=NB),
                                    sm2[:].rearrange("p a (g k) -> p a g k", g=4), axis=AX.X, op=ALU.max))
        VD()
        V(lambda e: e.tensor_tensor(M64, M64, M64b, ALU.add))
        VD()
        V(lambda e: e.tensor_reduce(m16.rearrange("p (a g) -> p a g", a=NB, g=1).squeeze(2),
                                    M64.rearrange("p (a g) -> p a g", a=NB), axis=AX.X, op=ALU.max))
        VD()
        V(lambda e: e.tensor_copy(M64b.rearrange("p (a g) -> p a g", a=NB),
                                  m16.unsqueeze(2).broadcast_to([128, 16, 4])))
        VD()
        V(lambda e: e.tensor_tensor(smf[:, 0:64], M64, M64b, ALU.is_ge))
        VD()
        V(lambda e: e.tensor_scalar(smf[:, 0:64], smf[:, 0:64], -BIG, None, ALU.mult, ALU.bypass))
        VD()
        V(lambda e: e.tensor_tensor(smf[:, 0:64], smf[:, 0:64], M64, ALU.add))
        VD()
        V(lambda e: e.tensor_reduce(m16b.rearrange("p (a g) -> p a g", a=NB, g=1).squeeze(2),
                                    smf[:, 0:64].rearrange("p (a g) -> p a g", a=NB), axis=AX.X, op=ALU.max))
        VD()
        V(lambda e: e.tensor_copy(M64b.rearrange("p (a g) -> p a g", a=NB),
                                  m16b.unsqueeze(2).broadcast_to([128, 16, 4])))
        VD()
        V(lambda e: e.tensor_tensor(smf[:, 0:64], M64, M64b, ALU.is_ge))
        VD()
        V(lambda e: e.tensor_scalar(smf[:, 0:64], smf[:, 0:64], 1.0, BIG, ALU.subtract, ALU.mult))
        VD()
        for hh in range(2):
            V(lambda e, hh=hh: e.tensor_copy(
                EXP.rearrange("p (a k) -> p a k", a=64)[:, hh * 32:(hh + 1) * 32, :],
                smf[:, hh * 32:(hh + 1) * 32].unsqueeze(2).broadcast_to([128, 32, 8])))
        VD()
        V(lambda e: e.tensor_tensor(sbf, sbf, EXP, ALU.add))
        VD()
        V(lambda e: e.memset(cwa[:], 0.0))
        VD()
        for it in range(8):
            V(lambda e: e.tensor_reduce(m16.rearrange("p (a g) -> p a g", a=NB, g=1).squeeze(2),
                                        r3(sbm), axis=AX.X, op=ALU.max))
            VD()
            for hh in range(2):
                V(lambda e, hh=hh: e.tensor_copy(
                    EXP.rearrange("p (a k) -> p a k", a=NB)[:, hh * 8:(hh + 1) * 8, :],
                    m16[:, hh * 8:(hh + 1) * 8].unsqueeze(2).broadcast_to([128, 8, 32])))
            VD()
            V(lambda e: e.tensor_tensor(smf, sbf, EXP, ALU.is_ge))
            VD()
            V(lambda e: e.scalar_tensor_tensor(EXP, smf, -BIG, sbf, ALU.mult, ALU.add))
            VD()
            V(lambda e: e.tensor_tensor(smf, smf, scf, ALU.mult))
            VD()
            V(lambda e: e.tensor_tensor(cwf, cwf, smf, ALU.add))
            VD()
            V(lambda e: e.tensor_copy(sbf, EXP))
            VD()
        V(lambda e: e.tensor_reduce(m16.rearrange("p (a g) -> p a g", a=NB, g=1).squeeze(2),
                                    r3(cwa), axis=AX.X, op=ALU.add))
        VD()
        V(lambda e: e.reciprocal(m16b, m16))
        VD()
        for hh in range(2):
            V(lambda e, hh=hh: e.tensor_copy(
                EXP.rearrange("p (a k) -> p a k", a=NB)[:, hh * 8:(hh + 1) * 8, :],
                m16b[:, hh * 8:(hh + 1) * 8].unsqueeze(2).broadcast_to([128, 8, 32])))
        VD()
        V(lambda e: e.tensor_scalar(cwf, cwf, 2.5, None, ALU.mult, ALU.bypass))
        VD()
        V(lambda e: e.tensor_tensor(cwf, cwf, EXP, ALU.mult), inc=("v", 1))
        self.vd()
        vv = sc.now("v")
        self.tap(f"cw{idx}", lambda: cwa[:].rearrange("p a b -> p (a b)"), [128, 512])
        cwb = self.av(52, [NB, NE], BF16)
        V(lambda e: e.tensor_copy(cwb[:], cwa[:]), waits=[("v", vv)], inc=("v", 1))
        vv = sc.now("v")
        cwT = self.av(32, [TOK], BF16, part0=32)
        for half in range(2):
            wpt = self.bank_wait("PT")
            for j in range(8):
                tb = half * 8 + j
                self.t(lambda e, tb=tb, j=j: e.transpose(
                    self.PT[0:32, j * 128:(j + 1) * 128],
                    cwb[:, tb, :], self.idn[:]),
                    waits=(wpt + [("v", vv)]) if j == 0 else [],
                    inc=("t", 1) if j == 7 else None)
            tm = sc.now("t")
            self.v(lambda e, half=half: e.tensor_copy(
                cwT[:, half * 1024:(half + 1) * 1024].rearrange("p (a b) -> p a b", a=8),
                self.PT[0:32].rearrange("p (a b) -> p a b", a=8)),
                waits=[("t", tm)], inc=("v", 1))
            self.bank_set("PT")
        self.cwT = cwT
        self.cwT_v = sc.now("v")

    def shared(self, idx):
        sc = self.sc
        Mub, R2b = self.Mub, self.R2b
        h_sh = self.av(56, [32, 512], BF16)
        supw = [self.av(88 + 8 * i, [NB, 256], BF16) for i in range(2)]
        sdnw = [self.av(104 + 8 * i, [32, 128], BF16) for i in range(2)]
        tf = [self.av(48 + 2 * i, [512], F32) for i in range(2)]
        rb = [self.av(52 + i, [512], BF16) for i in range(2)]
        pw = [("t", sc.now("t")), ("v", sc.now("v")), ("s", sc.now("s"))]
        self._pw = {0: pw, 1: pw}
        sup_t = [0, 0]
        sdn_t = [0, 0]
        relu_s = [0, 0]
        for grp in range(4):
            gsl = slice(grp * 512, (grp + 1) * 512)
            hshmark = [0] * 32
            for mbp in range(16):
                slp = mbp % 2
                self.g(lambda e, mbp=mbp, slp=slp: e.dma_start(
                    out=supw[slp][:],
                    in_=self.supT[:, mbp * 256:(mbp + 1) * 256].rearrange("(c p) n -> p c n", p=128)),
                    waits=([("t", sup_t[slp])] if (grp, mbp) >= (0, 2) else pw), inc=("qw", 16))
                qn = sc.now("qw")
                for sub in range(2):
                  mb = 2 * mbp + sub
                  sl = mb % 2
                  if True:
                    if not self.w1s_done and grp == 0:
                        self.w1s_from_slot(
                            supw[slp][:].rearrange("p a b -> p a b")[:, :, sub * 128:(sub + 1) * 128],
                            NB, self.w1s_sup, mb, 128, qn)
                    bk = "P1a" if sl == 0 else "P1b"
                    P = self.P1[sl]
                    wb = self.bank_wait(bk)
                    for c in range(NB):
                        sup_t[slp] = self.t(lambda e, c=c, P=P, grp=grp, slp=slp, sub=sub: e.matmul(
                            P[:], supw[slp][:, c, sub * 128:(sub + 1) * 128],
                            self.x[:, c * TOK + grp * 512: c * TOK + (grp + 1) * 512],
                            start=(c == 0), stop=(c == NB - 1)),
                            waits=(wb + [("qw", qn)]) if c == 0 else [],
                            inc=("t", 1) if c == NB - 1 else None) or sup_t[slp]
                    tm = sc.now("t")
                    self.v(lambda e, mb=mb, P=P, gsl=gsl, sl=sl: e.scalar_tensor_tensor(
                        tf[sl][:], Mub[:, gsl], self.w1s_sup[:, mb:mb + 1], P[:],
                        ALU.mult, ALU.subtract), waits=[("t", tm), ("s", relu_s[sl])], inc=("v", 1))
                    vv = sc.now("v")
                    self.bank_set(bk)
                    relu_s[sl] = self.s(lambda e, sl=sl: e.activation(rb[sl][:], tf[sl][:], AF.Relu, scale=-1.0),
                           waits=[("v", vv)], inc=("s", 1))
                    sv = sc.now("s")
                    self.v(lambda e, sl=sl: e.tensor_tensor(tf[sl][:], rb[sl][:], rb[sl][:], ALU.mult),
                           waits=[("s", sv)])
                    self.vd()
                    self.v(lambda e, mb=mb, sl=sl, gsl=gsl: e.tensor_tensor(
                        h_sh[:, mb, :], tf[sl][:], R2b[:, gsl], ALU.mult), inc=("v", 1))
                    hshmark[mb] = sc.now("v")
            hv = sc.now("v")
            if grp == 0:
                self.tap("hsh", lambda: h_sh[:].rearrange("p a b -> p (a b)"), [128, 32 * 512], BF16)
                self.tap("w1s_sup", lambda: self.w1s_sup[:], [128, 32], F32)
            for mb in range(NB):
                sl = mb % 2
                self.g(lambda e, mb=mb, sl=sl: e.dma_start(
                    out=sdnw[sl][:],
                    in_=self.sdnT[:, mb * 128:(mb + 1) * 128].rearrange("(c p) n -> p c n", p=128)),
                    waits=([("t", sdn_t[sl])] if (grp, mb) >= (0, 2) else pw), inc=("qw", 16))
                qn = sc.now("qw")
                bk = "P2a" if sl == 0 else "P2b"
                P = self.P2[sl][:, 0:512]
                wb = self.bank_wait(bk)
                for c in range(32):
                    ww = (wb + [("qw", qn)]) if c == 0 else []
                    if mb == 0:
                        ww = ww + [("v", hshmark[c])]
                    sdn_t[sl] = self.t(lambda e, c=c, P=P, sl=sl: e.matmul(
                        P, sdnw[sl][:, c, :], h_sh[:, c, :],
                        start=(c == 0), stop=(c == 31)),
                        waits=ww,
                        inc=("t", 1) if c == 31 else None) or sdn_t[sl]
                tm = sc.now("t")
                xsl = self.x[:, mb * TOK + grp * 512: mb * TOK + (grp + 1) * 512]
                self.v(lambda e, P=P, xsl=xsl: e.scalar_tensor_tensor(
                    xsl, P, 0.5, xsl, ALU.mult, ALU.add), waits=[("t", tm)], inc=("v", 1))
                self.bank_set(bk)

    def fc1(self, idx):
        sc = self.sc
        Rb, Mub = self.Rb, self.Mub
        xlat = self.av(0, [8, TOK], BF16)
        fw = [self.av(88 + 4 * i, [NB, 128], BF16) for i in range(2)]
        tf = [self.av(112 + 4 * i, [1024], F32) for i in range(2)]
        pw = [("t", sc.now("t")), ("v", sc.now("v")), ("s", sc.now("s"))]
        f_t = [0, 0]
        for mb in range(8):
            sl = mb % 2
            self.g(lambda e, mb=mb, sl=sl: e.dma_start(
                out=fw[sl][:],
                in_=self.fc1T[:, mb * 128:(mb + 1) * 128].rearrange("(c p) n -> p c n", p=128)),
                waits=([("t", f_t[sl])] if mb >= 2 else pw), inc=("qw", 16))
            qn = sc.now("qw")
            if not self.w1s_done:
                self.w1s_from_slot(fw[sl], NB, self.w1s_f1, mb, 256, qn)
            for nh in range(2):
                bk = "P2a" if sl == 0 else "P2b"
                P = self.P2[sl]
                wb = self.bank_wait(bk) if nh == 0 else [("v", sc.now("v"))]
                for half512 in range(2):
                    off = nh * 1024 + half512 * 512
                    for c in range(NB):
                        f_t[sl] = self.t(lambda e, c=c, off=off, P=P, sl=sl, half512=half512: e.matmul(
                            P[:, half512 * 512:(half512 + 1) * 512],
                            fw[sl][:, c, :], self.x[:, c * TOK + off: c * TOK + off + 512],
                            start=(c == 0), stop=(c == NB - 1)),
                            waits=(wb + [("qw", qn)]) if (c == 0 and half512 == 0) else [],
                            inc=("t", 1) if c == NB - 1 else None) or f_t[sl]
                tm = sc.now("t")
                tsl = slice(nh * 1024, (nh + 1) * 1024)
                self.v(lambda e, mb=mb, P=P, tsl=tsl, sl=sl: e.scalar_tensor_tensor(
                    tf[sl][:], Mub[:, tsl], self.w1s_f1[:, mb:mb + 1], P[:], ALU.mult, ALU.subtract),
                    waits=[("t", tm)], inc=("v", 1))
                self.vd()
                self.v(lambda e, mb=mb, tsl=tsl, sl=sl: e.scalar_tensor_tensor(
                    xlat[:, mb, tsl], tf[sl][:], -1.0, Rb[:, tsl], ALU.mult, ALU.mult),
                    inc=("v", 1))
                if nh == 1:
                    self.bank_set(bk)
        self.xlat_v = sc.now("v")
        self.tap(f"xlat{idx}", lambda: xlat[:].rearrange("p a b -> p (a b)"), [128, 8 * TOK], BF16)

    def experts(self, idx):
        sc = self.sc
        xlat = self.av(0, [8, TOK], BF16)
        cwT = self.cwT
        wA = [self.av(36, [8, 1024], BF16), self.av(104, [8, 1024], BF16)]
        BeS = [self.av(52, [TOK], BF16), self.av(52, [TOK], BF16)]
        y = self.av(56, [8, TOK], BF16)
        h = self.av(88, [8, 1024], BF16)
        pw = [("t", sc.now("t")), ("v", sc.now("v")), ("s", sc.now("s"))]
        w1_t = 0
        w2_t = 0
        yv = 0
        w1q = w2q = 0
        self.be_war = 0

        def build_Be(eb):
            banks = ["P1a", "P1b", "P1c"]

            def bmm(n, extra):
                wb = self.bank_wait(banks[n % 3])
                self.t(lambda e, eb=eb, n=n: e.matmul(
                    self.P1[n % 3][:], self.eh[:, eb * 128:(eb + 1) * 128],
                    cwT[:, n * 512:(n + 1) * 512], start=True, stop=True),
                    waits=wb + extra, inc=("t", 1))
                return sc.now("t")

            def bcp(n):
                tm = tmv[n]
                self.v(lambda e, n=n: e.tensor_copy(
                    BeS[0][:, n * 512:(n + 1) * 512], self.P1[n % 3][:]),
                    waits=[("t", tm)], inc=("v", 1))
                self.bank_set(banks[n % 3])

            tmv = {}
            w0 = [("v", max(self.be_war, self.cwT_v))]
            tmv[0] = bmm(0, w0)
            tmv[1] = bmm(1, [])
            tmv[2] = bmm(2, [])
            bcp(0)
            bcp(1)
            bcp(2)
            tmv[3] = bmm(3, [])
            bcp(3)

        build_Be(0)
        for e_i in range(NE):
            Be = BeS[0]
            if e_i > 0:
                self.be_war = sc.now("v")
                build_Be(e_i)
            self.g(lambda e, e_i=e_i: e.dma_start(
                out=wA[0][:],
                in_=self.w1T[e_i * LAT:(e_i + 1) * LAT, :].rearrange("(c p) n -> p c n", p=128)),
                waits=([("t", w1_t)] if e_i > 0 else pw), inc=("qw", 16))
            w1q = sc.now("qw")
            self.g(lambda e, e_i=e_i: e.dma_start(
                out=wA[1][:],
                in_=self.w2T[e_i * INTER:(e_i + 1) * INTER, :].rearrange("(c p) n -> p c n", p=128)),
                waits=([("t", w2_t)] if e_i > 0 else pw), inc=("qw", 16))
            w2q = sc.now("qw")
            for half in range(2):
                hsl = slice(half * 1024, (half + 1) * 1024)
                hmark = [0] * 8
                for ib in range(8):
                    bk = "P2a" if ib % 2 == 0 else "P2b"
                    P = self.P2[ib % 2]
                    wb = self.bank_wait(bk)
                    for kc in range(8):
                        for n2 in range(2):
                            w1_t = self.t(lambda e, kc=kc, ib=ib, n2=n2, P=P, half=half: e.matmul(
                                P[:, n2 * 512:(n2 + 1) * 512],
                                wA[0][:, kc, ib * 128:(ib + 1) * 128],
                                xlat[:, kc, half * 1024 + n2 * 512: half * 1024 + (n2 + 1) * 512],
                                start=(kc == 0), stop=(kc == 7)),
                                waits=(wb + [("qw", w1q), ("v", self.xlat_v)]) if (kc == 0 and n2 == 0) else [],
                                inc=("t", 1) if (kc == 7 and n2 == 1) else None) or w1_t
                    tm = sc.now("t")
                    self.s(lambda e, ib=ib, P=P: e.activation(h[:, ib, :], P[:], AF.Relu),
                           waits=[("t", tm)], inc=("s", 1))
                    sv = sc.now("s")
                    self.v(lambda e, ib=ib: e.tensor_tensor(h[:, ib, :], h[:, ib, :], h[:, ib, :],
                                                            ALU.mult), waits=[("s", sv)])
                    self.vd()
                    self.v(lambda e, ib=ib, hsl=hsl, Be=Be: e.tensor_tensor(
                        h[:, ib, :], h[:, ib, :], Be[:, hsl], ALU.mult), inc=("v", 1))
                    hmark[ib] = sc.now("v")
                    self.bank_set(bk, "s")
                hv = sc.now("v")
                for mb in range(8):
                    bk = "P2a" if mb % 2 == 0 else "P2b"
                    P = self.P2[mb % 2]
                    wb = self.bank_wait(bk)
                    for kc in range(8):
                        for n2 in range(2):
                            ww = list(wb) + [("qw", w2q)] if (kc == 0 and n2 == 0) else []
                            if mb == 0 and n2 == 0:
                                ww = ww + [("v", hmark[kc])]
                            w2_t = self.t(lambda e, kc=kc, mb=mb, n2=n2, P=P: e.matmul(
                                P[:, n2 * 512:(n2 + 1) * 512],
                                wA[1][:, kc, mb * 128:(mb + 1) * 128],
                                h[:, kc, n2 * 512:(n2 + 1) * 512],
                                start=(kc == 0), stop=(kc == 7)),
                                waits=ww,
                                inc=("t", 1) if (kc == 7 and n2 == 1) else None) or w2_t
                    tm = sc.now("t")
                    if e_i == 0:
                        yv = self.v(lambda e, mb=mb, P=P, hsl=hsl: e.tensor_copy(
                            y[:, mb, hsl], P[:]), waits=[("t", tm)], inc=("v", 1))
                    else:
                        yv = self.v(lambda e, mb=mb, P=P, hsl=hsl: e.tensor_tensor(
                            y[:, mb, hsl], P[:], y[:, mb, hsl], ALU.add),
                            waits=[("t", tm)], inc=("v", 1))
                    self.bank_set(bk)
        self.y_done = sc.now("v")
        self.tap(f"ylat{idx}", lambda: y[:].rearrange("p a b -> p (a b)"), [128, 8 * TOK], BF16)

    def fc2(self, idx):
        import os as _os
        NMB = int(_os.environ.get("FC2_MB", NB))
        sc = self.sc
        y = self.av(56, [8, TOK], BF16)
        fw = [self.av(88 + 2 * i, [8, 128], BF16) for i in range(2)]
        pw = [("t", sc.now("t")), ("v", sc.now("v")), ("s", sc.now("s"))]
        f_t = [0, 0]
        for mb in range(NMB):
            sl = mb % 2
            self.g(lambda e, mb=mb, sl=sl: e.dma_start(
                out=fw[sl][:],
                in_=self.fc2T[:, mb * 128:(mb + 1) * 128].rearrange("(c p) n -> p c n", p=128)),
                waits=([("t", f_t[sl])] if mb >= 2 else pw), inc=("qw", 16))
            qn = sc.now("qw")
            for nh in range(2):
                bk = "P2a" if sl == 0 else "P2b"
                P = self.P2[sl]
                wb = self.bank_wait(bk) if nh == 0 else [("v", sc.now("v"))]
                for n2 in range(2):
                    for kc in range(8):
                        f_t[sl] = self.t(lambda e, kc=kc, nh=nh, n2=n2, P=P, sl=sl: e.matmul(
                            P[:, n2 * 512:(n2 + 1) * 512],
                            fw[sl][:, kc, :], y[:, kc, nh * 1024 + n2 * 512: nh * 1024 + (n2 + 1) * 512],
                            start=(kc == 0), stop=(kc == 7)),
                            waits=(wb + ([("qw", qn), ("v", self.y_done)] if nh == 0 else [])) if (kc == 0 and n2 == 0) else [],
                            inc=("t", 1) if kc == 7 else None) or f_t[sl]
                tm = sc.now("t")
                xsl = self.x[:, mb * TOK + nh * 1024: mb * TOK + (nh + 1) * 1024]
                self.v(lambda e, P=P, xsl=xsl: e.scalar_tensor_tensor(
                    xsl, P[:], 0.5, xsl, ALU.mult, ALU.add), waits=[("t", tm)], inc=("v", 1))
                if nh == 1:
                    self.bank_set(bk)

    def attn(self, stop=None):
        sc = self.sc
        self.stats()
        if stop == 'atstats':
            return
        cosA = self.av(112, [NB, HD], BF16)
        sinA = self.av(116, [NB, HD], BF16)
        self.q(lambda e: e.dma_start(out=cosA[:].rearrange("p a b -> p (a b)"), in_=self.cosT[:]),
               waits=[("v", sc.now("v")), ("t", sc.now("t")), ("s", sc.now("s"))], inc=("qio", 16))
        self.q(lambda e: e.dma_start(out=sinA[:].rearrange("p a b -> p (a b)"), in_=self.sinT[:]),
               inc=("qio", 16))
        tabq = sc.now("qio")
        wq = self.av(0, [NB, 512], BF16)
        wk = self.av(16, [NB, 128], BF16)
        wv = self.av(20, [NB, 128], BF16)
        fx0 = self.av(24, [768], F32)
        fx1 = self.av(27, [640], F32)
        fx2 = self.av(30, [640], F32)
        qtok = self.av(33, [NB, 512], BF16)
        OT = self.av(16, [4, TOK], BF16)
        qfT = self.av(49, [4, TOK], BF16)
        ktok = self.av(65, [NB, 128], BF16)
        kT = self.av(69, [TOK], BF16)
        vaug = self.av(73, [NB, 132], BF16)
        osb = self.av(78, [NB, 4, 129], BF16)
        sg = self.av(95, [NCORES, 132], F32)
        sfin = self.av(100, [132], F32)
        ssb = self.av(101, [132], BF16)
        sinb = self.av(102, [132], BF16)
        acc = [self.av(103, [132], F32), self.av(104, [132], F32)]
        amT = [self.av(105, [512], BF16), self.av(106, [512], BF16)]
        W1Sb = self.av(107, [768], BF16)
        ft = self.av(109, [4, 129], F32)
        den = self.av(104, [8], F32)
        woT_s = self.av(0, [4, TOK], BF16)
        OHEAD = [0, 256, 512, 768]

        prev_corr_v = 0
        for grp in range(NKV):
            pwA = [("t", sc.now("t")), ("v", sc.now("v")), ("s", sc.now("s")),
                   ("qio", sc.now("qio"))]
            wqw = self.g(lambda e, grp=grp: e.dma_start(
                out=wq[:], in_=self.wqT[:, grp * 512:(grp + 1) * 512].rearrange("(c p) n -> p c n", p=128)),
                waits=pwA, inc=("qw", 16))
            self.g(lambda e, grp=grp: e.dma_start(
                out=wk[:], in_=self.wkT[:, grp * 128:(grp + 1) * 128].rearrange("(c p) n -> p c n", p=128)),
                inc=("qw", 16))
            self.g(lambda e, grp=grp: e.dma_start(
                out=wv[:], in_=self.wvT[:, grp * 128:(grp + 1) * 128].rearrange("(c p) n -> p c n", p=128)),
                inc=("qw", 16))
            qn = sc.now("qw")
            Pw = self.P2[0]
            wb = self.bank_wait("P2a")
            for c in range(NB):
                self.t(lambda e, c=c, Pw=Pw: e.matmul(Pw[0:1, 0:512], self.ones_c[:], wq[:, c, :],
                                                      start=(c == 0), stop=(c == NB - 1)),
                       waits=(wb + [("qw", qn), ("v", prev_corr_v)]) if c == 0 else [])
            for c in range(NB):
                self.t(lambda e, c=c, Pw=Pw: e.matmul(Pw[0:1, 512:640], self.ones_c[:], wk[:, c, :],
                                                      start=(c == 0), stop=(c == NB - 1)))
            for c in range(NB):
                self.t(lambda e, c=c, Pw=Pw: e.matmul(Pw[0:1, 640:768], self.ones_c[:], wv[:, c, :],
                                                      start=(c == 0), stop=(c == NB - 1)),
                       inc=("t", 1) if c == NB - 1 else None)
            tm = sc.now("t")
            self.v(lambda e, Pw=Pw: e.tensor_copy(self.wsrow[0:1, 0:768], Pw[0:1, 0:768]),
                   waits=[("t", tm)], inc=("v", 1))
            vm = sc.now("v")
            self.bank_set("P2a")
            for (o0, n0) in ((0, 512), (512, 256)):
                wb = self.bank_wait("P1a")
                self.t(lambda e, o0=o0, n0=n0: e.matmul(
                    self.P1[0][:, 0:n0], self.ones_r[:], self.wsrow[0:1, o0:o0 + n0],
                    start=True, stop=True), waits=wb + ([("v", vm)] if o0 == 0 else []),
                    inc=("t", 1))
                tm = sc.now("t")
                self.v(lambda e, o0=o0, n0=n0: e.tensor_copy(W1Sb[:, o0:o0 + n0], self.P1[0][:, 0:n0]),
                       waits=[("t", tm)], inc=("v", 1))
                self.bank_set("P1a")
            w1sv = sc.now("v")
            self.g(lambda e: e.memset(vaug[:, :, 128:129], 1.0),
                   waits=[("v", sc.now("v")), ("t", sc.now("t"))], inc=("p", 1))
            pmem = sc.now("p")
            for tb in range(NB):
                bk = "P2a" if tb % 2 == 0 else "P2b"
                P = self.P2[tb % 2]
                wb = self.bank_wait(bk)
                for c in range(NB):
                    self.t(lambda e, c=c, tb=tb, P=P: e.matmul(
                        P[:, 0:512], self.x[:, c * TOK + tb * 128: c * TOK + (tb + 1) * 128],
                        wq[:, c, :], start=(c == 0), stop=(c == NB - 1)),
                        waits=(wb + [("qw", qn), ("qio", tabq)]) if c == 0 else [])
                for c in range(NB):
                    self.t(lambda e, c=c, tb=tb, P=P: e.matmul(
                        P[:, 512:640], self.x[:, c * TOK + tb * 128: c * TOK + (tb + 1) * 128],
                        wk[:, c, :], start=(c == 0), stop=(c == NB - 1)))
                for c in range(NB):
                    self.t(lambda e, c=c, tb=tb, P=P: e.matmul(
                        P[:, 640:768], self.x[:, c * TOK + tb * 128: c * TOK + (tb + 1) * 128],
                        wv[:, c, :], start=(c == 0), stop=(c == NB - 1)),
                        inc=("t", 1) if c == NB - 1 else None)
                tm = sc.now("t")
                self.v(lambda e, tb=tb: e.tensor_scalar(
                    fx0[:], W1Sb[:], self.mucol[:, tb:tb + 1], None, ALU.mult, ALU.bypass),
                    waits=[("t", tm), ("v", w1sv), ("p", pmem), ("qio", tabq)], inc=("v", 1))
                self.vd()
                self.v(lambda e, tb=tb, P=P: e.scalar_tensor_tensor(
                    fx0[:], P[:, 0:768], self.rcol[:, tb:tb + 1], fx0[:], ALU.mult, ALU.subtract),
                    inc=("v", 1))
                self.vd()
                self.bank_set(bk)
                self.v(lambda e, tb=tb: e.tensor_copy(vaug[:, tb, 0:128], fx0[:, 640:768]), inc=("v", 1))
                qk5 = fx0[:, 0:640].rearrange("p (h d) -> p h d", h=5)
                f15 = fx1[:].rearrange("p (h d) -> p h d", h=5)
                f25 = fx2[:].rearrange("p (h d) -> p h d", h=5)
                self.v(lambda e, tb=tb, qk5=qk5, f15=f15: e.tensor_tensor(
                    f15, qk5, cosA[:, tb, :].unsqueeze(1).broadcast_to([128, 5, 128]), ALU.mult))
                self.v(lambda e, tb=tb, qk5=qk5, f25=f25: e.tensor_tensor(
                    f25[:, :, 0:64], qk5[:, :, 64:128],
                    sinA[:, tb, 0:64].unsqueeze(1).broadcast_to([128, 5, 64]), ALU.mult))
                self.v(lambda e, tb=tb, qk5=qk5, f25=f25: e.tensor_tensor(
                    f25[:, :, 64:128], qk5[:, :, 0:64],
                    sinA[:, tb, 64:128].unsqueeze(1).broadcast_to([128, 5, 64]), ALU.mult))
                self.vd()
                self.v(lambda e: e.tensor_tensor(fx1[:], fx1[:], fx2[:], ALU.add), inc=("v", 1))
                self.vd()
                self.v(lambda e: e.tensor_scalar(fx0[:, 0:640], fx1[:], 0.0, None, ALU.min, ALU.bypass),
                       inc=("v", 1))
                vv = sc.now("v")
                self.s(lambda e: e.activation(fx2[:], fx0[:, 0:640], AF.Exp), waits=[("v", vv)],
                       inc=("s", 1))
                sv = sc.now("s")
                self.v(lambda e: e.tensor_scalar(fx0[:, 0:640], fx1[:], 0.0, None, ALU.max, ALU.bypass),
                       waits=[("s", sv)])
                self.vd()
                self.v(lambda e, tb=tb: e.tensor_tensor(qtok[:, tb, :], fx2[:, 0:512],
                                                        fx0[:, 0:512], ALU.add))
                self.v(lambda e, tb=tb: e.tensor_tensor(ktok[:, tb, :], fx2[:, 512:640],
                                                        fx0[:, 512:640], ALU.add), inc=("v", 1))
                self.vd()
            projv = sc.now("v")
            self.bank["wq_t"] = ("t", sc.now("t"))
            if grp == 1:
                self.tap("qtok", lambda: qtok[:].rearrange("p a b -> p (a b)"), [128, NB * 512], BF16)
                self.tap("ktok", lambda: ktok[:].rearrange("p a b -> p (a b)"), [128, NB * 128], BF16)
                self.tap("vaug", lambda: vaug[:].rearrange("p a b -> p (a b)"), [128, NB * 132], BF16)
            if stop == f"atproj{grp}":
                return
            def tr_batch(srcs, dst, n_in, waitv):
                wpt = self.bank_wait("PT")
                for j, sap in enumerate(srcs):
                    self.t(lambda e, j=j, sap=sap: e.transpose(
                        self.PT[:, j * 128:(j + 1) * 128], sap, self.idn[:]),
                        waits=(wpt + [("v", waitv)]) if j == 0 else [],
                        inc=("t", 1) if j == len(srcs) - 1 else None)
                tm = sc.now("t")
                self.v(lambda e, dst=dst, n=len(srcs): e.tensor_copy(
                    dst, self.PT[:, 0:n * 128]), waits=[("t", tm)], inc=("v", 1))
                self.bank_set("PT")
            for hh in range(4):
                for half in range(2):
                    tr_batch([qtok[:, half * 8 + j, hh * 128:(hh + 1) * 128] for j in range(8)],
                             qfT[:, hh, half * 1024:(half + 1) * 1024], 8, projv)
            for half in range(2):
                tr_batch([ktok[:, half * 8 + j, :] for j in range(8)],
                         kT[:, half * 1024:(half + 1) * 1024], 8, projv)
            trv = sc.now("v")
            psS = self.P1[2][:, 0:132]
            self.g(lambda e: e.memset(ssb[:], 0.0), waits=[("t", sc.now("t"))], inc=("p", 1))
            pz = sc.now("p")
            ssb_v = 0
            st_t = 0
            for cchunk in range(NB):
                bkA = "P1a" if cchunk % 2 == 0 else "P1b"
                PA = self.P1[cchunk % 2]
                wbA = self.bank_wait(bkA)
                for hh in range(4):
                    self.t(lambda e, cchunk=cchunk, hh=hh, PA=PA: e.matmul(
                        PA[:, hh * 128:(hh + 1) * 128], kT[:, cchunk * 128:(cchunk + 1) * 128],
                        qfT[:, hh, cchunk * 128:(cchunk + 1) * 128], start=True, stop=True),
                        waits=(wbA + [("v", trv)]) if hh == 0 else [],
                        inc=("t", 1) if hh == 3 else None)
                tmA = sc.now("t")
                am = amT[cchunk % 2]
                self.v(lambda e, PA=PA, am=am: e.tensor_tensor(
                    am[:].rearrange("p (h d) -> p h d", h=4),
                    PA[:].rearrange("p (h d) -> p h d", h=4),
                    self.msk[:].unsqueeze(1).broadcast_to([128, 4, 128]), ALU.mult),
                    waits=[("t", tmA)], inc=("v", 1))
                amv = sc.now("v")
                self.bank_set(bkA)
                bkO = "P2a" if cchunk % 2 == 0 else "P2b"
                PO = self.P2[cchunk % 2]
                wbO = self.bank_wait(bkO)
                for hh in range(4):
                    self.t(lambda e, cchunk=cchunk, hh=hh, PO=PO, am=am: e.matmul(
                        PO[:, OHEAD[hh]:OHEAD[hh] + 129], am[:, hh * 128:(hh + 1) * 128],
                        vaug[:, cchunk, 0:129], start=True, stop=False),
                        waits=(wbO + [("v", amv), ("v", ssb_v), ("p", pz)]) if hh == 0 else [])
                    self.t(lambda e, cchunk=cchunk, hh=hh, PO=PO: e.matmul(
                        PO[:, OHEAD[hh]:OHEAD[hh] + 129], qfT[:, hh, cchunk * 128:(cchunk + 1) * 128],
                        ssb[:, 0:129], start=False, stop=True),
                        inc=("t", 1) if hh == 3 else None)
                tmO = sc.now("t")
                st_t = self.t(lambda e, cchunk=cchunk: e.matmul(
                    psS[:, 0:129], ktok[:, cchunk, :], vaug[:, cchunk, 0:129],
                    start=(cchunk == 0), stop=True, skip_group_check=True),
                    waits=[("v", ssb_v)], inc=("t", 1))
                self.v(lambda e: e.tensor_copy(ssb[:, 0:129], psS[:, 0:129]),
                       waits=[("t", st_t)], inc=("v", 1))
                ssb_v = sc.now("v")
                self.v(lambda e, cchunk=cchunk, PO=PO: e.tensor_copy(
                    osb[:, cchunk, :, :],
                    PO[:, 0:1024].rearrange("p (a b) -> p a b", a=4)[:, :, 0:129]),
                    waits=[("t", tmO)], inc=("v", 1))
                self.bank_set(bkO)
            scan_t = sc.now("t")
            if grp == 1:
                self.tap("osb", lambda: osb[:].rearrange("p a b c -> p (a b c)"), [128, NB * 4 * 129], BF16)
            if stop == f"atscan{grp}":
                self.v(lambda e: e.tensor_copy(sfin[:, 0:129], psS[:, 0:129]),
                       waits=[("t", scan_t)], inc=("v", 1))
                return
            self.v(lambda e: e.tensor_copy(sfin[:, 0:129], psS[:, 0:129]),
                   waits=[("t", scan_t)], inc=("v", 1))
            self.v(lambda e: e.memset(sfin[:, 129:132], 0.0), inc=("v", 1))
            sfv = sc.now("v")
            self.g(lambda e, grp=grp: e.dma_start(out=self.cc_in[grp][:], in_=sfin[:]),
                   waits=[("v", sfv)], inc=("qw", 16))
            qcc = sc.now("qw")
            self.g(lambda e, grp=grp: e.collective_compute(
                "AllGather", ALU.bypass, replica_groups=[list(range(NCORES))],
                ins=[self.cc_in[grp][:]], outs=[self.cc_out[grp][:]]),
                waits=[("qw", qcc)], inc=("cc", 1))
            ccv = sc.now("cc")
            self.g(lambda e, grp=grp: e.dma_start(
                out=sg[:].rearrange("p a b -> p a b"),
                in_=self.cc_out[grp][:].rearrange("(r p) n -> p r n", p=128)),
                waits=[("cc", ccv)], inc=("qw", 16))
            qsg = sc.now("qw")
            self.v(lambda e: e.tensor_scalar(acc[0][:], sg[:, 0, :], self.flg[:, 0:1], None,
                                             ALU.mult, ALU.bypass), waits=[("qw", qsg)], inc=("v", 1))
            for r in range(1, NCORES):
                self.vd()
                self.v(lambda e, r=r: e.scalar_tensor_tensor(
                    acc[r % 2][:], sg[:, r, :], self.flg[:, r:r + 1], acc[(r - 1) % 2][:],
                    ALU.mult, ALU.add), inc=("v", 1))
            self.vd()
            self.v(lambda e: e.tensor_copy(sinb[:], acc[(NCORES - 1) % 2][:]), inc=("v", 1))
            sinv = sc.now("v")
            ofin = qtok
            for cchunk in range(NB):
                bkO = "P2a" if cchunk % 2 == 0 else "P2b"
                PO = self.P2[cchunk % 2]
                wbO = self.bank_wait(bkO)
                for hh in range(4):
                    self.t(lambda e, cchunk=cchunk, hh=hh, PO=PO: e.matmul(
                        PO[:, OHEAD[hh]:OHEAD[hh] + 129], qfT[:, hh, cchunk * 128:(cchunk + 1) * 128],
                        sinb[:, 0:129], start=True, stop=True),
                        waits=(wbO + [("v", sinv)]) if hh == 0 else [],
                        inc=("t", 1) if hh == 3 else None)
                tm = sc.now("t")
                self.v(lambda e, cchunk=cchunk, PO=PO: e.tensor_tensor(
                    ft[:],
                    PO[:, 0:1024].rearrange("p (a b) -> p a b", a=4)[:, :, 0:129],
                    osb[:, cchunk, :, :], ALU.add), waits=[("t", tm)], inc=("v", 1))
                self.vd()
                self.bank_set(bkO)
                self.v(lambda e: e.tensor_scalar(den[:, 0:4], ft[:, :, 128], EPS_AT, None,
                                                 ALU.add, ALU.bypass), inc=("v", 1))
                self.vd()
                self.v(lambda e: e.reciprocal(den[:, 4:8], den[:, 0:4]), inc=("v", 1))
                self.vd()
                for hh in range(4):
                    self.v(lambda e, cchunk=cchunk, hh=hh: e.tensor_scalar(
                        ofin[:, cchunk, hh * 128:(hh + 1) * 128], ft[:, hh, 0:128],
                        den[:, 4 + hh:5 + hh], None, ALU.mult, ALU.bypass),
                        inc=("v", 1) if hh == 3 else None)
                self.vd()
            corrv = sc.now("v")
            for hh in range(4):
                for half in range(2):
                    tr_batch([ofin[:, half * 8 + j, hh * 128:(hh + 1) * 128] for j in range(8)],
                             OT[:, hh, half * 1024:(half + 1) * 1024], 8, corrv)
            otv = sc.now("v")
            self.q(lambda e, grp=grp: e.dma_start(
                out=self.otd[grp * 512:(grp + 1) * 512, :].rearrange("(c p) n -> p c n", p=128),
                in_=OT[:]), waits=[("v", otv)], inc=("qio", 16))
            self.ot_spill_q = sc.now("qio")
            self.bank["wo_t"] = ("t", sc.now("t"))
            prev_corr_v = sc.now("v")
            if stop == f"atg{grp}":
                return
        otf = self.av(0, [NB, TOK], BF16)
        wos = [self.av(64 + 4 * i, [NB, 128], BF16) for i in range(2)]
        self.q(lambda e: e.dma_start(
            out=otf[:], in_=self.otd[:].rearrange("(c p) n -> p c n", p=128)),
            waits=[("qio", sc.now("qio")), ("t", sc.now("t"))], inc=("qio", 16))
        qotf = sc.now("qio")
        wo_t = [0, 0]
        for mb in range(NB):
            sl = mb % 2
            self.g(lambda e, mb=mb, sl=sl: e.dma_start(
                out=wos[sl][:],
                in_=self.woT[:, mb * 128:(mb + 1) * 128].rearrange("(c p) n -> p c n", p=128)),
                waits=([("t", wo_t[sl])] if mb >= 2 else
                       [("t", sc.now("t")), ("v", sc.now("v")), ("s", sc.now("s"))]),
                inc=("qw", 16))
            qn = sc.now("qw")
            for n in range(4):
                bk = "P1a" if sl == 0 else "P1b"
                P = self.P1[sl]
                wb = self.bank_wait(bk) if n == 0 else [("v", sc.now("v"))]
                for kc in range(NB):
                    wo_t[sl] = self.t(lambda e, mb=mb, n=n, kc=kc, P=P, sl=sl: e.matmul(
                        P[:], wos[sl][:, kc, :], otf[:, kc, n * 512:(n + 1) * 512],
                        start=(kc == 0), stop=(kc == NB - 1)),
                        waits=(wb + [("qw", qn), ("qio", qotf)]) if kc == 0 else [],
                        inc=("t", 1) if kc == NB - 1 else None) or wo_t[sl]
                tm = sc.now("t")
                xsl = self.x[:, mb * TOK + n * 512: mb * TOK + (n + 1) * 512]
                self.v(lambda e, P=P, xsl=xsl: e.tensor_tensor(xsl, P[:], xsl, ALU.add),
                       waits=[("t", tm)], inc=("v", 1))
                self.bank_set(bk)

    def store_out(self):
        sc = self.sc
        self.g(lambda e: e.dma_start(
            out=self.xout[:].rearrange("(c p) t -> p c t", p=128),
            in_=self.x[:].rearrange("p (c t) -> p c t", c=NB)),
            waits=self.all_now(), inc=("qw", 16))
        fin = sc.now("qw")
        self.g(lambda e: e.nop(), waits=[("qw", fin)])


def _host_tables(core):
    b, half = core // 2, core % 2
    inv = 1.0 / (10000.0 ** (np.arange(0, HD, 2, dtype=np.float64) / HD))
    t = np.arange(S_FULL := 4096, dtype=np.float64)
    fr = np.outer(t, inv)
    emb = np.concatenate([fr, fr], axis=-1)
    cos = np.cos(emb).astype(np.float32)
    sin = np.sin(emb).astype(np.float32)
    sgn = np.ones((1, HD), np.float32)
    sgn[0, :64] = -1.0
    sinS = sin * sgn
    sl = slice(half * TOK, (half + 1) * TOK)
    def fold(tab):
        return np.ascontiguousarray(
            tab[sl].reshape(NB, 128, HD).transpose(1, 0, 2).reshape(128, NB * HD)).astype(bfdt)
    return fold(cos), fold(sinS)


def kernel(L0_attn_x, L0_attn_Wq, L0_attn_Wk, L0_attn_Wv, L0_attn_Wo,
           L0_ffn_gate_w, L0_ffn_gate_bias, L0_ffn_fc1_w, L0_ffn_fc2_w,
           L0_ffn_w1, L0_ffn_w2, L0_ffn_shared_up_w, L0_ffn_shared_down_w):
    f32 = np.float32
    C = np.ascontiguousarray
    wqT = C(np.asarray(L0_attn_Wq, f32).T)
    wkT = C(np.asarray(L0_attn_Wk, f32).T)
    wvT = C(np.asarray(L0_attn_Wv, f32).T)
    woT = C(np.asarray(L0_attn_Wo, f32).T)
    gwT = C(np.asarray(L0_ffn_gate_w, f32).T)
    biasT = C(np.broadcast_to(np.asarray(L0_ffn_gate_bias, f32), (128, NE)))
    fc1T = C(np.asarray(L0_ffn_fc1_w, f32).T)
    fc2T = C(np.asarray(L0_ffn_fc2_w, f32).T)
    w1T = C(np.asarray(L0_ffn_w1, f32).transpose(0, 2, 1).reshape(NE * LAT, INTER))
    w2T = C(np.asarray(L0_ffn_w2, f32).transpose(0, 2, 1).reshape(NE * INTER, LAT))
    supT = C(np.asarray(L0_ffn_shared_up_w, f32).T)
    sdnT = C(np.asarray(L0_ffn_shared_down_w, f32).T)
    maskU = np.triu(np.ones((128, 128), np.float32)).astype(bfdt)
    ident = np.eye(128, dtype=np.float32).astype(bfdt)
    ehot = np.zeros((NE, NE * 128), np.float32)
    for e in range(NE):
        ehot[e, e * 128:(e + 1) * 128] = 1.0
    ehot = ehot.astype(bfdt)
    x = np.asarray(L0_attn_x, f32)

    import os
    mk = MK(stop_after=os.environ.get("MK_STOP") or None)
    nc = mk.build()
    in_maps = _in_maps(x, wqT, wkT, wvT, woT, gwT, biasT, fc1T, fc2T, w1T, w2T,
                       supT, sdnT, maskU, ident, ehot)
    res = run_bass_kernel_spmd(nc, in_maps, core_ids=list(range(NCORES)))
    out = np.empty((4, 4096, DM), f32)
    for c in range(NCORES):
        b, half = c // 2, c % 2
        out[b, half * TOK:(half + 1) * TOK, :] = np.asarray(res.results[c]["xout"]).T
    return out


def _in_maps(x, wqT, wkT, wvT, woT, gwT, biasT, fc1T, fc2T, w1T, w2T,
             supT, sdnT, maskU, ident, ehot):
    f32 = np.float32
    C = np.ascontiguousarray
    in_maps = []
    for c in range(NCORES):
        b, half = c // 2, c % 2
        xin = C(x[b, half * TOK:(half + 1) * TOK, :].T)
        cosA, sinA = _host_tables(c)
        flags = np.zeros((128, NCORES), f32)
        if half == 1:
            flags[:, c - 1] = 1.0
        in_maps.append(dict(
            xin=xin, wqT=wqT, wkT=wkT, wvT=wvT, woT=woT, gwT=gwT, biasT=biasT,
            fc1T=fc1T, fc2T=fc2T, w1T=w1T, w2T=w2T, supT=supT, sdnT=sdnT,
            cosT=cosA, sinT=sinA, maskT=maskU, identT=ident, ehotT=ehot,
            flagsT=flags))
    return in_maps


def _host_prep(inputs):
    f32 = np.float32
    C = np.ascontiguousarray
    wqT = C(np.asarray(inputs["L0_attn_Wq"], f32).T)
    wkT = C(np.asarray(inputs["L0_attn_Wk"], f32).T)
    wvT = C(np.asarray(inputs["L0_attn_Wv"], f32).T)
    woT = C(np.asarray(inputs["L0_attn_Wo"], f32).T)
    gwT = C(np.asarray(inputs["L0_ffn_gate_w"], f32).T)
    biasT = C(np.broadcast_to(np.asarray(inputs["L0_ffn_gate_bias"], f32), (128, NE)))
    fc1T = C(np.asarray(inputs["L0_ffn_fc1_w"], f32).T)
    fc2T = C(np.asarray(inputs["L0_ffn_fc2_w"], f32).T)
    w1T = C(np.asarray(inputs["L0_ffn_w1"], f32).transpose(0, 2, 1).reshape(NE * LAT, INTER))
    w2T = C(np.asarray(inputs["L0_ffn_w2"], f32).transpose(0, 2, 1).reshape(NE * INTER, LAT))
    supT = C(np.asarray(inputs["L0_ffn_shared_up_w"], f32).T)
    sdnT = C(np.asarray(inputs["L0_ffn_shared_down_w"], f32).T)
    maskU = np.triu(np.ones((128, 128), np.float32)).astype(bfdt)
    ident = np.eye(128, dtype=np.float32).astype(bfdt)
    ehot = np.zeros((NE, NE * 128), np.float32)
    for e in range(NE):
        ehot[e, e * 128:(e + 1) * 128] = 1.0
    ehot = ehot.astype(bfdt)
    x = np.asarray(inputs["L0_attn_x"], f32)
    return _in_maps(x, wqT, wkT, wvT, woT, gwT, biasT, fc1T, fc2T, w1T, w2T,
                    supT, sdnT, maskU, ident, ehot)
